# version 15
# baseline (speedup 1.0000x reference)
import numpy as np
import ml_dtypes
import concourse.bass as bass
import concourse.mybir as mybir
from concourse.bass_utils import run_bass_kernel_spmd

F32 = mybir.dt.float32
BF16 = mybir.dt.bfloat16
I32 = mybir.dt.int32
AF = mybir.ActivationFunctionType
ALU = mybir.AluOpType
AX = mybir.AxisListType
BF = ml_dtypes.bfloat16

D = 1024
T = 8192
DI = 2048
NS = 16
HM = 5632
NE = 4
HF = 3072
NH = 8
DH = 128
EPS = 1e-5

ENGINES = ("sync", "tensor", "vector", "scalar", "gpsimd")


class _Op:
    __slots__ = ("engine", "fn", "reads", "writes", "kind", "idx", "waits",
                 "inc_needed", "pre_wait")

    def __init__(self, engine, fn, reads, writes, kind, idx):
        self.engine = engine
        self.fn = fn
        self.reads = list(reads)
        self.writes = list(writes)
        self.kind = kind
        self.idx = idx
        self.waits = []
        self.inc_needed = kind != "c"
        self.pre_wait = None


def _split(res):
    if "@" in res:
        name, tag = res.split("@", 1)
        return name, tag
    return res, "*"


class Prog:

    def __init__(self, nc):
        self.nc = nc
        self.ops = []
        self.seq = {}
        self.state = {}
        self.bar_waits = []
        self.dirty = {}

    def barrier(self):
        last = {}
        for op in self.ops:
            last[(op.engine, op.kind)] = op
        self.bar_waits = []
        for (e, k), op in last.items():
            op.inc_needed = True
            if k in ("d", "cc"):
                self.bar_waits.append((e, k, op.idx, self.seq[(e, k)]))
                self.dirty[(e, k)] = True
            else:
                self.bar_waits.append((e, k, op.idx, None))

    def add(self, engine, fn, reads=(), writes=(), kind="c"):
        key = (engine, kind)
        idx = self.seq.get(key, 0)
        self.seq[key] = idx + 1
        op = _Op(engine, fn, reads, writes, kind, idx)
        deps = set()

        def lookup(res):
            name, tag_ = _split(res)
            st = self.state.setdefault(name, {})
            if tag_ == "*":
                return st, list(st.keys())
            keys = [tag_]
            if "*" in st:
                keys.append("*")
            return st, keys

        for res in op.reads:
            st, keys = lookup(res)
            for k in keys:
                ent = st.get(k)
                if ent and ent["w"] is not None:
                    deps.add(ent["w"])
        for res in op.writes:
            st, keys = lookup(res)
            for k in keys:
                ent = st.get(k)
                if ent:
                    if ent["w"] is not None:
                        deps.add(ent["w"])
                    for r in ent["r"]:
                        deps.add(r)
        for res in op.reads:
            name, tag_ = _split(res)
            ent = self.state.setdefault(name, {}).setdefault(
                tag_, {"w": None, "r": []})
            ent["r"].append(op)
        for res in op.writes:
            name, tag_ = _split(res)
            st = self.state.setdefault(name, {})
            if tag_ == "*":
                st.clear()
            ent = st.setdefault(tag_, {"w": None, "r": []})
            ent["w"] = op
            ent["r"] = []

        if op.kind in ("d", "cc"):
            kk = (engine, op.kind)
            if self.dirty.get(kk):
                op.pre_wait = idx
                self.dirty[kk] = False
        for dep in deps:
            if dep is op:
                continue
            if dep.engine == op.engine and dep.kind == "d" and op.kind == "d":
                continue
            if (dep.engine == "tensor" and op.engine == "tensor"
                    and dep.kind == "c" and op.kind == "c"):
                continue
            dep.inc_needed = True
            if dep.kind in ("d", "cc"):
                kk = (dep.engine, dep.kind)
                op.waits.append((dep.engine, dep.kind, dep.idx,
                                 self.seq[kk]))
                self.dirty[kk] = True
            else:
                op.waits.append((dep.engine, dep.kind, dep.idx, None))
        for bw in self.bar_waits:
            if not (bw[0] == op.engine and bw[1] == "c" == op.kind):
                op.waits.append(bw)
        self.ops.append(op)
        return op

    def dma(self, engine, out, in_, reads=(), writes=()):
        return self.add(engine, lambda e: e.dma_start(out=out, in_=in_),
                        reads=reads, writes=writes, kind="d")

    def emit(self):
        import bisect
        nc = self.nc
        targets = {}
        for op in self.ops:
            if op.inc_needed:
                targets.setdefault((op.engine, op.kind), []).append(op.idx)
        for v in targets.values():
            v.sort()
        sem_cms, sems = [], {}
        for key in targets:
            cm = nc.semaphore(f"s_{key[0]}_{key[1]}")
            sems[key] = cm.__enter__()
            sem_cms.append(cm)

        def sig(engine, kind, idx, override=None):
            if override is not None:
                step = 16 if kind == "d" else 1
                return sems[(engine, kind)], override * step
            lst = targets[(engine, kind)]
            pos = bisect.bisect_left(lst, idx)
            rank = pos + 1
            step = 16 if kind == "d" else 1
            return sems[(engine, kind)], rank * step

        per_engine = {e: [] for e in ENGINES}
        for op in self.ops:
            per_engine[op.engine].append(op)

        inc_step = {"c": 1, "d": 16, "cc": 1}
        blk_cm = nc.Block()
        block = blk_cm.__enter__()
        for engine in ENGINES:
            ops = per_engine[engine]
            if not ops:
                continue

            def make_section(ops):
                def section(eng):
                    hw = {}
                    my_tot = {}
                    for op in ops:
                        if op.pre_wait:
                            kk = (op.engine, op.kind)
                            if kk in sems:
                                step = 16 if op.kind == "d" else 1
                                val = step * op.pre_wait
                                if hw.get(sems[kk].name, 0) < val:
                                    eng.wait_ge(sems[kk], val)
                                    hw[sems[kk].name] = val
                        for dep in op.waits:
                            sem, val = sig(*dep)
                            if hw.get(sem.name, 0) < val:
                                eng.wait_ge(sem, val)
                                hw[sem.name] = val
                        inst = op.fn(eng)
                        if op.inc_needed:
                            sem = sems[(op.engine, op.kind)]
                            step = inc_step[op.kind]
                            if op.kind == "cc":
                                inst.then_inc(sem)
                            else:
                                inst.then_inc(sem, step)
                            k = (op.engine, op.kind)
                            my_tot[k] = my_tot.get(k, 0) + step
                    for (e, k), total in my_tot.items():
                        if k in ("d", "cc"):
                            eng.wait_ge(sems[(e, k)], total)
                return section

            getattr(block, engine)(make_section(ops))
        blk_cm.__exit__(None, None, None)
        for cm in sem_cms:
            cm.__exit__(None, None, None)



def emit_s1(P, nc, sbp, psp, io):
    ldq = io.get("ldq", "gpsimd")
    x, w_in = io["x"], io["w_in_t"]
    lg, lb, cw, cb = io["lg_t"], io["lb_t"], io["cw_t"], io["cb_t"]
    xb_o, sz_o, xn0_o = io["xb"], io["sz"], io["xn0T"]

    NTH = 4
    TH = T // NTH
    sb, ps = sbp, psp
    xt = [sb(f"xt{i}", [128, D], F32) for i in range(2)]
    xmu = [sb(f"xmu{i}", [128, D], F32) for i in range(2)]
    sqb = [sb(f"sqb{i}", [128, D], F32) for i in range(2)]
    xn1 = [sb(f"xn1{i}", [128, D], F32) for i in range(2)]
    xn0b = [sb(f"xn0b{i}", [128, D], BF16) for i in range(2)]
    xn2b = [sb(f"xn2b{i}", [128, D], BF16) for i in range(2)]
    st = [sb(f"st{i}", [128, 12], F32) for i in range(2)]
    lgs = sb("lgs", [128, 8], F32)
    lbs = sb("lbs", [128, 8], F32)
    cws = sb("cws", [128, 32], F32)
    cbs = sb("cbs", [128, 8], F32)
    identB = sb("identB", [128, 128], BF16)
    xnT = sb("xnT", [128, 8 * TH], BF16)
    x0T = sb("x0T", [128, 8 * TH], BF16)
    wt = [sb(f"wt{i}", [128, 1024], BF16) for i in range(2)]
    strip = sb("strip", [128, 3 + TH], F32)
    carry = sb("carry", [128, 8 * 3], F32)
    acc = [sb(f"acc{i}", [128, TH], F32) for i in range(2)]
    xbb = sb("xbb", [128, TH], BF16)
    zbb = sb("zbb", [128, TH], BF16)
    psT = [ps(f"psT{i}", [128, 128], BF16) for i in range(2)]
    psM = [ps(f"psM{i}", [128, 512], F32) for i in range(2)]

    for t_, name, src in ((lgs, "lgs", lg), (lbs, "lbs", lb),
                          (cws, "cws", cw), (cbs, "cbs", cb)):
        P.dma("sync", t_[:], src[:, :], reads=[], writes=[name])
    P.add("gpsimd", lambda e: e.memset(identB[:], 0.0), writes=["identB"])
    P.add("gpsimd", lambda e: e.affine_select(
        out=identB[:], in_=identB[:], compare_op=ALU.not_equal, fill=1.0,
        base=0, pattern=[[-1, 128]], channel_multiplier=1),
        reads=["identB"], writes=["identB"])
    P.add("gpsimd", lambda e: e.memset(carry[:], 0.0), writes=["carry"])

    def ln_plain(q, xin, xname, xout, oname):
        s, sn = st[q], f"st{q}"
        P.add("vector", lambda e: e.reduce_sum(s[:, 0:1], xin[:], axis=AX.X),
              reads=[xname], writes=[sn + "@a"])
        P.add("vector", lambda e: e.tensor_scalar(
            s[:, 1:2], s[:, 0:1], 1.0 / D, None, ALU.mult),
            reads=[sn + "@a"], writes=[sn + "@b"])
        P.add("vector", lambda e: e.tensor_scalar(
            xmu[q][:], xin[:], s[:, 1:2], None, ALU.subtract),
            reads=[xname, sn + "@b"], writes=[f"xmu{q}"])
        P.add("scalar", lambda e: e.activation(
            sqb[q][:], xmu[q][:], AF.Square, accum_out=s[:, 2:3]),
            reads=[f"xmu{q}"], writes=[f"sqb{q}", sn + "@c"])
        P.add("vector", lambda e: e.tensor_scalar(
            s[:, 5:6], s[:, 2:3], 1.0 / D, EPS, ALU.mult, ALU.add),
            reads=[sn + "@c"], writes=[sn + "@f"])
        P.add("scalar", lambda e: e.sqrt(s[:, 3:4], s[:, 5:6]),
              reads=[sn + "@f"], writes=[sn + "@d"])
        P.add("vector", lambda e: e.reciprocal(s[:, 4:5], s[:, 3:4]),
              reads=[sn + "@d"], writes=[sn + "@e"])
        P.add("vector", lambda e: e.tensor_scalar(
            xout[:], xmu[q][:], s[:, 4:5], None, ALU.mult),
            reads=[f"xmu{q}", sn + "@e"], writes=[oname])

    for th in range(NTH):
        for tt in range(TH // 128):
            q = tt % 2
            tg = th * TH + tt * 128
            P.dma("sync", xt[q][:], x[tg:tg + 128, :],
                  reads=[], writes=[f"xt{q}"])
            ln_plain(q, xt[q], f"xt{q}", xn1[q], f"xn1{q}")
            P.add("vector", lambda e, q=q: e.tensor_copy(
                xn0b[q][:], xn1[q][:]),
                reads=[f"xn1{q}"], writes=[f"xn0b{q}"])
            ln_plain(q, xn1[q], f"xn1{q}", xn2b[q], f"xn2b{q}")
            for dc in range(8):
                col = dc * TH + tt * 128
                P.add("tensor", lambda e, q=q, dc=dc: e.transpose(
                    psT[q][:], xn2b[q][:, dc * 128:(dc + 1) * 128],
                    identB[:]),
                    reads=[f"xn2b{q}", "identB"], writes=[f"psT{q}"])
                P.add("vector", lambda e, q=q, dc=dc, col=col: e.tensor_scalar(
                    xnT[:, col:col + 128], psT[q][:],
                    lgs[:, dc:dc + 1], lbs[:, dc:dc + 1], ALU.mult, ALU.add),
                    reads=[f"psT{q}", "lgs", "lbs"], writes=["xnT"])
                P.add("tensor", lambda e, q=q, dc=dc: e.transpose(
                    psT[q][:], xn0b[q][:, dc * 128:(dc + 1) * 128],
                    identB[:]),
                    reads=[f"xn0b{q}", "identB"], writes=[f"psT{q}"])
                P.add("vector", lambda e, q=q, col=col: e.tensor_copy(
                    x0T[:, col:col + 128], psT[q][:]),
                    reads=[f"psT{q}"], writes=["x0T"])
        for dc in range(8):
            P.dma("sync", xn0_o[dc * 128:(dc + 1) * 128,
                                th * TH:(th + 1) * TH],
                  x0T[:, dc * TH:(dc + 1) * TH],
                  reads=["x0T"], writes=["xn0_o"])

        for oc in range(16):
            q = oc % 2
            P.dma(ldq, wt[q][:], w_in[oc, :, :],
                  reads=[], writes=[f"wt{q}"])
            for t5 in range(TH // 512):
                pq = t5 % 2
                for dc in range(8):
                    P.add("tensor",
                          lambda e, q=q, pq=pq, dc=dc, t5=t5: e.matmul(
                              psM[pq][:], wt[q][:, dc * 128:(dc + 1) * 128],
                              xnT[:, dc * TH + t5 * 512: dc * TH + t5 * 512 + 512],
                              start=(dc == 0), stop=(dc == 7)),
                          reads=[f"wt{q}", "xnT"], writes=[f"psM{pq}"])
                if oc < 8:
                    P.add("vector", lambda e, pq=pq, t5=t5: e.tensor_copy(
                        strip[:, 3 + t5 * 512: 3 + t5 * 512 + 512],
                        psM[pq][:]),
                        reads=[f"psM{pq}"], writes=["strip"])
                else:
                    P.add("scalar", lambda e, pq=pq, t5=t5: e.activation(
                        zbb[:, t5 * 512: t5 * 512 + 512], psM[pq][:],
                        AF.Silu),
                        reads=[f"psM{pq}"], writes=["zbb"])
            if oc < 8:
                if th == 0:
                    P.add("gpsimd", lambda e: e.memset(strip[:, 0:3], 0.0),
                          reads=[], writes=["strip"])
                else:
                    P.add("vector", lambda e, oc=oc: e.tensor_copy(
                        strip[:, 0:3], carry[:, oc * 3: oc * 3 + 3]),
                        reads=["carry"], writes=["strip"])
                P.add("vector", lambda e, oc=oc: e.tensor_copy(
                    carry[:, oc * 3: oc * 3 + 3], strip[:, TH: TH + 3]),
                    reads=["strip"], writes=["carry"])
                P.add("vector", lambda e, oc=oc: e.tensor_scalar(
                    acc[0][:], strip[:, 0: TH], cws[:, oc * 4: oc * 4 + 1],
                    None, ALU.mult),
                    reads=["strip", "cws"], writes=["acc0"])
                cur = 0
                for j in range(1, 4):
                    nxt = 1 - cur
                    P.add("vector",
                          lambda e, oc=oc, j=j, cur=cur, nxt=nxt:
                          e.scalar_tensor_tensor(
                              acc[nxt][:], strip[:, j: j + TH],
                              cws[:, oc * 4 + j: oc * 4 + j + 1],
                              acc[cur][:], ALU.mult, ALU.add),
                          reads=["strip", "cws", f"acc{cur}"],
                          writes=[f"acc{nxt}"])
                    cur = nxt
                P.add("scalar", lambda e, oc=oc, cur=cur: e.activation(
                    xbb[:], acc[cur][:], AF.Silu, bias=cbs[:, oc:oc + 1]),
                    reads=[f"acc{cur}", "cbs"], writes=["xbb"])
                P.dma("sync", xb_o[oc * 128:(oc + 1) * 128,
                                   th * TH:(th + 1) * TH], xbb[:],
                      reads=["xbb"], writes=["xb_o"])
            else:
                P.dma("sync", sz_o[(oc - 8) * 128:(oc - 7) * 128,
                                   th * TH:(th + 1) * TH], zbb[:],
                      reads=["zbb"], writes=["sz_o"])


def build_s1(p=0):
    nc = bass.Bass("TRN2", target_bir_lowering=False, debug=False)
    P = Prog(nc)
    io = {
        "x": nc.dram_tensor("x", [T, D], F32, kind="ExternalInput"),
        "w_in_t": nc.dram_tensor("w_in_t", [16, 128, 1024], F32,
                                 kind="ExternalInput"),
        "lg_t": nc.dram_tensor("lg_t", [128, 8], F32, kind="ExternalInput"),
        "lb_t": nc.dram_tensor("lb_t", [128, 8], F32, kind="ExternalInput"),
        "cw_t": nc.dram_tensor("cw_t", [128, 32], F32, kind="ExternalInput"),
        "cb_t": nc.dram_tensor("cb_t", [128, 8], F32, kind="ExternalInput"),
        "xb": nc.dram_tensor("xb", [DI // 2, T], BF16, kind="ExternalOutput"),
        "sz": nc.dram_tensor("sz", [DI // 2, T], BF16, kind="ExternalOutput"),
        "xn0T": nc.dram_tensor("xn0", [D, T], BF16, kind="ExternalOutput"),
    }
    emit_s1(P, nc, nc.alloc_sbuf_tensor, nc.alloc_psum_tensor, io)
    P.emit()
    return nc



def emit_scan(P, nc, sbp, psp, io):
    ldq = io.get("ldq", "gpsimd")
    xbF, xbO, szO, x0T = io["xbF"], io["xbO"], io["szO"], io["xn0T"]
    wd, bd, wb, wc = io["wd_t"], io["bd_t"], io["wb_t"], io["wc_t"]
    la, dp, wo = io["la_t"], io["dp_t"], io["wo_t"]
    part_ap = io["part_ap"]
    bscr = io["bscr"]

    TB = 512
    NTB = T // TB
    sb, ps = sbp, psp
    xbB = [sb(f"xbB{i}", [128, 16 * TB], BF16) for i in range(2)]
    xboB = sb("xboB", [128, 8 * TB], BF16)
    szB = sb("szB", [128, 8 * TB], BF16)
    x0B = [sb(f"x0B{i}", [128, 512], BF16) for i in range(2)]
    dsb = [sb(f"dsb{i}", [128, 8 * TB], F32) for i in range(2)]
    dxb = [sb(f"dxb{i}", [128, 8 * TB], BF16) for i in range(2)]
    yB = sb("yB", [128, 8 * TB], F32)
    PBs = [sb(f"PBs{i}", [128, TB], BF16) for i in range(2)]
    PCs = [sb(f"PCs{i}", [128, TB], BF16) for i in range(2)]
    aT = [sb(f"aT{i}", [128, TB], F32) for i in range(2)] * 2
    bxT = [sb(f"bxT{i}", [128, TB], BF16) for i in range(4)]
    hT = [sb(f"hT{i}", [128, TB], BF16) for i in range(4)]
    hCT = [sb(f"hCT{i}", [128, TB], BF16) for i in range(4)]
    gB = sb("gB", [128, 8 * TB], BF16)
    pevac = [sb(f"pevac{i}", [128, 512], BF16) for i in range(2)]
    wds = sb("wds", [128, 8 * 2048], BF16)
    wos = sb("wos", [128, 8 * 1024], BF16)
    wbs = sb("wbs", [128, 256], BF16)
    wcs = sb("wcs", [128, 256], BF16)
    bds = sb("bds", [128, 8], F32)
    dps = sb("dps", [128, 8], F32)
    asb = sb("asb", [128, 128], F32)
    seeds = sb("seeds", [128, 128], F32)
    ones1 = sb("ones1", [1, 128], BF16)
    brow = [sb(f"brow{i}", [1, TB], BF16) for i in range(2)]
    bsb = sb("bsb", [16, TB], BF16)
    csb = sb("csb", [16, TB], BF16)
    spA = [sb(f"spA{i}", [128, 512], F32) for i in range(2)]
    spB = [sb(f"spB{i}", [128, 512], F32) for i in range(2)]
    ps_ = [ps(f"ps{i}", [128, 512], F32) for i in range(6)]

    for i in range(8):
        P.dma(ldq, wds[:, i * 2048:(i + 1) * 2048], wd[i, :, :],
              writes=["wds"])
        P.dma(ldq, wos[:, i * 1024:(i + 1) * 1024], wo[i, :, :],
              writes=["wos"])
    P.dma(ldq, wbs[:], wb[:, :], writes=["wbs"])
    P.dma(ldq, wcs[:], wc[:, :], writes=["wcs"])
    P.dma("sync", bds[:], bd[:, :], writes=["bds"])
    P.dma("sync", dps[:], dp[:, :], writes=["dps"])
    P.dma("sync", asb[:], la[:, :], writes=["asb"])
    P.add("scalar", lambda e: e.activation(asb[:], asb[:], AF.Exp),
          reads=["asb"], writes=["asb"])
    P.add("vector", lambda e: e.tensor_scalar(
        asb[:], asb[:], -1.0, None, ALU.mult), reads=["asb"], writes=["asb"])
    P.add("gpsimd", lambda e: e.memset(seeds[:], 0.0), writes=["seeds"])
    P.add("gpsimd", lambda e: e.memset(ones1[:], 1.0), writes=["ones1"])

    for tb in range(NTB):
        t0 = tb * TB
        tq = tb % 2
        xbBt, dsbt, dxbt = xbB[tq], dsb[tq], dxb[tq]
        for ch in range(16):
            P.dma("sync", xbBt[:, ch * TB:(ch + 1) * TB],
                  xbF[ch * 128:(ch + 1) * 128, t0:t0 + TB], writes=[f"xbB{tq}"])
        for ch in range(8):
            P.dma("sync", xboB[:, ch * TB:(ch + 1) * TB],
                  xbO[ch * 128:(ch + 1) * 128, t0:t0 + TB], writes=["xboB"])
            P.dma("sync", szB[:, ch * TB:(ch + 1) * TB],
                  szO[ch * 128:(ch + 1) * 128, t0:t0 + TB], writes=["szB"])

        for which, wsb, wname, osb, oname in (
                (0, wbs, "wbs", bsb, "bsb"), (1, wcs, "wcs", csb, "csb")):
            pp = ps_[4]
            for dc in range(16):
                P.add("tensor", lambda e, wsb=wsb, dc=dc, pp=pp, xbBt=xbBt, dsbt=dsbt, dxbt=dxbt, tq=tq:
                      e.matmul(pp[0:16, :], wsb[:, dc * 16:(dc + 1) * 16],
                               xbBt[:, dc * TB: dc * TB + TB],
                               start=(dc == 0), stop=(dc == 15)),
                      reads=[wname, f"xbB{tq}"], writes=["ps4"])
            P.add("vector", lambda e, osb=osb, pp=pp, xbBt=xbBt, dsbt=dsbt, dxbt=dxbt, tq=tq:
                  e.tensor_copy(osb[:], pp[0:16, :]),
                  reads=["ps4"], writes=[oname])
            P.dma("sync", bscr[which, :, :], osb[:], reads=[oname],
                  writes=[f"bscr{which}"])

        for oc in range(8):
            pp = ps_[oc % 2]
            for dc in range(16):
                P.add("tensor", lambda e, oc=oc, dc=dc, pp=pp, xbBt=xbBt, dsbt=dsbt, dxbt=dxbt, tq=tq:
                      e.matmul(pp[:],
                               wds[:, oc * 2048 + dc * 128:
                                   oc * 2048 + (dc + 1) * 128],
                               xbBt[:, dc * TB: dc * TB + TB],
                               start=(dc == 0), stop=(dc == 15)),
                      reads=["wds", f"xbB{tq}"], writes=[f"ps{oc % 2}"])
            q = oc % 2
            P.add("scalar", lambda e, oc=oc, pp=pp, q=q, xbBt=xbBt, dsbt=dsbt, dxbt=dxbt, tq=tq: e.activation(
                spA[q][:], pp[:], AF.Abs, bias=bds[:, oc:oc + 1]),
                reads=[f"ps{oc % 2}", "bds"], writes=[f"spA{q}"])
            P.add("scalar", lambda e, q=q, xbBt=xbBt, dsbt=dsbt, dxbt=dxbt, tq=tq: e.activation(
                spA[q][:], spA[q][:], AF.Exp, scale=-1.0),
                reads=[f"spA{q}"], writes=[f"spA{q}"])
            P.add("scalar", lambda e, q=q, xbBt=xbBt, dsbt=dsbt, dxbt=dxbt, tq=tq: e.activation(
                spA[q][:], spA[q][:], AF.Ln, bias=1.0),
                reads=[f"spA{q}"], writes=[f"spA{q}"])
            P.add("scalar", lambda e, oc=oc, pp=pp, q=q, xbBt=xbBt, dsbt=dsbt, dxbt=dxbt, tq=tq: e.activation(
                spB[q][:], pp[:], AF.Relu, bias=bds[:, oc:oc + 1]),
                reads=[f"ps{oc % 2}", "bds"], writes=[f"spB{q}"])
            P.add("vector", lambda e, oc=oc, q=q, xbBt=xbBt, dsbt=dsbt, dxbt=dxbt, tq=tq: e.tensor_add(
                dsbt[:, oc * TB:(oc + 1) * TB], spA[q][:], spB[q][:]),
                reads=[f"spA{q}", f"spB{q}"], writes=[f"dsb{tq}"])
            P.add("vector", lambda e, oc=oc, xbBt=xbBt, dsbt=dsbt, dxbt=dxbt, tq=tq: e.tensor_mul(
                dxbt[:, oc * TB:(oc + 1) * TB], dsbt[:, oc * TB:(oc + 1) * TB],
                xboB[:, oc * TB:(oc + 1) * TB]),
                reads=[f"dsb{tq}", "xboB"], writes=[f"dxb{tq}"])

        for n in range(NS):
            nq = n % 2
            for which, plane, pname in ((0, PBs[nq], f"PBs{nq}"),
                                        (1, PCs[nq], f"PCs{nq}")):
                P.dma("sync", brow[which][:], bscr[which, n:n + 1, :],
                      reads=[f"bscr{which}"], writes=[f"brow{which}"])
                pp = ps_[2 + which]
                P.add("tensor", lambda e, which=which, pp=pp, xbBt=xbBt, dsbt=dsbt, dxbt=dxbt, tq=tq:
                      e.matmul(pp[:], ones1[:], brow[which][0:1, :],
                               start=True, stop=True),
                      reads=[f"brow{which}", "ones1"],
                      writes=[f"ps{2 + which}"])
                P.add("scalar", lambda e, plane=plane, pp=pp, xbBt=xbBt, dsbt=dsbt, dxbt=dxbt, tq=tq:
                      e.activation(plane[:], pp[:], AF.Copy),
                      reads=[f"ps{2 + which}"], writes=[pname])
            for oc in range(8):
                q = (n * 8 + oc) % 4
                ncol = oc * 16 + n
                P.add("scalar", lambda e, oc=oc, ncol=ncol, q=q, xbBt=xbBt, dsbt=dsbt, dxbt=dxbt, tq=tq: e.activation(
                    aT[q][:], dsbt[:, oc * TB:(oc + 1) * TB], AF.Exp,
                    scale=asb[:, ncol:ncol + 1]),
                    reads=[f"dsb{tq}", "asb"], writes=[f"aT{q % 2}"])
                P.add("vector", lambda e, oc=oc, q=q, nq=nq, xbBt=xbBt, dsbt=dsbt, dxbt=dxbt, tq=tq: e.tensor_mul(
                    bxT[q][:], dxbt[:, oc * TB:(oc + 1) * TB], PBs[nq][:]),
                    reads=[f"dxb{tq}", f"PBs{nq}"], writes=[f"bxT{q}"])
                P.add("vector", lambda e, ncol=ncol, q=q: e.tensor_tensor_scan(
                    hT[q][:], aT[q][:], bxT[q][:], seeds[:, ncol:ncol + 1],
                    ALU.mult, ALU.add),
                    reads=[f"aT{q % 2}", f"bxT{q}", f"seeds@{ncol}"],
                    writes=[f"hT{q}"])
                P.add("scalar", lambda e, ncol=ncol, q=q, xbBt=xbBt, dsbt=dsbt, dxbt=dxbt, tq=tq: e.copy(
                    seeds[:, ncol:ncol + 1], hT[q][:, TB - 1:TB]),
                    reads=[f"hT{q}"], writes=[f"seeds@{ncol}"])
                if n == 0:
                    P.add("vector", lambda e, oc=oc, q=q, nq=nq, xbBt=xbBt, dsbt=dsbt, dxbt=dxbt, tq=tq: e.tensor_mul(
                        yB[:, oc * TB:(oc + 1) * TB], hT[q][:], PCs[nq][:]),
                        reads=[f"hT{q}", f"PCs{nq}"], writes=[f"yB@{oc}"])
                else:
                    P.add("vector", lambda e, q=q, nq=nq, xbBt=xbBt, dsbt=dsbt, dxbt=dxbt, tq=tq: e.tensor_mul(
                        hCT[q][:], hT[q][:], PCs[nq][:]),
                        reads=[f"hT{q}", f"PCs{nq}"], writes=[f"hCT{q}"])
                    P.add("gpsimd", lambda e, oc=oc, q=q: e.tensor_add(
                        yB[:, oc * TB:(oc + 1) * TB],
                        yB[:, oc * TB:(oc + 1) * TB], hCT[q][:]),
                        reads=[f"hCT{q}", f"yB@{oc}"], writes=[f"yB@{oc}"])

        for oc in range(8):
            q = oc % 2
            P.add("vector", lambda e, oc=oc, q=q, xbBt=xbBt, dsbt=dsbt, dxbt=dxbt, tq=tq: e.scalar_tensor_tensor(
                hCT[q][:], xboB[:, oc * TB:(oc + 1) * TB], dps[:, oc:oc + 1],
                yB[:, oc * TB:(oc + 1) * TB], ALU.mult, ALU.add),
                reads=["xboB", "dps", f"yB@{oc}"], writes=[f"hCT{q}"])
            P.add("vector", lambda e, oc=oc, q=q, xbBt=xbBt, dsbt=dsbt, dxbt=dxbt, tq=tq: e.tensor_mul(
                gB[:, oc * TB:(oc + 1) * TB], hCT[q][:],
                szB[:, oc * TB:(oc + 1) * TB]),
                reads=[f"hCT{q}", "szB"], writes=["gB"])

        for do in range(8):
            pp = ps_[4 + do % 2]
            P.dma("sync", x0B[do % 2][:],
                  x0T[do * 128:(do + 1) * 128, t0:t0 + TB],
                  writes=[f"x0B{do % 2}"])
            for dc in range(8):
                P.add("tensor", lambda e, do=do, dc=dc, pp=pp, xbBt=xbBt, dsbt=dsbt, dxbt=dxbt, tq=tq:
                      e.matmul(pp[:],
                               wos[:, do * 1024 + dc * 128:
                                   do * 1024 + (dc + 1) * 128],
                               gB[:, dc * TB: dc * TB + TB],
                               start=(dc == 0), stop=(dc == 7)),
                      reads=["wos", "gB"], writes=[f"ps{4 + do % 2}"])
            P.add("vector", lambda e, pp=pp, do=do, xbBt=xbBt, dsbt=dsbt, dxbt=dxbt, tq=tq: e.scalar_tensor_tensor(
                pevac[do % 2][:], x0B[do % 2][:], 0.5, pp[:],
                ALU.mult, ALU.add),
                reads=[f"ps{4 + do % 2}", f"x0B{do % 2}"],
                writes=[f"pevac{do % 2}"])
            P.dma("sync", part_ap(do, t0), pevac[do % 2][:],
                  reads=[f"pevac{do % 2}"], writes=["part_o"])


def build_scan():
    nc = bass.Bass("TRN2", target_bir_lowering=False, debug=False)
    P = Prog(nc)
    io = {
        "xbF": nc.dram_tensor("xbF", [DI, T], BF16, kind="ExternalInput"),
        "xbO": nc.dram_tensor("xbO", [DI // 2, T], BF16, kind="ExternalInput"),
        "szO": nc.dram_tensor("sz", [DI // 2, T], BF16, kind="ExternalInput"),
        "xn0T": nc.dram_tensor("xn0", [D, T], BF16, kind="ExternalInput"),
        "wd_t": nc.dram_tensor("wd_t", [8, 128, 2048], F32, kind="ExternalInput"),
        "bd_t": nc.dram_tensor("bd_t", [128, 8], F32, kind="ExternalInput"),
        "wb_t": nc.dram_tensor("wb_t", [128, 256], F32, kind="ExternalInput"),
        "wc_t": nc.dram_tensor("wc_t", [128, 256], F32, kind="ExternalInput"),
        "la_t": nc.dram_tensor("la_t", [128, 128], F32, kind="ExternalInput"),
        "dp_t": nc.dram_tensor("dp_t", [128, 8], F32, kind="ExternalInput"),
        "wo_t": nc.dram_tensor("wo_t", [8, 128, 1024], F32, kind="ExternalInput"),
        "bscr": nc.dram_tensor("bscr", [2, 16, 512], BF16, kind="Internal"),
    }
    part = nc.dram_tensor("part", [2, D, T // 2], BF16, kind="ExternalOutput")
    io["part_ap"] = lambda do, t0: part[t0 // 4096, do * 128:(do + 1) * 128,
                                        t0 % 4096: t0 % 4096 + 512]
    emit_scan(P, nc, nc.alloc_sbuf_tensor, nc.alloc_psum_tensor, io)
    P.emit()
    return nc


def stage_scan_inputs(inputs, cid, xb_even_first, xb_own, sz_own, xn0T):
    b, p = cid >> 1, cid & 1
    own = slice(p * 1024, (p + 1) * 1024)
    Wd = np.asarray(inputs["L0_attn_W_delta"])[:, own]
    Wo = np.asarray(inputs["L0_attn_W_out"])[own]
    return {
        "xbF": xb_even_first,
        "xbO": xb_own,
        "sz": sz_own,
        "xn0": xn0T,
        "wd_t": tile_lhsT(Wd).reshape(8, 128, 2048),
        "bd_t": vec_t(np.asarray(inputs["L0_attn_b_delta"])[own]),
        "wb_t": np.ascontiguousarray(
            np.asarray(inputs["L0_attn_W_b"]).reshape(16, 128, 16)
            .transpose(1, 0, 2).reshape(128, 256)),
        "wc_t": np.ascontiguousarray(
            np.asarray(inputs["L0_attn_W_c"]).reshape(16, 128, 16)
            .transpose(1, 0, 2).reshape(128, 256)),
        "la_t": np.ascontiguousarray(
            np.asarray(inputs["L0_attn_log_A"])[own].reshape(8, 128, 16)
            .transpose(1, 0, 2).reshape(128, 128)),
        "dp_t": vec_t(np.asarray(inputs["L0_attn_Dp"])[own]),
        "wo_t": tile_lhsT(Wo).reshape(8, 128, 1024),
    }



def emit_combine(P, nc, sbp, psp, io, want_xn, summed):
    TO = T // 2
    rp = io["rp"]
    rn_o = io["rn"]
    sb, ps = sbp, psp

    pat = [sb(f"pat{i}", [128, 128], BF16) for i in range(2)]
    pbt = [sb(f"pbt{i}", [128, 128], BF16) for i in range(2)]
    sumb = sb("sumb", [128, 128], BF16)
    mo = [sb(f"mo{i}", [128, D], F32) for i in range(2)]
    rt = [sb(f"rt{i}", [128, D], F32) for i in range(2)]
    xmu = [sb(f"xmu{i}", [128, D], F32) for i in range(2)]
    sqb = [sb(f"sqb{i}", [128, D], F32) for i in range(2)]
    lnv = [sb(f"lnv{i}", [128, D], F32) for i in range(2)]
    rnb = [sb(f"rnb{i}", [128, D], F32) for i in range(2)]
    xnb = [sb(f"xnb{i}", [128, D], BF16) for i in range(2)]
    st = [sb(f"st{i}", [128, 12], F32) for i in range(2)]
    identB = sb("identB", [128, 128], BF16)
    psT = [ps(f"psT{i}", [128, 128], BF16) for i in range(2)]

    P.add("gpsimd", lambda e: e.memset(identB[:], 0.0), writes=["identB"])
    P.add("gpsimd", lambda e: e.affine_select(
        out=identB[:], in_=identB[:], compare_op=ALU.not_equal, fill=1.0,
        base=0, pattern=[[-1, 128]], channel_multiplier=1),
        reads=["identB"], writes=["identB"])

    def ln_plain(q, xin, xname, xout, oname):
        s, sn = st[q], f"st{q}"
        P.add("vector", lambda e: e.reduce_sum(s[:, 0:1], xin[:], axis=AX.X),
              reads=[xname], writes=[sn + "@a"])
        P.add("vector", lambda e: e.tensor_scalar(
            s[:, 1:2], s[:, 0:1], 1.0 / D, None, ALU.mult),
            reads=[sn + "@a"], writes=[sn + "@b"])
        P.add("vector", lambda e: e.tensor_scalar(
            xmu[q][:], xin[:], s[:, 1:2], None, ALU.subtract),
            reads=[xname, sn + "@b"], writes=[f"xmu{q}"])
        P.add("scalar", lambda e: e.activation(
            sqb[q][:], xmu[q][:], AF.Square, accum_out=s[:, 2:3]),
            reads=[f"xmu{q}"], writes=[f"sqb{q}", sn + "@c"])
        P.add("vector", lambda e: e.tensor_scalar(
            s[:, 5:6], s[:, 2:3], 1.0 / D, EPS, ALU.mult, ALU.add),
            reads=[sn + "@c"], writes=[sn + "@f"])
        P.add("scalar", lambda e: e.sqrt(s[:, 3:4], s[:, 5:6]),
              reads=[sn + "@f"], writes=[sn + "@d"])
        P.add("vector", lambda e: e.reciprocal(s[:, 4:5], s[:, 3:4]),
              reads=[sn + "@d"], writes=[sn + "@e"])
        P.add("vector", lambda e: e.tensor_scalar(
            xout[:], xmu[q][:], s[:, 4:5], None, ALU.mult),
            reads=[f"xmu{q}", sn + "@e"], writes=[oname])

    for tt in range(TO // 128):
        q = tt % 2
        t0 = tt * 128
        for dc in range(8):
            dq = dc % 2
            if summed:
                pa = io["psum"]
                P.dma("sync", pat[dq][:],
                      pa[dc * 128:(dc + 1) * 128, t0:t0 + 128],
                      writes=[f"pat{dq}"])
                P.add("tensor", lambda e, q=q, dq=dq: e.transpose(
                    psT[q][:], pat[dq][:], identB[:]),
                    reads=[f"pat{dq}", "identB"], writes=[f"psT{q}"])
            else:
                P.dma("sync", pat[dq][:],
                      io["pa"][dc * 128:(dc + 1) * 128, t0:t0 + 128],
                      writes=[f"pat{dq}"])
                P.dma("sync", pbt[dq][:],
                      io["pb"][dc * 128:(dc + 1) * 128, t0:t0 + 128],
                      writes=[f"pbt{dq}"])
                P.add("vector", lambda e, dq=dq: e.tensor_add(
                    sumb[:], pat[dq][:], pbt[dq][:]),
                    reads=[f"pat{dq}", f"pbt{dq}"], writes=["sumb"])
                P.add("tensor", lambda e, q=q: e.transpose(
                    psT[q][:], sumb[:], identB[:]),
                    reads=["sumb", "identB"], writes=[f"psT{q}"])
            P.add("vector", lambda e, q=q, dc=dc: e.tensor_copy(
                mo[q][:, dc * 128:(dc + 1) * 128], psT[q][:]),
                reads=[f"psT{q}"], writes=[f"mo{q}"])
        ln_plain(q, mo[q], f"mo{q}", lnv[q], f"lnv{q}")
        P.dma("sync", rt[q][:], rp[t0:t0 + 128, :], writes=[f"rt{q}"])
        P.add("vector", lambda e, q=q: e.tensor_add(
            rnb[q][:], rt[q][:], lnv[q][:]),
            reads=[f"rt{q}", f"lnv{q}"], writes=[f"rnb{q}"])
        P.dma("sync", rn_o[t0:t0 + 128, :], rnb[q][:],
              reads=[f"rnb{q}"], writes=["rn_o"])
        if want_xn:
            xn_ap = io["xn_ap"]
            ln_plain(q, rnb[q], f"rnb{q}", xnb[q], f"xnb{q}")
            for dc in range(8):
                P.add("tensor", lambda e, q=q, dc=dc: e.transpose(
                    psT[q][:], xnb[q][:, dc * 128:(dc + 1) * 128], identB[:]),
                    reads=[f"xnb{q}", "identB"], writes=[f"psT{q}"])
                P.add("vector", lambda e, q=q: e.tensor_copy(
                    sumb[:], psT[q][:]),
                    reads=[f"psT{q}"], writes=["sumb"])
                P.dma("sync", xn_ap(dc, t0), sumb[:],
                      reads=["sumb"], writes=["xn_o"])


def build_combine(want_xn, summed=False):
    nc = bass.Bass("TRN2", target_bir_lowering=False, debug=False)
    P = Prog(nc)
    TO = T // 2
    io = {"rp": nc.dram_tensor("rp", [TO, D], F32, kind="ExternalInput"),
          "rn": nc.dram_tensor("rn", [TO, D], F32, kind="ExternalOutput")}
    if summed:
        io["psum"] = nc.dram_tensor("psm", [D, TO], BF16, kind="ExternalInput")
    else:
        io["pa"] = nc.dram_tensor("pa", [D, TO], BF16, kind="ExternalInput")
        io["pb"] = nc.dram_tensor("pb", [D, TO], BF16, kind="ExternalInput")
    if want_xn:
        xn_o = nc.dram_tensor("xnx", [D, TO], BF16, kind="ExternalOutput")
        io["xn_ap"] = lambda dc, t0: xn_o[dc * 128:(dc + 1) * 128, t0:t0 + 128]
    emit_combine(P, nc, nc.alloc_sbuf_tensor, nc.alloc_psum_tensor, io,
                 want_xn, summed)
    P.emit()
    return nc



def emit_moe(P, nc, sbp, psp, io):
    ldq = io.get("ldq", "gpsimd")
    sb, ps = sbp, psp
    gw, sel = io["gw_t"], io["sel_t"]
    w1, w3, w2 = io["w1_t"], io["w3_t"], io["w2_t"]
    xn_src = io["xn_src"]
    part_ap = io["part_ap"]

    TBK = 2048
    NBK = T // TBK
    SUP = (8, 8, 8, 8, 8, 4)

    gws = sb("gws", [128, 32], F32)
    xng = sb("xng", [128, 8 * 128], F32)
    sels = sb("sels", [4, 256], F32)
    xnB = sb("xnB", [128, 8 * TBK], BF16)
    lg4 = sb("lg4", [128, 4], F32)
    cw4 = sb("cw4", [128, 4], F32)
    cwT = sb("cwT", [4, T], F32)
    pl = [sb(f"pl{i}", [128, TBK], F32) for i in range(2)]
    sil = [sb(f"sil{i}", [128, 512], F32) for i in range(2)]
    hstr = sb("hstr", [128, 8 * TBK], BF16)
    oac = sb("oac", [128, 8 * TBK], F32)
    tmp5 = [sb(f"tmp5{i}", [128, 512], F32) for i in range(2)]
    pevac = [sb(f"pevac{i}", [128, 512], BF16) for i in range(2)]
    w13s = [sb(f"w13s{i}", [128, 1024], BF16) for i in range(4)]
    w2s = [sb(f"w2s{i}", [128, 1024], BF16) for i in range(2)]
    stm = sb("stm", [128, 10], F32)
    cmpb = sb("cmpb", [128, 4], F32)
    identF = sb("identF", [128, 128], F32)
    ps_ = [ps(f"ps{i}", [128, 512], F32) for i in range(8)]

    P.dma("sync", gws[:], gw[:, :], writes=["gws"])
    P.dma("sync", sels[:], sel[:, :], writes=["sels"])
    P.add("gpsimd", lambda e: e.memset(identF[:], 0.0), writes=["identF"])
    P.add("gpsimd", lambda e: e.affine_select(
        out=identF[:], in_=identF[:], compare_op=ALU.not_equal, fill=1.0,
        base=0, pattern=[[-1, 128]], channel_multiplier=1),
        reads=["identF"], writes=["identF"])

    for tt in range(T // 128):
        t0 = tt * 128
        pp = ps_[0]
        for dc in range(8):
            P.dma("gpsimd", xng[:, dc * 128:(dc + 1) * 128],
                  xn_src(dc, t0, 128), writes=["xng"])
        for dc in range(8):
            P.add("tensor", lambda e, dc=dc, pp=pp: e.matmul(
                pp[0:128, 0:4], xng[:, dc * 128:(dc + 1) * 128],
                gws[:, dc * 4:(dc + 1) * 4],
                start=(dc == 0), stop=(dc == 7)),
                reads=["xng", "gws"], writes=["ps0"])
        P.add("vector", lambda e, pp=pp: e.tensor_copy(lg4[:], pp[0:128, 0:4]),
              reads=["ps0"], writes=["lg4"])
        P.add("gpsimd", lambda e: e.memset(cw4[:], 0.0), writes=["cw4"])
        for e_ in range(4):
            for j in range(4):
                if j == e_:
                    continue
                op = ALU.is_ge if j < e_ else ALU.is_gt
                P.add("vector", lambda e, e_=e_, j=j, op=op: e.tensor_tensor(
                    cmpb[:, e_:e_ + 1], lg4[:, j:j + 1], lg4[:, e_:e_ + 1], op),
                    reads=["lg4"], writes=[f"cmpb@{e_}"])
                P.add("vector", lambda e, e_=e_: e.tensor_add(
                    cw4[:, e_:e_ + 1], cw4[:, e_:e_ + 1], cmpb[:, e_:e_ + 1]),
                    reads=[f"cmpb@{e_}"], writes=["cw4"])
        P.add("vector", lambda e: e.tensor_scalar(
            cw4[:], cw4[:], 1.5, None, ALU.is_lt),
            reads=["cw4"], writes=["cw4"])
        P.add("vector", lambda e: e.reduce_max(
            stm[:, 0:1], lg4[:], axis=AX.X, negate=True),
            reads=["lg4"], writes=["stm@a"])
        P.add("scalar", lambda e: e.activation(
            cmpb[:], lg4[:], AF.Exp, bias=stm[:, 0:1]),
            reads=["lg4", "stm@a"], writes=["cmpb"])
        P.add("vector", lambda e: e.tensor_mul(cw4[:], cw4[:], cmpb[:]),
              reads=["cw4", "cmpb"], writes=["cw4"])
        P.add("vector", lambda e: e.reduce_sum(
            stm[:, 1:2], cw4[:], axis=AX.X),
            reads=["cw4"], writes=["stm@b"])
        P.add("vector", lambda e: e.reciprocal(stm[:, 2:3], stm[:, 1:2]),
              reads=["stm@b"], writes=["stm@c"])
        P.add("vector", lambda e: e.tensor_scalar(
            cw4[:], cw4[:], stm[:, 2:3], None, ALU.mult),
            reads=["cw4", "stm@c"], writes=["cw4"])
        P.add("tensor", lambda e, pp=pp: e.transpose(
            pp[0:4, 0:128], cw4[:], identF[:]),
            reads=["cw4", "identF"], writes=["ps0"])
        P.add("vector", lambda e, t0=t0, pp=pp: e.tensor_copy(
            cwT[:, t0:t0 + 128], pp[0:4, 0:128]),
            reads=["ps0"], writes=["cwT"])

    for bk in range(NBK):
        t0 = bk * TBK
        for dc in range(8):
            P.dma("sync", xnB[:, dc * TBK:(dc + 1) * TBK],
                  xn_src(dc, t0, TBK), writes=["xnB"])
        for le in range(2):
            for t5 in range(4):
                pp = ps_[5]
                P.add("tensor", lambda e, le=le, t5=t5, t0=t0, pp=pp:
                      e.matmul(pp[:],
                               sels[:, le * 128:(le + 1) * 128],
                               cwT[:, t0 + t5 * 512: t0 + t5 * 512 + 512],
                               start=True, stop=True),
                      reads=["sels", "cwT"], writes=["ps5"])
                P.add("vector", lambda e, le=le, t5=t5, pp=pp: e.tensor_copy(
                    pl[le][:, t5 * 512: t5 * 512 + 512], pp[:]),
                    reads=["ps5"], writes=[f"pl{le}"])
        first = True
        for le in range(2):
            hbase = 0
            for si, sn_ in enumerate(SUP):
                for hi in range(sn_):
                    ht = hbase + hi
                    qa, qb = (ht % 2) * 2, (ht % 2) * 2 + 1
                    P.dma(ldq, w13s[qa][:], w1[le, ht, :, :],
                          writes=[f"w13s{qa}"])
                    P.dma(ldq, w13s[qb][:], w3[le, ht, :, :],
                          writes=[f"w13s{qb}"])
                    for t5 in range(4):
                        gi, ui = t5 % 2, 2 + t5 % 2
                        for dc in range(8):
                            P.add("tensor",
                                  lambda e, qa=qa, dc=dc, t5=t5, gi=gi:
                                  e.matmul(ps_[gi][:],
                                           w13s[qa][:, dc * 128:(dc + 1) * 128],
                                           xnB[:, dc * TBK + t5 * 512:
                                               dc * TBK + t5 * 512 + 512],
                                           start=(dc == 0), stop=(dc == 7)),
                                  reads=[f"w13s{qa}", "xnB"],
                                  writes=[f"ps{gi}"])
                            P.add("tensor",
                                  lambda e, qb=qb, dc=dc, t5=t5, ui=ui:
                                  e.matmul(ps_[ui][:],
                                           w13s[qb][:, dc * 128:(dc + 1) * 128],
                                           xnB[:, dc * TBK + t5 * 512:
                                               dc * TBK + t5 * 512 + 512],
                                           start=(dc == 0), stop=(dc == 7)),
                                  reads=[f"w13s{qb}", "xnB"],
                                  writes=[f"ps{ui}"])
                        P.add("scalar", lambda e, t5=t5, gi=gi: e.activation(
                            sil[t5 % 2][:], ps_[gi][:], AF.Silu),
                            reads=[f"ps{gi}"], writes=[f"sil{t5 % 2}"])
                        P.add("vector",
                              lambda e, hi=hi, t5=t5, ui=ui: e.tensor_mul(
                                  hstr[:, hi * TBK + t5 * 512:
                                       hi * TBK + t5 * 512 + 512],
                                  sil[t5 % 2][:], ps_[ui][:]),
                              reads=[f"sil{t5 % 2}", f"ps{ui}"],
                              writes=["hstr"])
                for do in range(8):
                    qd = do % 2
                    P.dma(ldq, w2s[qd][:, 0:sn_ * 128],
                          w2[le, do, :, hbase * 128:(hbase + sn_) * 128],
                          writes=[f"w2s{qd}"])
                    for t5 in range(4):
                        pi = 4 + (qd * 2 + t5 % 2) % 4
                        for hc in range(sn_):
                            P.add("tensor",
                                  lambda e, qd=qd, hc=hc, t5=t5, pi=pi:
                                  e.matmul(ps_[pi][:],
                                           w2s[qd][:, hc * 128:(hc + 1) * 128],
                                           hstr[:, hc * TBK + t5 * 512:
                                                hc * TBK + t5 * 512 + 512],
                                           start=(hc == 0), stop=(hc == sn_ - 1)),
                                  reads=[f"w2s{qd}", "hstr"],
                                  writes=[f"ps{pi}"])
                        P.add("vector",
                              lambda e, le=le, t5=t5, pi=pi: e.tensor_mul(
                                  tmp5[t5 % 2][:], ps_[pi][:],
                                  pl[le][:, t5 * 512: t5 * 512 + 512]),
                              reads=[f"ps{pi}", f"pl{le}"],
                              writes=[f"tmp5{t5 % 2}"])
                        if first:
                            P.add("gpsimd",
                                  lambda e, do=do, t5=t5: e.tensor_copy(
                                      oac[:, do * TBK + t5 * 512:
                                          do * TBK + t5 * 512 + 512],
                                      tmp5[t5 % 2][:]),
                                  reads=[f"tmp5{t5 % 2}"],
                                  writes=[f"oac@{do}_{t5}"])
                        else:
                            P.add("gpsimd",
                                  lambda e, do=do, t5=t5: e.tensor_add(
                                      oac[:, do * TBK + t5 * 512:
                                          do * TBK + t5 * 512 + 512],
                                      oac[:, do * TBK + t5 * 512:
                                          do * TBK + t5 * 512 + 512],
                                      tmp5[t5 % 2][:]),
                                  reads=[f"tmp5{t5 % 2}", f"oac@{do}_{t5}"],
                                  writes=[f"oac@{do}_{t5}"])
                hbase += sn_
                first = False
        for do in range(8):
            for t5 in range(4):
                P.add("vector", lambda e, do=do, t5=t5: e.tensor_copy(
                    pevac[t5 % 2][:], oac[:, do * TBK + t5 * 512:
                                          do * TBK + t5 * 512 + 512]),
                    reads=[f"oac@{do}_{t5}"], writes=[f"pevac{t5 % 2}"])
                P.dma("sync", part_ap(do, t0 + t5 * 512),
                      pevac[t5 % 2][:], reads=[f"pevac{t5 % 2}"],
                      writes=["mp_o"])





def build_attn():
    nc = bass.Bass("TRN2", target_bir_lowering=False, debug=False)
    P = Prog(nc)
    xna = nc.dram_tensor("xna", [D, T], BF16, kind="ExternalInput")
    ap_o = nc.dram_tensor("apart", [2, D, T // 2], BF16, kind="ExternalOutput")
    io = {
        "oscr": nc.dram_tensor("oscr", [4, 128, T], BF16, kind="Internal"),
        "wq_t": nc.dram_tensor("wq_t", [4, 128, 1024], F32, kind="ExternalInput"),
        "wk_t": nc.dram_tensor("wk_t", [4, 128, 1024], F32, kind="ExternalInput"),
        "wv_t": nc.dram_tensor("wv_t", [4, 128, 1024], F32, kind="ExternalInput"),
        "wo_t": nc.dram_tensor("wo_t", [8, 128, 512], F32, kind="ExternalInput"),
        "cos_t": nc.dram_tensor("cos_t", [32, T], F32, kind="ExternalInput"),
        "sin_t": nc.dram_tensor("sin_t", [32, T], F32, kind="ExternalInput"),
        "mask_t": nc.dram_tensor("mask_t", [128, 4 * 512], BF16, kind="ExternalInput"),
        "csc_t": nc.dram_tensor("csc_t", [128, 1], F32, kind="ExternalInput"),
        "xn_src": lambda dc, t0, w: xna[dc * 128:(dc + 1) * 128, t0:t0 + w],
        "part_ap": lambda do, t0: ap_o[t0 // 4096, do * 128:(do + 1) * 128,
                                       t0 % 4096: t0 % 4096 + 512],
    }
    emit_attn(P, nc, nc.alloc_sbuf_tensor, nc.alloc_psum_tensor, io)
    P.emit()
    return nc


def emit_attn(P, nc, sbp, psp, io):
    ldq = io.get("ldq", "gpsimd")
    oscr = io["oscr"]
    sb, ps = sbp, psp
    wq, wk, wv, wo = io["wq_t"], io["wk_t"], io["wv_t"], io["wo_t"]
    cosd, sind = io["cos_t"], io["sin_t"]
    maskd, cscd = io["mask_t"], io["csc_t"]
    xn_src = io["xn_src"]
    part_ap = io["part_ap"]

    QT = sb("QT", [128, T], BF16)
    KT = sb("KT", [128, T], BF16)
    VSB = sb("VSB", [128, T], BF16)
    oT4 = [sb(f"oT4_{i}", [128, 512], BF16) for i in range(2)]
    Sstr = [sb(f"Sstr{i}", [128, T], F32) for i in range(2)]
    Pstr = [sb("Pstr0", [128, T], BF16)] * 2
    xnb = sb("xnb", [128, 8 * 512], BF16)
    wqs = sb("wqs", [128, 4 * 1024], BF16)
    wks = sb("wks", [128, 4 * 1024], BF16)
    wvs = sb("wvs", [128, 4 * 1024], BF16)
    wos = sb("wos", [128, 8 * 512], BF16)
    coss = sb("coss", [32, 512], F32)
    sins = sb("sins", [32, 512], F32)
    masks = sb("masks", [128, 4 * 512], BF16)
    cscs = sb("cscs", [128, 1], F32)
    identB = sb("identB", [128, 128], BF16)
    identF = sb("identF", [128, 128], F32)
    rtmp = [sb(f"rtmp{i}", [32, 512], F32) for i in range(4)]
    PTs = [sb(f"PTs{i}", [128, 1024], BF16) for i in range(2)]
    lac = [sb(f"lac{i}", [128, 4], F32) for i in range(2)]
    rowred = [sb(f"rowred{i}", [128, 1], F32) for i in range(2)]
    lrs = sb("lrs", [1, 128], F32)
    nmrow = [sb(f"nmrow{i}", [128, 64], F32) for i in range(2)]
    lpls = sb("lpls", [128, 128], F32)
    ones1 = sb("ones1", [1, 128], F32)
    oev = sb("oev", [128, 512], BF16)
    psS = ps("psS", [128, 512], F32)
    psS2 = ps("psS2", [128, 512], F32)
    psPV_ = [ps(f"psPV{i}", [128, 512], F32) for i in range(2)]
    psAux = ps("psAux", [128, 512], F32)
    psQ = ps("psQ", [128, 512], F32)
    psTb = [ps(f"psTb{i}", [128, 1024], BF16) for i in range(2)]

    for tens, name, src_, n_ in ((wqs, "wqs", wq, 4), (wks, "wks", wk, 4),
                                 (wvs, "wvs", wv, 4)):
        for h in range(n_):
            P.dma(ldq, tens[:, h * 1024:(h + 1) * 1024], src_[h, :, :],
                  writes=[name])
    for do in range(8):
        P.dma(ldq, wos[:, do * 512:(do + 1) * 512], wo[do, :, :],
              writes=["wos"])
    P.dma("sync", masks[:], maskd[:, :], writes=["masks"])
    P.dma("sync", cscs[:], cscd[:, :], writes=["cscs"])
    P.add("gpsimd", lambda e: e.memset(identB[:], 0.0), writes=["identB"])
    P.add("gpsimd", lambda e: e.affine_select(
        out=identB[:], in_=identB[:], compare_op=ALU.not_equal, fill=1.0,
        base=0, pattern=[[-1, 128]], channel_multiplier=1),
        reads=["identB"], writes=["identB"])
    P.add("gpsimd", lambda e: e.memset(identF[:], 0.0), writes=["identF"])
    P.add("gpsimd", lambda e: e.affine_select(
        out=identF[:], in_=identF[:], compare_op=ALU.not_equal, fill=1.0,
        base=0, pattern=[[-1, 128]], channel_multiplier=1),
        reads=["identF"], writes=["identF"])
    P.add("gpsimd", lambda e: e.memset(ones1[:], 1.0), writes=["ones1"])

    def rope_evac(head, t5, pp, ppname, dst, dname):
        c0 = t5 * 512
        P.add("vector", lambda e: e.tensor_mul(
            rtmp[0][:], pp[0:32, :], coss[:, :]),
            reads=[ppname, "coss"], writes=["rtmp0"])
        P.add("vector", lambda e: e.tensor_mul(
            rtmp[1][:], pp[32:64, :], sins[:, :]),
            reads=[ppname, "sins"], writes=["rtmp1"])
        P.add("vector", lambda e: e.tensor_mul(
            rtmp[2][:], pp[0:32, :], sins[:, :]),
            reads=[ppname, "sins"], writes=["rtmp2"])
        P.add("vector", lambda e: e.tensor_mul(
            rtmp[3][:], pp[32:64, :], coss[:, :]),
            reads=[ppname, "coss"], writes=["rtmp3"])
        P.add("vector", lambda e: e.tensor_sub(
            dst[0:32, c0:c0 + 512], rtmp[0][:], rtmp[1][:]),
            reads=["rtmp0", "rtmp1"], writes=[dname])
        P.add("vector", lambda e: e.tensor_add(
            dst[32:64, c0:c0 + 512], rtmp[2][:], rtmp[3][:]),
            reads=["rtmp2", "rtmp3"], writes=[dname])
        P.add("vector", lambda e: e.tensor_copy(
            dst[64:128, c0:c0 + 512], pp[64:128, :]),
            reads=[ppname], writes=[dname])

    NKT = T // 512
    for h in range(4):
        for t5 in range(16):
            t0 = t5 * 512
            P.dma("sync", coss[:], cosd[:, t0:t0 + 512], writes=["coss"])
            P.dma("sync", sins[:], sind[:, t0:t0 + 512], writes=["sins"])
            for dc in range(8):
                P.dma("sync", xnb[:, dc * 512:(dc + 1) * 512],
                      xn_src(dc, t0, 512),
                      writes=["xnb"])
            for wset, wname, pp, ppname in ((wks, "wks", psQ, "psQ"),):
                for dc in range(8):
                    P.add("tensor", lambda e, h=h, dc=dc, wset=wset, pp=pp:
                          e.matmul(pp[:],
                                   wset[:, h * 1024 + dc * 128:
                                        h * 1024 + (dc + 1) * 128],
                                   xnb[:, dc * 512:(dc + 1) * 512],
                                   start=(dc == 0), stop=(dc == 7)),
                          reads=[wname, "xnb"], writes=[ppname])
                rope_evac(h, t5, pp, ppname, KT, "KT")
            for dc in range(8):
                P.add("tensor", lambda e, h=h, dc=dc: e.matmul(
                    psQ[:], wqs[:, h * 1024 + dc * 128:
                                h * 1024 + (dc + 1) * 128],
                    xnb[:, dc * 512:(dc + 1) * 512],
                    start=(dc == 0), stop=(dc == 7)),
                    reads=["wqs", "xnb"], writes=["psQ"])
            rope_evac(h, t5, psQ, "psQ", QT, "QT")
            for tt in range(4):
                for dc in range(8):
                    P.add("tensor", lambda e, h=h, dc=dc, tt=tt: e.matmul(
                        psAux[:, 0:128],
                        xnb[:, dc * 512 + tt * 128: dc * 512 + tt * 128 + 128],
                        wvs[:, h * 1024 + dc * 128: h * 1024 + (dc + 1) * 128],
                        start=(dc == 0), stop=(dc == 7)),
                        reads=["wvs", "xnb"], writes=["psAux"])
                P.add("vector", lambda e, t5=t5, tt=tt: e.tensor_copy(
                    VSB[:, (t5 * 4 + tt) * 128: (t5 * 4 + tt) * 128 + 128],
                    psAux[:, 0:128]),
                    reads=["psAux"], writes=["VSB"])

        for qt in range(T // 128):
            ktmax = qt // 4
            nkt = ktmax + 1
            qq = qt % 2
            lacq = lac[qq]
            psPV, psPVn = psPV_[qq], f"psPV{qq}"
            Ss, Ssn = Sstr[qq], f"Sstr{qq}"
            Ps, Psn = Pstr[0], "Pstr0"
            SB4 = [(psS, "psS"), (psQ, "psQ"), (psS2, "psS2")]
            for kt in range(nkt):
                pS, pn = SB4[kt % 3]
                P.add("tensor", lambda e, qt=qt, kt=kt, pS=pS: e.matmul(
                    pS[:], QT[:, qt * 128:(qt + 1) * 128],
                    KT[:, kt * 512:(kt + 1) * 512], start=True, stop=True),
                    reads=["QT", "KT"], writes=[pn])
                if kt == ktmax:
                    v = qt % 4
                    P.add("vector", lambda e, v=v, pS=pS, kt=kt, Ss=Ss:
                          e.tensor_add(Ss[:, kt * 512:(kt + 1) * 512],
                                       pS[:], masks[:, v * 512:(v + 1) * 512]),
                          reads=[pn, "masks"], writes=[Ssn])
                elif kt % 2 == 0:
                    P.add("vector", lambda e, pS=pS, kt=kt, Ss=Ss:
                          e.tensor_copy(Ss[:, kt * 512:(kt + 1) * 512], pS[:]),
                          reads=[pn], writes=[Ssn])
                else:
                    P.add("scalar", lambda e, pS=pS, kt=kt, Ss=Ss:
                          e.activation(Ss[:, kt * 512:(kt + 1) * 512], pS[:],
                                       AF.Copy),
                          reads=[pn], writes=[Ssn])
                P.add("vector", lambda e, kt=kt, Ss=Ss, qq=qq: e.reduce_max(
                    nmrow[qq][:, kt:kt + 1], Ss[:, kt * 512:(kt + 1) * 512],
                    axis=AX.X, negate=True),
                    reads=[Ssn], writes=[f"nmrow{qq}@{kt}"])
            P.add("vector", lambda e, nkt=nkt, lacq=lacq, qq=qq: e.tensor_reduce(
                lacq[:, 1:2], nmrow[qq][:, 0:nkt], axis=AX.X, op=ALU.min),
                reads=[f"nmrow{qq}"], writes=[f"lac{qq}@nm"])
            P.add("vector", lambda e, lacq=lacq: e.tensor_scalar(
                lacq[:, 2:3], lacq[:, 1:2], cscs[:, 0:1], None, ALU.mult),
                reads=[f"lac{qq}@nm", "cscs"], writes=[f"lac{qq}@bias"])
            P.add("scalar", lambda e, Ss=Ss, Ps=Ps, nkt=nkt, lacq=lacq:
                  e.activation(Ps[:, 0:nkt * 512], Ss[:, 0:nkt * 512], AF.Exp,
                               scale=cscs[:, 0:1], bias=lacq[:, 2:3],
                               accum_out=lacq[:, 0:1]),
                  reads=[Ssn, "cscs", f"lac{qq}@bias"],
                  writes=[Psn, f"lac{qq}@l"])
            for k0 in range(0, nkt, 2):
                pq2 = (k0 // 2) % 2
                kts = [kt for kt in (k0, k0 + 1) if kt < nkt]
                for kt in kts:
                    half = (kt % 2) * 512
                    for j in range(4):
                        P.add("tensor", lambda e, j=j, kt=kt, pq2=pq2,
                              half=half, Ps=Ps: e.transpose(
                                  psTb[pq2][:, half + j * 128:
                                             half + (j + 1) * 128],
                                  Ps[:, kt * 512 + j * 128:
                                     kt * 512 + (j + 1) * 128],
                                  identB[:]),
                              reads=[Psn, "identB"], writes=[f"psTb{pq2}"])
                w = len(kts) * 512
                P.add("vector", lambda e, pq2=pq2, w=w: e.tensor_copy(
                    PTs[pq2][:, 0:w], psTb[pq2][:, 0:w]),
                    reads=[f"psTb{pq2}"], writes=[f"PTs{pq2}"])
                for kt in kts:
                    half = (kt % 2) * 512
                    for j in range(4):
                        P.add("tensor", lambda e, qt=qt, kt=kt, j=j, pq2=pq2,
                              half=half, psPV=psPV:
                              e.matmul(
                            psPV[:, 0:128],
                            VSB[:, (kt * 4 + j) * 128:
                                (kt * 4 + j) * 128 + 128],
                            PTs[pq2][:, half + j * 128: half + (j + 1) * 128],
                            start=(kt == 0 and j == 0),
                            stop=(kt == qt // 4 and j == 3)),
                            reads=["VSB", f"PTs{pq2}"], writes=[psPVn])
            P.add("vector", lambda e, lacq=lacq: e.reciprocal(
                lacq[:, 3:4], lacq[:, 0:1]),
                reads=[f"lac{qq}@l"], writes=[f"lac{qq}@linv"])
            P.add("tensor", lambda e, lacq=lacq: e.transpose(
                psAux[0:1, 0:128], lacq[:, 3:4], identF[:]),
                reads=[f"lac{qq}@linv", "identF"], writes=["psAux"])
            P.add("vector", lambda e: e.tensor_copy(lrs[:], psAux[0:1, 0:128]),
                  reads=["psAux"], writes=["lrs"])
            P.add("tensor", lambda e: e.matmul(
                psAux[:, 0:128], ones1[:], lrs[:], start=True, stop=True),
                reads=["ones1", "lrs"], writes=["psAux"])
            P.add("vector", lambda e: e.tensor_copy(lpls[:], psAux[:, 0:128]),
                  reads=["psAux"], writes=["lpls"])
            og = (qt // 4) % 2
            P.add("vector", lambda e, qt=qt, og=og, psPV=psPV: e.tensor_mul(
                oT4[og][:, (qt % 4) * 128: (qt % 4) * 128 + 128],
                psPV[:, 0:128], lpls[:]),
                reads=[psPVn, "lpls"], writes=[f"oT4_{og}@{qt % 4}"])
            if qt % 4 == 3:
                P.dma("sync", oscr[h, :, (qt - 3) * 128:(qt + 1) * 128],
                      oT4[og][:],
                      reads=[f"oT4_{og}"], writes=["oscr"])

    oTk = [sb(f"oTk{i}", [128, 4 * 512], BF16) for i in range(2)]
    for t5 in range(16):
        t0 = t5 * 512
        tq5 = t5 % 2
        for kc in range(4):
            P.dma("sync", oTk[tq5][:, kc * 512:(kc + 1) * 512],
                  oscr[kc, :, t0:t0 + 512], writes=[f"oTk{tq5}"])
        for do in range(8):
            pp = psS if do % 2 == 0 else psQ
            ppn = "psS" if do % 2 == 0 else "psQ"
            for kc in range(4):
                P.add("tensor", lambda e, do=do, kc=kc, t0=t0, pp=pp, tq5=tq5:
                      e.matmul(pp[:],
                               wos[:, do * 512 + kc * 128:
                                   do * 512 + (kc + 1) * 128],
                               oTk[tq5][:, kc * 512:(kc + 1) * 512],
                               start=(kc == 0), stop=(kc == 3)),
                      reads=["wos", f"oTk{tq5}"], writes=[ppn])
            P.add("vector", lambda e, pp=pp, ppn=ppn: e.tensor_copy(
                oev[:], pp[:]), reads=[ppn], writes=["oev"])
            P.dma("sync", part_ap(do, t0),
                  oev[:], reads=["oev"], writes=["ap_o"])


def stage_attn_inputs(inputs, cid):
    b, p = cid >> 1, cid & 1
    hs = slice(p * 4, (p + 1) * 4)
    wqt = np.stack([tile_lhsT(np.asarray(inputs["L1_attn_wq"])[h]).reshape(128, 1024)
                    for h in range(p * 4, p * 4 + 4)])
    wkt = np.stack([tile_lhsT(np.asarray(inputs["L1_attn_wk"])[h]).reshape(128, 1024)
                    for h in range(p * 4, p * 4 + 4)])
    wvt = np.stack([tile_lhsT(np.asarray(inputs["L1_attn_wv"])[h]).reshape(128, 1024)
                    for h in range(p * 4, p * 4 + 4)])
    wot = tile_lhsT(np.asarray(inputs["L1_attn_wo"])[p * 512:(p + 1) * 512])
    rad = np.arange(T)[None, :] * (10000.0 ** (-np.arange(32) / 32.0))[:, None]
    mask = np.zeros((128, 4, 512), np.float32)
    for v in range(4):
        i = np.arange(128)[:, None]
        j = np.arange(512)[None, :]
        mask[:, v, :] = np.where(v * 128 + i >= j, 0.0, -1e30)
    return {
        "xna": None,
        "wq_t": wqt, "wk_t": wkt, "wv_t": wvt, "wo_t": wot,
        "cos_t": np.cos(rad).astype(np.float32),
        "sin_t": np.sin(rad).astype(np.float32),
        "mask_t": mask.reshape(128, 2048).astype(BF),
        "csc_t": np.full((128, 1), 1.0 / 128.0, np.float32),
    }




def build_moe():
    nc = bass.Bass("TRN2", target_bir_lowering=False, debug=False)
    P = Prog(nc)
    xnm = nc.dram_tensor("xnm", [D, T], BF16, kind="ExternalInput")
    mp_o = nc.dram_tensor("mpart", [2, D, T // 2], BF16, kind="ExternalOutput")
    io = {
        "gw_t": nc.dram_tensor("gw_t", [128, 32], F32, kind="ExternalInput"),
        "sel_t": nc.dram_tensor("sel_t", [4, 256], F32, kind="ExternalInput"),
        "w1_t": nc.dram_tensor("w1_t", [2, 44, 128, 1024], F32, kind="ExternalInput"),
        "w3_t": nc.dram_tensor("w3_t", [2, 44, 128, 1024], F32, kind="ExternalInput"),
        "w2_t": nc.dram_tensor("w2_t", [2, 8, 128, 5632], F32, kind="ExternalInput"),
        "xn_src": lambda dc, t0, w: xnm[dc * 128:(dc + 1) * 128, t0:t0 + w],
        "part_ap": lambda do, t0: mp_o[t0 // 4096, do * 128:(do + 1) * 128,
                                       t0 % 4096: t0 % 4096 + 512],
    }
    emit_moe(P, nc, nc.alloc_sbuf_tensor, nc.alloc_psum_tensor, io)
    P.emit()
    return nc





def build_ffn1():
    nc = bass.Bass("TRN2", target_bir_lowering=False, debug=False)
    P = Prog(nc)
    xnf = nc.dram_tensor("xnf", [D, T], BF16, kind="ExternalInput")
    fp_o = nc.dram_tensor("fpart", [2, D, T // 2], BF16, kind="ExternalOutput")
    io = {
        "wg_t": nc.dram_tensor("wg_t", [12, 128, 1024], I32, kind="ExternalInput"),
        "wu_t": nc.dram_tensor("wu_t", [12, 128, 1024], I32, kind="ExternalInput"),
        "wdf_t": nc.dram_tensor("wd_t", [8, 128, 1536], I32, kind="ExternalInput"),
        "c127_t": nc.dram_tensor("c127_t", [128, 1], F32, kind="ExternalInput"),
        "xn_src": lambda dc, t0, w: xnf[dc * 128:(dc + 1) * 128, t0:t0 + w],
        "part_ap": lambda do, t0: fp_o[t0 // 4096, do * 128:(do + 1) * 128,
                                       t0 % 4096: t0 % 4096 + 512],
    }
    emit_ffn1(P, nc, nc.alloc_sbuf_tensor, nc.alloc_psum_tensor, io)
    P.emit()
    return nc


def emit_ffn1(P, nc, sbp, psp, io):
    ldq = io.get("ldq", "gpsimd")
    sb, ps = sbp, psp
    wg, wu, wdn, c127 = io["wg_t"], io["wu_t"], io["wdf_t"], io["c127_t"]
    xn_src = io["xn_src"]
    part_ap = io["part_ap"]

    wgs = sb("wgs", [128, 12 * 1024], BF16)
    wus = sb("wus", [128, 12 * 1024], BF16)
    wds = sb("wds", [128, 8 * 1536], BF16)
    c127s = sb("c127s", [128, 1], F32)
    xnb = sb("xnb", [128, 8 * 512], BF16)
    sil = sb("sil", [128, 512], F32)
    hstr = sb("hstr", [128, 12 * 512], BF16)
    pevac = sb("pevac", [128, 512], BF16)
    ps_ = [ps(f"ps{i}", [128, 512], F32) for i in range(6)]

    for i in range(12):
        P.dma(ldq, wgs[:, i * 1024:(i + 1) * 1024], wg[i, :, :],
              writes=["wgs"])
        P.dma(ldq, wus[:, i * 1024:(i + 1) * 1024], wu[i, :, :],
              writes=["wus"])
    for i in range(8):
        P.dma(ldq, wds[:, i * 1536:(i + 1) * 1536], wdn[i, :, :],
              writes=["wds"])
    P.dma("sync", c127s[:], c127[:, :], writes=["c127s"])

    for t5 in range(16):
        t0 = t5 * 512
        for dc in range(8):
            P.dma("sync", xnb[:, dc * 512:(dc + 1) * 512],
                  xn_src(dc, t0, 512), writes=["xnb"])
        for ht in range(12):
            pg, pu = ps_[ht % 2], ps_[2 + ht % 2]
            for dc in range(8):
                P.add("tensor", lambda e, ht=ht, dc=dc, pg=pg: e.matmul(
                    pg[:], wgs[:, ht * 1024 + dc * 128:
                               ht * 1024 + (dc + 1) * 128],
                    xnb[:, dc * 512:(dc + 1) * 512],
                    start=(dc == 0), stop=(dc == 7)),
                    reads=["wgs", "xnb"], writes=[f"ps{ht % 2}"])
                P.add("tensor", lambda e, ht=ht, dc=dc, pu=pu: e.matmul(
                    pu[:], wus[:, ht * 1024 + dc * 128:
                               ht * 1024 + (dc + 1) * 128],
                    xnb[:, dc * 512:(dc + 1) * 512],
                    start=(dc == 0), stop=(dc == 7)),
                    reads=["wus", "xnb"], writes=[f"ps{2 + ht % 2}"])
            P.add("scalar", lambda e, ht=ht, pg=pg: e.activation(
                sil[:], pg[:], AF.Silu, scale=c127s[:, 0:1]),
                reads=[f"ps{ht % 2}", "c127s"], writes=["sil"])
            P.add("vector", lambda e, ht=ht, pu=pu: e.scalar_tensor_tensor(
                hstr[:, ht * 512:(ht + 1) * 512], pu[:], c127s[:, 0:1],
                sil[:], ALU.mult, ALU.mult),
                reads=[f"ps{2 + ht % 2}", "c127s", "sil"], writes=["hstr"])
        for do in range(8):
            pp = ps_[4 + do % 2]
            for hc in range(12):
                P.add("tensor", lambda e, do=do, hc=hc, pp=pp: e.matmul(
                    pp[:], wds[:, do * 1536 + hc * 128:
                               do * 1536 + (hc + 1) * 128],
                    hstr[:, hc * 512:(hc + 1) * 512],
                    start=(hc == 0), stop=(hc == 11)),
                    reads=["wds", "hstr"], writes=[f"ps{4 + do % 2}"])
            P.add("vector", lambda e, pp=pp: e.tensor_scalar(
                pevac[:], pp[:], c127s[:, 0:1], None, ALU.mult),
                reads=[f"ps{4 + do % 2}", "c127s"], writes=["pevac"])
            P.dma("sync", part_ap(do, t0),
                  pevac[:], reads=["pevac"], writes=["fp_o"])


def stage_ffn1_inputs(inputs, cid):
    b, p = cid >> 1, cid & 1
    own = slice(p * 1536, (p + 1) * 1536)
    wgT = np.ascontiguousarray(np.asarray(inputs["L1_ffn_w_gate"]).T[:, own])
    wuT = np.ascontiguousarray(np.asarray(inputs["L1_ffn_w_up"]).T[:, own])
    wdT = np.ascontiguousarray(np.asarray(inputs["L1_ffn_w_down"]).T[own, :])
    return {
        "wg_t": tile_lhsT(wgT), "wu_t": tile_lhsT(wuT),
        "wd_t": tile_lhsT(wdT),
        "c127_t": np.full((128, 1), 1.0 / 127.0, np.float32),
    }


def stage_moe_inputs(inputs, cid):
    b, p = cid >> 1, cid & 1
    gw4 = np.asarray(inputs["L0_ffn_gate_w"])[0:4]
    gwt = np.ascontiguousarray(
        gw4.T.reshape(8, 128, 4).transpose(1, 0, 2).reshape(128, 32))
    sel = np.zeros((4, 256), np.float32)
    sel[p * 2, 0:128] = 1.0
    sel[p * 2 + 1, 128:256] = 1.0
    w1 = np.asarray(inputs["L0_ffn_w1"])
    w3 = np.asarray(inputs["L0_ffn_w3"])
    w2 = np.asarray(inputs["L0_ffn_w2"])
    w1t = np.stack([tile_lhsT(w1[p * 2 + i]) for i in range(2)])
    w3t = np.stack([tile_lhsT(w3[p * 2 + i]) for i in range(2)])
    w2t = np.stack([tile_lhsT(w2[p * 2 + i]) for i in range(2)])
    return {"gw_t": gwt, "sel_t": sel, "w1_t": w1t, "w3_t": w3t, "w2_t": w2t}


def tile_lhsT(w, blk_out=128):
    Din, Dout = w.shape
    dcn, ocn = Din // 128, Dout // blk_out
    return np.ascontiguousarray(
        w.reshape(dcn, 128, ocn, blk_out).transpose(2, 1, 0, 3)
        .reshape(ocn, 128, dcn * blk_out))


def vec_t(v):
    return np.ascontiguousarray(v.reshape(-1, 128).T)


def stage_s1_inputs(inputs, cid):
    b, p = cid >> 1, cid & 1
    x = inputs["L0_attn_x"][b]
    W_in = np.asarray(inputs["L0_attn_W_in"])
    cols = np.r_[p * 1024:(p + 1) * 1024, 2048 + p * 1024: 2048 + (p + 1) * 1024]
    w_half = W_in[:, cols]
    cwo = np.asarray(inputs["L0_attn_conv_w"])[p * 1024:(p + 1) * 1024, 0, :]
    return {
        "x": np.ascontiguousarray(x),
        "w_in_t": tile_lhsT(w_half),
        "lg_t": vec_t(np.asarray(inputs["L0_attn_ln_g"])),
        "lb_t": vec_t(np.asarray(inputs["L0_attn_ln_b"])),
        "cw_t": np.ascontiguousarray(
            cwo.reshape(8, 128, 4).transpose(1, 0, 2).reshape(128, 32)),
        "cb_t": vec_t(np.asarray(inputs["L0_attn_conv_b"])[p * 1024:(p + 1) * 1024]),
    }


PAIRS = [[0, 1], [2, 3], [4, 5], [6, 7]]
_FUSED = True


def build_fused():
    from contextlib import ExitStack
    nc = bass.Bass("TRN2", target_bir_lowering=False, debug=False)
    P = Prog(nc)
    dt = nc.dram_tensor
    TO = T // 2

    ext = {
        "x": dt("x", [T, D], F32, kind="ExternalInput"),
        "rp0": dt("rp0", [TO, D], F32, kind="ExternalInput"),
        "w_in_t": dt("w_in_t", [16, 128, 1024], F32, kind="ExternalInput"),
        "lg_t": dt("lg_t", [128, 8], F32, kind="ExternalInput"),
        "lb_t": dt("lb_t", [128, 8], F32, kind="ExternalInput"),
        "cw_t": dt("cw_t", [128, 32], F32, kind="ExternalInput"),
        "cb_t": dt("cb_t", [128, 8], F32, kind="ExternalInput"),
        "wd_t": dt("wd_t", [8, 128, 2048], F32, kind="ExternalInput"),
        "bd_t": dt("bd_t", [128, 8], F32, kind="ExternalInput"),
        "wb_t": dt("wb_t", [128, 256], F32, kind="ExternalInput"),
        "wc_t": dt("wc_t", [128, 256], F32, kind="ExternalInput"),
        "la_t": dt("la_t", [128, 128], F32, kind="ExternalInput"),
        "dp_t": dt("dp_t", [128, 8], F32, kind="ExternalInput"),
        "wo_t": dt("wo_t", [8, 128, 1024], F32, kind="ExternalInput"),
        "gw_t": dt("gw_t", [128, 32], F32, kind="ExternalInput"),
        "sel_t": dt("sel_t", [4, 256], F32, kind="ExternalInput"),
        "w1_t": dt("w1_t", [2, 44, 128, 1024], F32, kind="ExternalInput"),
        "w3_t": dt("w3_t", [2, 44, 128, 1024], F32, kind="ExternalInput"),
        "w2_t": dt("w2_t", [2, 8, 128, 5632], F32, kind="ExternalInput"),
        "wq_t": dt("wq_t", [4, 128, 1024], F32, kind="ExternalInput"),
        "wk_t": dt("wk_t", [4, 128, 1024], F32, kind="ExternalInput"),
        "wv_t": dt("wv_t", [4, 128, 1024], F32, kind="ExternalInput"),
        "woA_t": dt("woA_t", [8, 128, 512], F32, kind="ExternalInput"),
        "cos_t": dt("cos_t", [32, T], F32, kind="ExternalInput"),
        "sin_t": dt("sin_t", [32, T], F32, kind="ExternalInput"),
        "mask_t": dt("mask_t", [128, 4 * 512], BF16, kind="ExternalInput"),
        "csc_t": dt("csc_t", [128, 1], F32, kind="ExternalInput"),
        "wg_t": dt("wg_t", [12, 128, 1024], I32, kind="ExternalInput"),
        "wu_t": dt("wu_t", [12, 128, 1024], I32, kind="ExternalInput"),
        "wdf_t": dt("wdf_t", [8, 128, 1536], I32, kind="ExternalInput"),
        "c127_t": dt("c127_t", [128, 1], F32, kind="ExternalInput"),
    }
    xb_own = dt("xb_own", [DI // 2, T], BF16, kind="Internal")
    sz_own = dt("sz_own", [DI // 2, T], BF16, kind="Internal")
    xn0T = dt("xn0T", [D, T], BF16, kind="Internal")
    ag_xb = dt("ag_xb", [DI, T], BF16, kind="Internal")
    rs_in = dt("rs_in", [2 * D, TO], BF16, kind="Internal")
    rs_out = dt("rs_out", [D, TO], BF16, kind="Internal")
    ag_xn_in = dt("ag_xn_in", [D, TO], BF16, kind="Internal")
    ag_xn_out = dt("ag_xn_out", [2 * D, TO], BF16, kind="Internal")
    r1 = dt("r1", [TO, D], F32, kind="Internal")
    r2 = dt("r2", [TO, D], F32, kind="Internal")
    r3 = dt("r3", [TO, D], F32, kind="Internal")
    bscr = dt("bscr", [2, 16, 512], BF16, kind="Internal")
    oscr = dt("oscr", [4, 128, T], BF16, kind="Internal")
    rn_o = dt("rn", [TO, D], F32, kind="ExternalOutput")

    def part_ap(do, t0):
        return rs_in[(t0 // 4096) * D + do * 128:
                     (t0 // 4096) * D + (do + 1) * 128,
                     t0 % 4096: t0 % 4096 + 512]

    def xn_src(dc, t0, w):
        r0 = (t0 // 4096) * D + dc * 128
        c0 = t0 % 4096
        return ag_xn_out[r0:r0 + 128, c0:c0 + w]

    def xn_ap(dc, t0):
        return ag_xn_in[dc * 128:(dc + 1) * 128, t0:t0 + 128]

    stage_n = [0]

    def stage(emit_fn, io, name=None):
        pre = f"s{stage_n[0]}_"
        stage_n[0] += 1
        es = ExitStack()
        sbp = lambda n, sh, dty: es.enter_context(
            nc.sbuf_tensor(pre + n, sh, dty))
        psp = lambda n, sh, dty: es.enter_context(
            nc.psum_tensor(pre + n, sh, dty))
        with nc.named_scope(name or pre):
            emit_fn(P, nc, sbp, psp, io)
        es.close()
        P.barrier()

    def cc(kind, op, ins, outs):
        P.add("gpsimd", lambda e: e.collective_compute(
            kind, op, PAIRS, ins, outs), kind="cc")
        P.barrier()

    twin = {
        "wd_t": dt("wd_b", [8, 128, 2048], BF16, kind="Internal"),
        "wo_t": dt("wo_b", [8, 128, 1024], BF16, kind="Internal"),
        "wb_t": dt("wb_b", [128, 256], BF16, kind="Internal"),
        "wc_t": dt("wc_b", [128, 256], BF16, kind="Internal"),
        "w1_t": dt("w1_b", [2, 44, 128, 1024], BF16, kind="Internal"),
        "w3_t": dt("w3_b", [2, 44, 128, 1024], BF16, kind="Internal"),
        "w2_t": dt("w2_b", [2, 8, 128, 5632], BF16, kind="Internal"),
        "wq_t": dt("wq_b", [4, 128, 1024], BF16, kind="Internal"),
        "wk_t": dt("wk_b", [4, 128, 1024], BF16, kind="Internal"),
        "wv_t": dt("wv_b", [4, 128, 1024], BF16, kind="Internal"),
        "woA_t": dt("woA_b", [8, 128, 512], BF16, kind="Internal"),
        "wg_t": dt("wg_b", [12, 128, 1024], BF16, kind="Internal"),
        "wu_t": dt("wu_b", [12, 128, 1024], BF16, kind="Internal"),
        "wdf_t": dt("wdf_b", [8, 128, 1536], BF16, kind="Internal"),
    }

    stage(emit_s1, {"x": ext["x"], "w_in_t": ext["w_in_t"],
                    "lg_t": ext["lg_t"], "lb_t": ext["lb_t"],
                    "cw_t": ext["cw_t"], "cb_t": ext["cb_t"],
                    "xb": xb_own, "sz": sz_own, "xn0T": xn0T}, "S1")
    def conv(dst, srct, slices):
        for sl in slices:
            P.dma("gpsimd", dst[sl], srct[sl], writes=["twin"])
    conv(twin["wd_t"], ext["wd_t"], [(i,) for i in range(8)])
    conv(twin["wo_t"], ext["wo_t"], [(i,) for i in range(8)])
    conv(twin["wb_t"], ext["wb_t"], [slice(None)])
    conv(twin["wc_t"], ext["wc_t"], [slice(None)])
    for le in range(2):
        for a in range(4):
            conv(twin["w1_t"], ext["w1_t"], [(le, slice(a * 11, a * 11 + 11))])
            conv(twin["w3_t"], ext["w3_t"], [(le, slice(a * 11, a * 11 + 11))])
        for do_ in range(8):
            conv(twin["w2_t"], ext["w2_t"], [(le, do_)])
    for k in ("wq_t", "wk_t", "wv_t", "woA_t", "wg_t", "wu_t", "wdf_t"):
        conv(twin[k], ext[k], [(i,) for i in range(ext[k].shape[0])])
    cc("AllGather", ALU.bypass, [xb_own[:, :]], [ag_xb[:, :]])
    stage(emit_scan, {"xbF": ag_xb, "xbO": xb_own, "szO": sz_own,
                      "xn0T": xn0T, "wd_t": twin["wd_t"], "bd_t": ext["bd_t"],
                      "wb_t": twin["wb_t"], "wc_t": twin["wc_t"],
                      "la_t": ext["la_t"], "dp_t": ext["dp_t"],
                      "wo_t": twin["wo_t"], "bscr": bscr,
                      "part_ap": part_ap, "ldq": "sync"}, "SCAN")
    cc("ReduceScatter", ALU.add, [rs_in[:, :]], [rs_out[:, :]])
    stage(lambda P_, nc_, sbp, psp, io: emit_combine(
        P_, nc_, sbp, psp, io, True, True),
        {"psum": rs_out, "rp": ext["rp0"], "rn": r1, "xn_ap": xn_ap}, "C1")
    cc("AllGather", ALU.bypass, [ag_xn_in[:, :]], [ag_xn_out[:, :]])
    stage(emit_moe, {"gw_t": ext["gw_t"], "sel_t": ext["sel_t"],
                     "w1_t": twin["w1_t"], "w3_t": twin["w3_t"],
                     "w2_t": twin["w2_t"], "xn_src": xn_src,
                     "part_ap": part_ap, "ldq": "sync"}, "MOE")
    cc("ReduceScatter", ALU.add, [rs_in[:, :]], [rs_out[:, :]])
    stage(lambda P_, nc_, sbp, psp, io: emit_combine(
        P_, nc_, sbp, psp, io, True, True),
        {"psum": rs_out, "rp": r1, "rn": r2, "xn_ap": xn_ap}, "C2")
    cc("AllGather", ALU.bypass, [ag_xn_in[:, :]], [ag_xn_out[:, :]])
    stage(emit_attn, {"wq_t": twin["wq_t"], "wk_t": twin["wk_t"],
                      "wv_t": twin["wv_t"], "wo_t": twin["woA_t"],
                      "cos_t": ext["cos_t"], "sin_t": ext["sin_t"],
                      "mask_t": ext["mask_t"], "csc_t": ext["csc_t"],
                      "xn_src": xn_src, "part_ap": part_ap,
                      "oscr": oscr, "ldq": "sync"}, "ATTN")
    cc("ReduceScatter", ALU.add, [rs_in[:, :]], [rs_out[:, :]])
    stage(lambda P_, nc_, sbp, psp, io: emit_combine(
        P_, nc_, sbp, psp, io, True, True),
        {"psum": rs_out, "rp": r2, "rn": r3, "xn_ap": xn_ap}, "C3")
    cc("AllGather", ALU.bypass, [ag_xn_in[:, :]], [ag_xn_out[:, :]])
    stage(emit_ffn1, {"wg_t": twin["wg_t"], "wu_t": twin["wu_t"],
                      "wdf_t": twin["wdf_t"], "c127_t": ext["c127_t"],
                      "xn_src": xn_src, "part_ap": part_ap,
                      "ldq": "sync"}, "FFN")
    cc("ReduceScatter", ALU.add, [rs_in[:, :]], [rs_out[:, :]])
    stage(lambda P_, nc_, sbp, psp, io: emit_combine(
        P_, nc_, sbp, psp, io, False, True),
        {"psum": rs_out, "rp": r3, "rn": rn_o}, "C4")
    P.emit()
    return nc


def stage_fused_inputs(inputs, cid):
    b, p = cid >> 1, cid & 1
    own = slice(p * 4096, (p + 1) * 4096)
    m = {}
    m.update(stage_s1_inputs(inputs, cid))
    m.update(stage_scan_inputs(inputs, cid, None, None, None, None))
    for k in ("xbF", "xbO", "sz", "xn0"):
        m.pop(k, None)
    m.update(stage_moe_inputs(inputs, cid))
    att = stage_attn_inputs(inputs, cid)
    att.pop("xna", None)
    att["woA_t"] = att.pop("wo_t")
    m.update(att)
    ffn = stage_ffn1_inputs(inputs, cid)
    ffn["wdf_t"] = ffn.pop("wd_t")
    m.update(ffn)
    m["rp0"] = np.ascontiguousarray(np.asarray(inputs["L0_attn_x"])[b, own])
    return m


def kernel(**inputs):
    cores = list(range(8))
    if _FUSED:
        nc = build_fused()
        in_maps = [stage_fused_inputs(inputs, c) for c in cores]
        res = run_bass_kernel_spmd(nc, in_maps, core_ids=cores)
        out = np.zeros((4, 8192, 1024), np.float32)
        for c in cores:
            b, p = c >> 1, c & 1
            out[b, p * 4096:(p + 1) * 4096] = res.results[c]["rn"]
        return out
    return _kernel_unfused(inputs, cores)


def _kernel_unfused(inputs, cores):
    nc = build_s1()
    res = run_bass_kernel_spmd(nc, [stage_s1_inputs(inputs, c) for c in cores],
                               core_ids=cores)
    l1 = res.results

    nc = build_scan()
    in_maps = []
    for c in cores:
        ev, od = (c >> 1) * 2, (c >> 1) * 2 + 1
        xbf = np.ascontiguousarray(np.concatenate(
            [l1[ev]["xb"], l1[od]["xb"]], axis=0))
        in_maps.append(stage_scan_inputs(inputs, c, xbf, l1[c]["xb"],
                                         l1[c]["sz"], l1[c]["xn0"]))
    res = run_bass_kernel_spmd(nc, in_maps, core_ids=cores)
    l2 = res.results

    def combine(nc_c, parts, rps, want_xn):
        in_maps = []
        for c in cores:
            b, p = c >> 1, c & 1
            m = {"pa": np.ascontiguousarray(parts[2 * b][p]),
                 "pb": np.ascontiguousarray(parts[2 * b + 1][p]),
                 "rp": rps[c]}
            in_maps.append(m)
        return run_bass_kernel_spmd(nc_c, in_maps, core_ids=cores).results

    nc_cx = build_combine(True, False)
    nc_cl = build_combine(False, False)

    xs = np.asarray(inputs["L0_attn_x"])
    rp0 = [np.ascontiguousarray(xs[c >> 1, (c & 1) * 4096:(c & 1) * 4096 + 4096])
           for c in cores]
    l3 = combine(nc_cx, [l2[c]["part"] for c in cores], rp0, True)

    nc = build_moe()
    in_maps = []
    for c in cores:
        b = c >> 1
        m = stage_moe_inputs(inputs, c)
        m["xnm"] = np.ascontiguousarray(np.concatenate(
            [l3[2 * b]["xnx"], l3[2 * b + 1]["xnx"]], axis=1))
        in_maps.append(m)
    res = run_bass_kernel_spmd(nc, in_maps, core_ids=cores)
    l4 = res.results

    l5 = combine(nc_cx, [l4[c]["mpart"] for c in cores],
                 [l3[c]["rn"] for c in cores], True)

    nc = build_attn()
    in_maps = []
    for c in cores:
        b = c >> 1
        m = stage_attn_inputs(inputs, c)
        m["xna"] = np.ascontiguousarray(np.concatenate(
            [l5[2 * b]["xnx"], l5[2 * b + 1]["xnx"]], axis=1))
        in_maps.append(m)
    res = run_bass_kernel_spmd(nc, in_maps, core_ids=cores)
    l6 = res.results

    l7 = combine(nc_cx, [l6[c]["apart"] for c in cores],
                 [l5[c]["rn"] for c in cores], True)

    nc = build_ffn1()
    in_maps = []
    for c in cores:
        b = c >> 1
        m = stage_ffn1_inputs(inputs, c)
        m["xnf"] = np.ascontiguousarray(np.concatenate(
            [l7[2 * b]["xnx"], l7[2 * b + 1]["xnx"]], axis=1))
        in_maps.append(m)
    res = run_bass_kernel_spmd(nc, in_maps, core_ids=cores)
    l8 = res.results

    l9 = combine(nc_cl, [l8[c]["fpart"] for c in cores],
                 [l7[c]["rn"] for c in cores], False)

    out = np.zeros((4, 8192, 1024), np.float32)
    for c in cores:
        b, p = c >> 1, c & 1
        out[b, p * 4096:(p + 1) * 4096] = l9[c]["rn"]
    return out


# version 16
# speedup vs baseline: 1.1581x; 1.1581x over previous
import numpy as np
import ml_dtypes
import concourse.bass as bass
import concourse.mybir as mybir
from concourse.bass_utils import run_bass_kernel_spmd

F32 = mybir.dt.float32
BF16 = mybir.dt.bfloat16
I32 = mybir.dt.int32
AF = mybir.ActivationFunctionType
ALU = mybir.AluOpType
AX = mybir.AxisListType
BF = ml_dtypes.bfloat16

D = 1024
T = 8192
DI = 2048
NS = 16
HM = 5632
NE = 4
HF = 3072
NH = 8
DH = 128
EPS = 1e-5

ENGINES = ("sync", "tensor", "vector", "scalar", "gpsimd")


class _Op:
    __slots__ = ("engine", "fn", "reads", "writes", "kind", "idx", "waits",
                 "inc_needed", "pre_wait")

    def __init__(self, engine, fn, reads, writes, kind, idx):
        self.engine = engine
        self.fn = fn
        self.reads = list(reads)
        self.writes = list(writes)
        self.kind = kind
        self.idx = idx
        self.waits = []
        self.inc_needed = kind != "c"
        self.pre_wait = None


def _split(res):
    if "@" in res:
        name, tag = res.split("@", 1)
        return name, tag
    return res, "*"


class Prog:

    def __init__(self, nc):
        self.nc = nc
        self.ops = []
        self.seq = {}
        self.state = {}
        self.bar_waits = []
        self.dirty = {}

    def barrier(self):
        last = {}
        for op in self.ops:
            last[(op.engine, op.kind)] = op
        self.bar_waits = []
        for (e, k), op in last.items():
            op.inc_needed = True
            if k in ("d", "cc"):
                self.bar_waits.append((e, k, op.idx, self.seq[(e, k)]))
                self.dirty[(e, k)] = True
            else:
                self.bar_waits.append((e, k, op.idx, None))

    def add(self, engine, fn, reads=(), writes=(), kind="c"):
        key = (engine, kind)
        idx = self.seq.get(key, 0)
        self.seq[key] = idx + 1
        op = _Op(engine, fn, reads, writes, kind, idx)
        deps = set()

        def lookup(res):
            name, tag_ = _split(res)
            st = self.state.setdefault(name, {})
            if tag_ == "*":
                return st, list(st.keys())
            keys = [tag_]
            if "*" in st:
                keys.append("*")
            return st, keys

        for res in op.reads:
            st, keys = lookup(res)
            for k in keys:
                ent = st.get(k)
                if ent and ent["w"] is not None:
                    deps.add(ent["w"])
        for res in op.writes:
            st, keys = lookup(res)
            for k in keys:
                ent = st.get(k)
                if ent:
                    if ent["w"] is not None:
                        deps.add(ent["w"])
                    for r in ent["r"]:
                        deps.add(r)
        for res in op.reads:
            name, tag_ = _split(res)
            ent = self.state.setdefault(name, {}).setdefault(
                tag_, {"w": None, "r": []})
            ent["r"].append(op)
        for res in op.writes:
            name, tag_ = _split(res)
            st = self.state.setdefault(name, {})
            if tag_ == "*":
                st.clear()
            ent = st.setdefault(tag_, {"w": None, "r": []})
            ent["w"] = op
            ent["r"] = []

        if op.kind in ("d", "cc"):
            kk = (engine, op.kind)
            if self.dirty.get(kk):
                op.pre_wait = idx
                self.dirty[kk] = False
        for dep in deps:
            if dep is op:
                continue
            if dep.engine == op.engine and dep.kind == "d" and op.kind == "d":
                continue
            if (dep.engine == "tensor" and op.engine == "tensor"
                    and dep.kind == "c" and op.kind == "c"):
                continue
            dep.inc_needed = True
            if dep.kind in ("d", "cc"):
                kk = (dep.engine, dep.kind)
                op.waits.append((dep.engine, dep.kind, dep.idx,
                                 self.seq[kk]))
                self.dirty[kk] = True
            else:
                op.waits.append((dep.engine, dep.kind, dep.idx, None))
        for bw in self.bar_waits:
            if not (bw[0] == op.engine and bw[1] == "c" == op.kind):
                op.waits.append(bw)
        self.ops.append(op)
        return op

    def dma(self, engine, out, in_, reads=(), writes=()):
        return self.add(engine, lambda e: e.dma_start(out=out, in_=in_),
                        reads=reads, writes=writes, kind="d")

    def emit(self):
        import bisect
        nc = self.nc
        targets = {}
        for op in self.ops:
            if op.inc_needed:
                targets.setdefault((op.engine, op.kind), []).append(op.idx)
        for v in targets.values():
            v.sort()
        sem_cms, sems = [], {}
        for key in targets:
            cm = nc.semaphore(f"s_{key[0]}_{key[1]}")
            sems[key] = cm.__enter__()
            sem_cms.append(cm)

        def sig(engine, kind, idx, override=None):
            if override is not None:
                step = 16 if kind == "d" else 1
                return sems[(engine, kind)], override * step
            lst = targets[(engine, kind)]
            pos = bisect.bisect_left(lst, idx)
            rank = pos + 1
            step = 16 if kind == "d" else 1
            return sems[(engine, kind)], rank * step

        per_engine = {e: [] for e in ENGINES}
        for op in self.ops:
            per_engine[op.engine].append(op)

        inc_step = {"c": 1, "d": 16, "cc": 1}
        blk_cm = nc.Block()
        block = blk_cm.__enter__()
        for engine in ENGINES:
            ops = per_engine[engine]
            if not ops:
                continue

            def make_section(ops):
                def section(eng):
                    hw = {}
                    my_tot = {}
                    for op in ops:
                        if op.pre_wait:
                            kk = (op.engine, op.kind)
                            if kk in sems:
                                step = 16 if op.kind == "d" else 1
                                val = step * op.pre_wait
                                if hw.get(sems[kk].name, 0) < val:
                                    eng.wait_ge(sems[kk], val)
                                    hw[sems[kk].name] = val
                        for dep in op.waits:
                            sem, val = sig(*dep)
                            if hw.get(sem.name, 0) < val:
                                eng.wait_ge(sem, val)
                                hw[sem.name] = val
                        inst = op.fn(eng)
                        if op.inc_needed:
                            sem = sems[(op.engine, op.kind)]
                            step = inc_step[op.kind]
                            if op.kind == "cc":
                                inst.then_inc(sem)
                            else:
                                inst.then_inc(sem, step)
                            k = (op.engine, op.kind)
                            my_tot[k] = my_tot.get(k, 0) + step
                    for (e, k), total in my_tot.items():
                        if k in ("d", "cc"):
                            eng.wait_ge(sems[(e, k)], total)
                return section

            getattr(block, engine)(make_section(ops))
        blk_cm.__exit__(None, None, None)
        for cm in sem_cms:
            cm.__exit__(None, None, None)



def emit_s1(P, nc, sbp, psp, io):
    ldq = io.get("ldq", "gpsimd")
    x, w_in = io["x"], io["w_in_t"]
    lg, lb, cw, cb = io["lg_t"], io["lb_t"], io["cw_t"], io["cb_t"]
    xb_o, sz_o, xn0_o = io["xb"], io["sz"], io["xn0T"]

    NTH = 4
    TH = T // NTH
    sb, ps = sbp, psp
    xt = [sb(f"xt{i}", [128, D], F32) for i in range(2)]
    xmu = [sb(f"xmu{i}", [128, D], F32) for i in range(2)]
    sqb = [sb(f"sqb{i}", [128, D], F32) for i in range(2)]
    xn1 = [sb(f"xn1{i}", [128, D], F32) for i in range(2)]
    xn0b = [sb(f"xn0b{i}", [128, D], BF16) for i in range(2)]
    xn2b = [sb(f"xn2b{i}", [128, D], BF16) for i in range(2)]
    st = [sb(f"st{i}", [128, 12], F32) for i in range(2)]
    lgs = sb("lgs", [128, 8], F32)
    lbs = sb("lbs", [128, 8], F32)
    cws = sb("cws", [128, 32], F32)
    cbs = sb("cbs", [128, 8], F32)
    identB = sb("identB", [128, 128], BF16)
    xnT = sb("xnT", [128, 8 * TH], BF16)
    x0T = sb("x0T", [128, 8 * TH], BF16)
    wt = [sb(f"wt{i}", [128, 1024], BF16) for i in range(2)]
    strip = sb("strip", [128, 3 + TH], F32)
    carry = sb("carry", [128, 8 * 3], F32)
    acc = [sb(f"acc{i}", [128, TH], F32) for i in range(2)]
    xbb = sb("xbb", [128, TH], BF16)
    zbb = sb("zbb", [128, TH], BF16)
    psT = [ps(f"psT{i}", [128, 128], BF16) for i in range(2)]
    psM = [ps(f"psM{i}", [128, 512], F32) for i in range(2)]

    for t_, name, src in ((lgs, "lgs", lg), (lbs, "lbs", lb),
                          (cws, "cws", cw), (cbs, "cbs", cb)):
        P.dma("sync", t_[:], src[:, :], reads=[], writes=[name])
    P.add("gpsimd", lambda e: e.memset(identB[:], 0.0), writes=["identB"])
    P.add("gpsimd", lambda e: e.affine_select(
        out=identB[:], in_=identB[:], compare_op=ALU.not_equal, fill=1.0,
        base=0, pattern=[[-1, 128]], channel_multiplier=1),
        reads=["identB"], writes=["identB"])
    P.add("gpsimd", lambda e: e.memset(carry[:], 0.0), writes=["carry"])

    def ln_plain(q, xin, xname, xout, oname):
        s, sn = st[q], f"st{q}"
        P.add("vector", lambda e: e.reduce_sum(s[:, 0:1], xin[:], axis=AX.X),
              reads=[xname], writes=[sn + "@a"])
        P.add("vector", lambda e: e.tensor_scalar(
            s[:, 1:2], s[:, 0:1], 1.0 / D, None, ALU.mult),
            reads=[sn + "@a"], writes=[sn + "@b"])
        P.add("vector", lambda e: e.tensor_scalar(
            xmu[q][:], xin[:], s[:, 1:2], None, ALU.subtract),
            reads=[xname, sn + "@b"], writes=[f"xmu{q}"])
        P.add("scalar", lambda e: e.activation(
            sqb[q][:], xmu[q][:], AF.Square, accum_out=s[:, 2:3]),
            reads=[f"xmu{q}"], writes=[f"sqb{q}", sn + "@c"])
        P.add("vector", lambda e: e.tensor_scalar(
            s[:, 5:6], s[:, 2:3], 1.0 / D, EPS, ALU.mult, ALU.add),
            reads=[sn + "@c"], writes=[sn + "@f"])
        P.add("scalar", lambda e: e.sqrt(s[:, 3:4], s[:, 5:6]),
              reads=[sn + "@f"], writes=[sn + "@d"])
        P.add("vector", lambda e: e.reciprocal(s[:, 4:5], s[:, 3:4]),
              reads=[sn + "@d"], writes=[sn + "@e"])
        P.add("vector", lambda e: e.tensor_scalar(
            xout[:], xmu[q][:], s[:, 4:5], None, ALU.mult),
            reads=[f"xmu{q}", sn + "@e"], writes=[oname])

    for th in range(NTH):
        for tt in range(TH // 128):
            q = tt % 2
            tg = th * TH + tt * 128
            P.dma("sync", xt[q][:], x[tg:tg + 128, :],
                  reads=[], writes=[f"xt{q}"])
            ln_plain(q, xt[q], f"xt{q}", xn1[q], f"xn1{q}")
            P.add("vector", lambda e, q=q: e.tensor_copy(
                xn0b[q][:], xn1[q][:]),
                reads=[f"xn1{q}"], writes=[f"xn0b{q}"])
            ln_plain(q, xn1[q], f"xn1{q}", xn2b[q], f"xn2b{q}")
            for dc in range(8):
                col = dc * TH + tt * 128
                P.add("tensor", lambda e, q=q, dc=dc: e.transpose(
                    psT[q][:], xn2b[q][:, dc * 128:(dc + 1) * 128],
                    identB[:]),
                    reads=[f"xn2b{q}", "identB"], writes=[f"psT{q}"])
                P.add("vector", lambda e, q=q, dc=dc, col=col: e.tensor_scalar(
                    xnT[:, col:col + 128], psT[q][:],
                    lgs[:, dc:dc + 1], lbs[:, dc:dc + 1], ALU.mult, ALU.add),
                    reads=[f"psT{q}", "lgs", "lbs"], writes=["xnT"])
                P.add("tensor", lambda e, q=q, dc=dc: e.transpose(
                    psT[q][:], xn0b[q][:, dc * 128:(dc + 1) * 128],
                    identB[:]),
                    reads=[f"xn0b{q}", "identB"], writes=[f"psT{q}"])
                P.add("vector", lambda e, q=q, col=col: e.tensor_copy(
                    x0T[:, col:col + 128], psT[q][:]),
                    reads=[f"psT{q}"], writes=["x0T"])
        for dc in range(8):
            P.dma("sync", xn0_o[dc * 128:(dc + 1) * 128,
                                th * TH:(th + 1) * TH],
                  x0T[:, dc * TH:(dc + 1) * TH],
                  reads=["x0T"], writes=["xn0_o"])

        for oc in range(16):
            q = oc % 2
            P.dma(ldq, wt[q][:], w_in[oc, :, :],
                  reads=[], writes=[f"wt{q}"])
            for t5 in range(TH // 512):
                pq = t5 % 2
                for dc in range(8):
                    P.add("tensor",
                          lambda e, q=q, pq=pq, dc=dc, t5=t5: e.matmul(
                              psM[pq][:], wt[q][:, dc * 128:(dc + 1) * 128],
                              xnT[:, dc * TH + t5 * 512: dc * TH + t5 * 512 + 512],
                              start=(dc == 0), stop=(dc == 7)),
                          reads=[f"wt{q}", "xnT"], writes=[f"psM{pq}"])
                if oc < 8:
                    P.add("vector", lambda e, pq=pq, t5=t5: e.tensor_copy(
                        strip[:, 3 + t5 * 512: 3 + t5 * 512 + 512],
                        psM[pq][:]),
                        reads=[f"psM{pq}"], writes=["strip"])
                else:
                    P.add("scalar", lambda e, pq=pq, t5=t5: e.activation(
                        zbb[:, t5 * 512: t5 * 512 + 512], psM[pq][:],
                        AF.Silu),
                        reads=[f"psM{pq}"], writes=["zbb"])
            if oc < 8:
                if th == 0:
                    P.add("gpsimd", lambda e: e.memset(strip[:, 0:3], 0.0),
                          reads=[], writes=["strip"])
                else:
                    P.add("vector", lambda e, oc=oc: e.tensor_copy(
                        strip[:, 0:3], carry[:, oc * 3: oc * 3 + 3]),
                        reads=["carry"], writes=["strip"])
                P.add("vector", lambda e, oc=oc: e.tensor_copy(
                    carry[:, oc * 3: oc * 3 + 3], strip[:, TH: TH + 3]),
                    reads=["strip"], writes=["carry"])
                P.add("vector", lambda e, oc=oc: e.tensor_scalar(
                    acc[0][:], strip[:, 0: TH], cws[:, oc * 4: oc * 4 + 1],
                    None, ALU.mult),
                    reads=["strip", "cws"], writes=["acc0"])
                cur = 0
                for j in range(1, 4):
                    nxt = 1 - cur
                    P.add("vector",
                          lambda e, oc=oc, j=j, cur=cur, nxt=nxt:
                          e.scalar_tensor_tensor(
                              acc[nxt][:], strip[:, j: j + TH],
                              cws[:, oc * 4 + j: oc * 4 + j + 1],
                              acc[cur][:], ALU.mult, ALU.add),
                          reads=["strip", "cws", f"acc{cur}"],
                          writes=[f"acc{nxt}"])
                    cur = nxt
                P.add("scalar", lambda e, oc=oc, cur=cur: e.activation(
                    xbb[:], acc[cur][:], AF.Silu, bias=cbs[:, oc:oc + 1]),
                    reads=[f"acc{cur}", "cbs"], writes=["xbb"])
                P.dma("sync", xb_o[oc * 128:(oc + 1) * 128,
                                   th * TH:(th + 1) * TH], xbb[:],
                      reads=["xbb"], writes=["xb_o"])
            else:
                P.dma("sync", sz_o[(oc - 8) * 128:(oc - 7) * 128,
                                   th * TH:(th + 1) * TH], zbb[:],
                      reads=["zbb"], writes=["sz_o"])


def build_s1(p=0):
    nc = bass.Bass("TRN2", target_bir_lowering=False, debug=False)
    P = Prog(nc)
    io = {
        "x": nc.dram_tensor("x", [T, D], F32, kind="ExternalInput"),
        "w_in_t": nc.dram_tensor("w_in_t", [16, 128, 1024], F32,
                                 kind="ExternalInput"),
        "lg_t": nc.dram_tensor("lg_t", [128, 8], F32, kind="ExternalInput"),
        "lb_t": nc.dram_tensor("lb_t", [128, 8], F32, kind="ExternalInput"),
        "cw_t": nc.dram_tensor("cw_t", [128, 32], F32, kind="ExternalInput"),
        "cb_t": nc.dram_tensor("cb_t", [128, 8], F32, kind="ExternalInput"),
        "xb": nc.dram_tensor("xb", [DI // 2, T], BF16, kind="ExternalOutput"),
        "sz": nc.dram_tensor("sz", [DI // 2, T], BF16, kind="ExternalOutput"),
        "xn0T": nc.dram_tensor("xn0", [D, T], BF16, kind="ExternalOutput"),
    }
    emit_s1(P, nc, nc.alloc_sbuf_tensor, nc.alloc_psum_tensor, io)
    P.emit()
    return nc



def emit_scan(P, nc, sbp, psp, io):
    ldq = io.get("ldq", "gpsimd")
    xbF, xbO, szO, x0T = io["xbF"], io["xbO"], io["szO"], io["xn0T"]
    wd, bd, wb, wc = io["wd_t"], io["bd_t"], io["wb_t"], io["wc_t"]
    la, dp, wo = io["la_t"], io["dp_t"], io["wo_t"]
    part_ap = io["part_ap"]
    bscr = io["bscr"]

    TB = 512
    NTB = T // TB
    sb, ps = sbp, psp
    xbB = [sb(f"xbB{i}", [128, 16 * TB], BF16) for i in range(2)]
    xboB = sb("xboB", [128, 8 * TB], BF16)
    szB = sb("szB", [128, 8 * TB], BF16)
    x0B = [sb(f"x0B{i}", [128, 512], BF16) for i in range(2)]
    dsb = [sb(f"dsb{i}", [128, 8 * TB], F32) for i in range(2)]
    dxb = [sb(f"dxb{i}", [128, 8 * TB], BF16) for i in range(2)]
    yB = sb("yB", [128, 8 * TB], F32)
    PBs = [sb(f"PBs{i}", [128, TB], BF16) for i in range(2)]
    PCs = [sb(f"PCs{i}", [128, TB], BF16) for i in range(2)]
    aT = [sb(f"aT{i}", [128, TB], F32) for i in range(2)] * 2
    bxT = [sb(f"bxT{i}", [128, TB], BF16) for i in range(4)]
    hT = [sb(f"hT{i}", [128, TB], BF16) for i in range(4)]
    hCT = [sb(f"hCT{i}", [128, TB], BF16) for i in range(4)]
    gB = sb("gB", [128, 8 * TB], BF16)
    pevac = [sb(f"pevac{i}", [128, 512], BF16) for i in range(2)]
    wds = sb("wds", [128, 8 * 2048], BF16)
    wos = sb("wos", [128, 8 * 1024], BF16)
    wbs = sb("wbs", [128, 256], BF16)
    wcs = sb("wcs", [128, 256], BF16)
    bds = sb("bds", [128, 8], F32)
    dps = sb("dps", [128, 8], F32)
    asb = sb("asb", [128, 128], F32)
    seeds = sb("seeds", [128, 128], F32)
    ones1 = sb("ones1", [1, 128], BF16)
    brow = [sb(f"brow{i}", [1, TB], BF16) for i in range(2)]
    bsb = sb("bsb", [16, TB], BF16)
    csb = sb("csb", [16, TB], BF16)
    spA = [sb(f"spA{i}", [128, 512], F32) for i in range(2)]
    spB = [sb(f"spB{i}", [128, 512], F32) for i in range(2)]
    ps_ = [ps(f"ps{i}", [128, 512], F32) for i in range(6)]

    for i in range(8):
        P.dma(ldq, wds[:, i * 2048:(i + 1) * 2048], wd[i, :, :],
              writes=["wds"])
        P.dma(ldq, wos[:, i * 1024:(i + 1) * 1024], wo[i, :, :],
              writes=["wos"])
    P.dma(ldq, wbs[:], wb[:, :], writes=["wbs"])
    P.dma(ldq, wcs[:], wc[:, :], writes=["wcs"])
    P.dma("sync", bds[:], bd[:, :], writes=["bds"])
    P.dma("sync", dps[:], dp[:, :], writes=["dps"])
    P.dma("sync", asb[:], la[:, :], writes=["asb"])
    P.add("scalar", lambda e: e.activation(asb[:], asb[:], AF.Exp),
          reads=["asb"], writes=["asb"])
    P.add("vector", lambda e: e.tensor_scalar(
        asb[:], asb[:], -1.0, None, ALU.mult), reads=["asb"], writes=["asb"])
    P.add("gpsimd", lambda e: e.memset(seeds[:], 0.0), writes=["seeds"])
    P.add("gpsimd", lambda e: e.memset(ones1[:], 1.0), writes=["ones1"])

    for tb in range(NTB):
        t0 = tb * TB
        tq = tb % 2
        xbBt, dsbt, dxbt = xbB[tq], dsb[tq], dxb[tq]
        for ch in range(16):
            P.dma("sync", xbBt[:, ch * TB:(ch + 1) * TB],
                  xbF[ch * 128:(ch + 1) * 128, t0:t0 + TB], writes=[f"xbB{tq}"])
        for ch in range(8):
            P.dma("sync", xboB[:, ch * TB:(ch + 1) * TB],
                  xbO[ch * 128:(ch + 1) * 128, t0:t0 + TB], writes=["xboB"])
            P.dma("sync", szB[:, ch * TB:(ch + 1) * TB],
                  szO[ch * 128:(ch + 1) * 128, t0:t0 + TB], writes=["szB"])

        for which, wsb, wname, osb, oname in (
                (0, wbs, "wbs", bsb, "bsb"), (1, wcs, "wcs", csb, "csb")):
            pp = ps_[4]
            for dc in range(16):
                P.add("tensor", lambda e, wsb=wsb, dc=dc, pp=pp, xbBt=xbBt, dsbt=dsbt, dxbt=dxbt, tq=tq:
                      e.matmul(pp[0:16, :], wsb[:, dc * 16:(dc + 1) * 16],
                               xbBt[:, dc * TB: dc * TB + TB],
                               start=(dc == 0), stop=(dc == 15)),
                      reads=[wname, f"xbB{tq}"], writes=["ps4"])
            P.add("vector", lambda e, osb=osb, pp=pp, xbBt=xbBt, dsbt=dsbt, dxbt=dxbt, tq=tq:
                  e.tensor_copy(osb[:], pp[0:16, :]),
                  reads=["ps4"], writes=[oname])
            P.dma("sync", bscr[which, :, :], osb[:], reads=[oname],
                  writes=[f"bscr{which}"])

        for oc in range(8):
            pp = ps_[oc % 2]
            for dc in range(16):
                P.add("tensor", lambda e, oc=oc, dc=dc, pp=pp, xbBt=xbBt, dsbt=dsbt, dxbt=dxbt, tq=tq:
                      e.matmul(pp[:],
                               wds[:, oc * 2048 + dc * 128:
                                   oc * 2048 + (dc + 1) * 128],
                               xbBt[:, dc * TB: dc * TB + TB],
                               start=(dc == 0), stop=(dc == 15)),
                      reads=["wds", f"xbB{tq}"], writes=[f"ps{oc % 2}"])
            q = oc % 2
            P.add("scalar", lambda e, oc=oc, pp=pp, q=q, xbBt=xbBt, dsbt=dsbt, dxbt=dxbt, tq=tq: e.activation(
                spA[q][:], pp[:], AF.Abs, bias=bds[:, oc:oc + 1]),
                reads=[f"ps{oc % 2}", "bds"], writes=[f"spA{q}"])
            P.add("scalar", lambda e, q=q, xbBt=xbBt, dsbt=dsbt, dxbt=dxbt, tq=tq: e.activation(
                spA[q][:], spA[q][:], AF.Exp, scale=-1.0),
                reads=[f"spA{q}"], writes=[f"spA{q}"])
            P.add("scalar", lambda e, q=q, xbBt=xbBt, dsbt=dsbt, dxbt=dxbt, tq=tq: e.activation(
                spA[q][:], spA[q][:], AF.Ln, bias=1.0),
                reads=[f"spA{q}"], writes=[f"spA{q}"])
            P.add("scalar", lambda e, oc=oc, pp=pp, q=q, xbBt=xbBt, dsbt=dsbt, dxbt=dxbt, tq=tq: e.activation(
                spB[q][:], pp[:], AF.Relu, bias=bds[:, oc:oc + 1]),
                reads=[f"ps{oc % 2}", "bds"], writes=[f"spB{q}"])
            P.add("vector", lambda e, oc=oc, q=q, xbBt=xbBt, dsbt=dsbt, dxbt=dxbt, tq=tq: e.tensor_add(
                dsbt[:, oc * TB:(oc + 1) * TB], spA[q][:], spB[q][:]),
                reads=[f"spA{q}", f"spB{q}"], writes=[f"dsb{tq}"])
            P.add("vector", lambda e, oc=oc, xbBt=xbBt, dsbt=dsbt, dxbt=dxbt, tq=tq: e.tensor_mul(
                dxbt[:, oc * TB:(oc + 1) * TB], dsbt[:, oc * TB:(oc + 1) * TB],
                xboB[:, oc * TB:(oc + 1) * TB]),
                reads=[f"dsb{tq}", "xboB"], writes=[f"dxb{tq}"])

        for n in range(NS):
            nq = n % 2
            for which, plane, pname in ((0, PBs[nq], f"PBs{nq}"),
                                        (1, PCs[nq], f"PCs{nq}")):
                P.dma("sync", brow[which][:], bscr[which, n:n + 1, :],
                      reads=[f"bscr{which}"], writes=[f"brow{which}"])
                pp = ps_[2 + which]
                P.add("tensor", lambda e, which=which, pp=pp, xbBt=xbBt, dsbt=dsbt, dxbt=dxbt, tq=tq:
                      e.matmul(pp[:], ones1[:], brow[which][0:1, :],
                               start=True, stop=True),
                      reads=[f"brow{which}", "ones1"],
                      writes=[f"ps{2 + which}"])
                P.add("scalar", lambda e, plane=plane, pp=pp, xbBt=xbBt, dsbt=dsbt, dxbt=dxbt, tq=tq:
                      e.activation(plane[:], pp[:], AF.Copy),
                      reads=[f"ps{2 + which}"], writes=[pname])
            for oc in range(8):
                q = (n * 8 + oc) % 4
                ncol = oc * 16 + n
                P.add("scalar", lambda e, oc=oc, ncol=ncol, q=q, xbBt=xbBt, dsbt=dsbt, dxbt=dxbt, tq=tq: e.activation(
                    aT[q][:], dsbt[:, oc * TB:(oc + 1) * TB], AF.Exp,
                    scale=asb[:, ncol:ncol + 1]),
                    reads=[f"dsb{tq}", "asb"], writes=[f"aT{q % 2}"])
                P.add("vector", lambda e, oc=oc, q=q, nq=nq, xbBt=xbBt, dsbt=dsbt, dxbt=dxbt, tq=tq: e.tensor_mul(
                    bxT[q][:], dxbt[:, oc * TB:(oc + 1) * TB], PBs[nq][:]),
                    reads=[f"dxb{tq}", f"PBs{nq}"], writes=[f"bxT{q}"])
                P.add("vector", lambda e, ncol=ncol, q=q: e.tensor_tensor_scan(
                    hT[q][:], aT[q][:], bxT[q][:], seeds[:, ncol:ncol + 1],
                    ALU.mult, ALU.add),
                    reads=[f"aT{q % 2}", f"bxT{q}", f"seeds@{ncol}"],
                    writes=[f"hT{q}"])
                P.add("scalar", lambda e, ncol=ncol, q=q, xbBt=xbBt, dsbt=dsbt, dxbt=dxbt, tq=tq: e.copy(
                    seeds[:, ncol:ncol + 1], hT[q][:, TB - 1:TB]),
                    reads=[f"hT{q}"], writes=[f"seeds@{ncol}"])
                if n == 0:
                    P.add("vector", lambda e, oc=oc, q=q, nq=nq, xbBt=xbBt, dsbt=dsbt, dxbt=dxbt, tq=tq: e.tensor_mul(
                        yB[:, oc * TB:(oc + 1) * TB], hT[q][:], PCs[nq][:]),
                        reads=[f"hT{q}", f"PCs{nq}"], writes=[f"yB@{oc}"])
                else:
                    P.add("vector", lambda e, q=q, nq=nq, xbBt=xbBt, dsbt=dsbt, dxbt=dxbt, tq=tq: e.tensor_mul(
                        hCT[q][:], hT[q][:], PCs[nq][:]),
                        reads=[f"hT{q}", f"PCs{nq}"], writes=[f"hCT{q}"])
                    P.add("gpsimd", lambda e, oc=oc, q=q: e.tensor_add(
                        yB[:, oc * TB:(oc + 1) * TB],
                        yB[:, oc * TB:(oc + 1) * TB], hCT[q][:]),
                        reads=[f"hCT{q}", f"yB@{oc}"], writes=[f"yB@{oc}"])

        for oc in range(8):
            q = oc % 2
            P.add("vector", lambda e, oc=oc, q=q, xbBt=xbBt, dsbt=dsbt, dxbt=dxbt, tq=tq: e.scalar_tensor_tensor(
                hCT[q][:], xboB[:, oc * TB:(oc + 1) * TB], dps[:, oc:oc + 1],
                yB[:, oc * TB:(oc + 1) * TB], ALU.mult, ALU.add),
                reads=["xboB", "dps", f"yB@{oc}"], writes=[f"hCT{q}"])
            P.add("vector", lambda e, oc=oc, q=q, xbBt=xbBt, dsbt=dsbt, dxbt=dxbt, tq=tq: e.tensor_mul(
                gB[:, oc * TB:(oc + 1) * TB], hCT[q][:],
                szB[:, oc * TB:(oc + 1) * TB]),
                reads=[f"hCT{q}", "szB"], writes=["gB"])

        for do in range(8):
            pp = ps_[4 + do % 2]
            P.dma("sync", x0B[do % 2][:],
                  x0T[do * 128:(do + 1) * 128, t0:t0 + TB],
                  writes=[f"x0B{do % 2}"])
            for dc in range(8):
                P.add("tensor", lambda e, do=do, dc=dc, pp=pp, xbBt=xbBt, dsbt=dsbt, dxbt=dxbt, tq=tq:
                      e.matmul(pp[:],
                               wos[:, do * 1024 + dc * 128:
                                   do * 1024 + (dc + 1) * 128],
                               gB[:, dc * TB: dc * TB + TB],
                               start=(dc == 0), stop=(dc == 7)),
                      reads=["wos", "gB"], writes=[f"ps{4 + do % 2}"])
            P.add("vector", lambda e, pp=pp, do=do, xbBt=xbBt, dsbt=dsbt, dxbt=dxbt, tq=tq: e.scalar_tensor_tensor(
                pevac[do % 2][:], x0B[do % 2][:], 0.5, pp[:],
                ALU.mult, ALU.add),
                reads=[f"ps{4 + do % 2}", f"x0B{do % 2}"],
                writes=[f"pevac{do % 2}"])
            P.dma("sync", part_ap(do, t0), pevac[do % 2][:],
                  reads=[f"pevac{do % 2}"], writes=["part_o"])


def build_scan():
    nc = bass.Bass("TRN2", target_bir_lowering=False, debug=False)
    P = Prog(nc)
    io = {
        "xbF": nc.dram_tensor("xbF", [DI, T], BF16, kind="ExternalInput"),
        "xbO": nc.dram_tensor("xbO", [DI // 2, T], BF16, kind="ExternalInput"),
        "szO": nc.dram_tensor("sz", [DI // 2, T], BF16, kind="ExternalInput"),
        "xn0T": nc.dram_tensor("xn0", [D, T], BF16, kind="ExternalInput"),
        "wd_t": nc.dram_tensor("wd_t", [8, 128, 2048], F32, kind="ExternalInput"),
        "bd_t": nc.dram_tensor("bd_t", [128, 8], F32, kind="ExternalInput"),
        "wb_t": nc.dram_tensor("wb_t", [128, 256], F32, kind="ExternalInput"),
        "wc_t": nc.dram_tensor("wc_t", [128, 256], F32, kind="ExternalInput"),
        "la_t": nc.dram_tensor("la_t", [128, 128], F32, kind="ExternalInput"),
        "dp_t": nc.dram_tensor("dp_t", [128, 8], F32, kind="ExternalInput"),
        "wo_t": nc.dram_tensor("wo_t", [8, 128, 1024], F32, kind="ExternalInput"),
        "bscr": nc.dram_tensor("bscr", [2, 16, 512], BF16, kind="Internal"),
    }
    part = nc.dram_tensor("part", [2, D, T // 2], BF16, kind="ExternalOutput")
    io["part_ap"] = lambda do, t0: part[t0 // 4096, do * 128:(do + 1) * 128,
                                        t0 % 4096: t0 % 4096 + 512]
    emit_scan(P, nc, nc.alloc_sbuf_tensor, nc.alloc_psum_tensor, io)
    P.emit()
    return nc


def stage_scan_inputs(inputs, cid, xb_even_first, xb_own, sz_own, xn0T):
    b, p = cid >> 1, cid & 1
    own = slice(p * 1024, (p + 1) * 1024)
    Wd = np.asarray(inputs["L0_attn_W_delta"])[:, own]
    Wo = np.asarray(inputs["L0_attn_W_out"])[own]
    return {
        "xbF": xb_even_first,
        "xbO": xb_own,
        "sz": sz_own,
        "xn0": xn0T,
        "wd_t": tile_lhsT(Wd).reshape(8, 128, 2048),
        "bd_t": vec_t(np.asarray(inputs["L0_attn_b_delta"])[own]),
        "wb_t": np.ascontiguousarray(
            np.asarray(inputs["L0_attn_W_b"]).reshape(16, 128, 16)
            .transpose(1, 0, 2).reshape(128, 256)),
        "wc_t": np.ascontiguousarray(
            np.asarray(inputs["L0_attn_W_c"]).reshape(16, 128, 16)
            .transpose(1, 0, 2).reshape(128, 256)),
        "la_t": np.ascontiguousarray(
            np.asarray(inputs["L0_attn_log_A"])[own].reshape(8, 128, 16)
            .transpose(1, 0, 2).reshape(128, 128)),
        "dp_t": vec_t(np.asarray(inputs["L0_attn_Dp"])[own]),
        "wo_t": tile_lhsT(Wo).reshape(8, 128, 1024),
    }



def emit_combine(P, nc, sbp, psp, io, want_xn, summed):
    TO = T // 2
    rp = io["rp"]
    rn_o = io["rn"]
    sb, ps = sbp, psp

    pat = [sb(f"pat{i}", [128, 128], BF16) for i in range(2)]
    pbt = [sb(f"pbt{i}", [128, 128], BF16) for i in range(2)]
    sumb = sb("sumb", [128, 128], BF16)
    mo = [sb(f"mo{i}", [128, D], F32) for i in range(2)]
    rt = [sb(f"rt{i}", [128, D], F32) for i in range(2)]
    xmu = [sb(f"xmu{i}", [128, D], F32) for i in range(2)]
    sqb = [sb(f"sqb{i}", [128, D], F32) for i in range(2)]
    lnv = [sb(f"lnv{i}", [128, D], F32) for i in range(2)]
    rnb = [sb(f"rnb{i}", [128, D], F32) for i in range(2)]
    xnb = [sb(f"xnb{i}", [128, D], BF16) for i in range(2)]
    st = [sb(f"st{i}", [128, 12], F32) for i in range(2)]
    identB = sb("identB", [128, 128], BF16)
    psT = [ps(f"psT{i}", [128, 128], BF16) for i in range(2)]

    P.add("gpsimd", lambda e: e.memset(identB[:], 0.0), writes=["identB"])
    P.add("gpsimd", lambda e: e.affine_select(
        out=identB[:], in_=identB[:], compare_op=ALU.not_equal, fill=1.0,
        base=0, pattern=[[-1, 128]], channel_multiplier=1),
        reads=["identB"], writes=["identB"])

    def ln_plain(q, xin, xname, xout, oname):
        s, sn = st[q], f"st{q}"
        P.add("vector", lambda e: e.reduce_sum(s[:, 0:1], xin[:], axis=AX.X),
              reads=[xname], writes=[sn + "@a"])
        P.add("vector", lambda e: e.tensor_scalar(
            s[:, 1:2], s[:, 0:1], 1.0 / D, None, ALU.mult),
            reads=[sn + "@a"], writes=[sn + "@b"])
        P.add("vector", lambda e: e.tensor_scalar(
            xmu[q][:], xin[:], s[:, 1:2], None, ALU.subtract),
            reads=[xname, sn + "@b"], writes=[f"xmu{q}"])
        P.add("scalar", lambda e: e.activation(
            sqb[q][:], xmu[q][:], AF.Square, accum_out=s[:, 2:3]),
            reads=[f"xmu{q}"], writes=[f"sqb{q}", sn + "@c"])
        P.add("vector", lambda e: e.tensor_scalar(
            s[:, 5:6], s[:, 2:3], 1.0 / D, EPS, ALU.mult, ALU.add),
            reads=[sn + "@c"], writes=[sn + "@f"])
        P.add("scalar", lambda e: e.sqrt(s[:, 3:4], s[:, 5:6]),
              reads=[sn + "@f"], writes=[sn + "@d"])
        P.add("vector", lambda e: e.reciprocal(s[:, 4:5], s[:, 3:4]),
              reads=[sn + "@d"], writes=[sn + "@e"])
        P.add("vector", lambda e: e.tensor_scalar(
            xout[:], xmu[q][:], s[:, 4:5], None, ALU.mult),
            reads=[f"xmu{q}", sn + "@e"], writes=[oname])

    for tt in range(TO // 128):
        q = tt % 2
        t0 = tt * 128
        for dc in range(8):
            dq = dc % 2
            if summed:
                pa = io["psum"]
                P.dma("sync", pat[dq][:],
                      pa[dc * 128:(dc + 1) * 128, t0:t0 + 128],
                      writes=[f"pat{dq}"])
                P.add("tensor", lambda e, q=q, dq=dq: e.transpose(
                    psT[q][:], pat[dq][:], identB[:]),
                    reads=[f"pat{dq}", "identB"], writes=[f"psT{q}"])
            else:
                P.dma("sync", pat[dq][:],
                      io["pa"][dc * 128:(dc + 1) * 128, t0:t0 + 128],
                      writes=[f"pat{dq}"])
                P.dma("sync", pbt[dq][:],
                      io["pb"][dc * 128:(dc + 1) * 128, t0:t0 + 128],
                      writes=[f"pbt{dq}"])
                P.add("vector", lambda e, dq=dq: e.tensor_add(
                    sumb[:], pat[dq][:], pbt[dq][:]),
                    reads=[f"pat{dq}", f"pbt{dq}"], writes=["sumb"])
                P.add("tensor", lambda e, q=q: e.transpose(
                    psT[q][:], sumb[:], identB[:]),
                    reads=["sumb", "identB"], writes=[f"psT{q}"])
            P.add("vector", lambda e, q=q, dc=dc: e.tensor_copy(
                mo[q][:, dc * 128:(dc + 1) * 128], psT[q][:]),
                reads=[f"psT{q}"], writes=[f"mo{q}"])
        ln_plain(q, mo[q], f"mo{q}", lnv[q], f"lnv{q}")
        P.dma("sync", rt[q][:], rp[t0:t0 + 128, :], writes=[f"rt{q}"])
        P.add("vector", lambda e, q=q: e.tensor_add(
            rnb[q][:], rt[q][:], lnv[q][:]),
            reads=[f"rt{q}", f"lnv{q}"], writes=[f"rnb{q}"])
        P.dma("sync", rn_o[t0:t0 + 128, :], rnb[q][:],
              reads=[f"rnb{q}"], writes=["rn_o"])
        if want_xn:
            xn_ap = io["xn_ap"]
            ln_plain(q, rnb[q], f"rnb{q}", xnb[q], f"xnb{q}")
            for dc in range(8):
                P.add("tensor", lambda e, q=q, dc=dc: e.transpose(
                    psT[q][:], xnb[q][:, dc * 128:(dc + 1) * 128], identB[:]),
                    reads=[f"xnb{q}", "identB"], writes=[f"psT{q}"])
                P.add("vector", lambda e, q=q: e.tensor_copy(
                    sumb[:], psT[q][:]),
                    reads=[f"psT{q}"], writes=["sumb"])
                P.dma("sync", xn_ap(dc, t0), sumb[:],
                      reads=["sumb"], writes=["xn_o"])


def build_combine(want_xn, summed=False):
    nc = bass.Bass("TRN2", target_bir_lowering=False, debug=False)
    P = Prog(nc)
    TO = T // 2
    io = {"rp": nc.dram_tensor("rp", [TO, D], F32, kind="ExternalInput"),
          "rn": nc.dram_tensor("rn", [TO, D], F32, kind="ExternalOutput")}
    if summed:
        io["psum"] = nc.dram_tensor("psm", [D, TO], BF16, kind="ExternalInput")
    else:
        io["pa"] = nc.dram_tensor("pa", [D, TO], BF16, kind="ExternalInput")
        io["pb"] = nc.dram_tensor("pb", [D, TO], BF16, kind="ExternalInput")
    if want_xn:
        xn_o = nc.dram_tensor("xnx", [D, TO], BF16, kind="ExternalOutput")
        io["xn_ap"] = lambda dc, t0: xn_o[dc * 128:(dc + 1) * 128, t0:t0 + 128]
    emit_combine(P, nc, nc.alloc_sbuf_tensor, nc.alloc_psum_tensor, io,
                 want_xn, summed)
    P.emit()
    return nc



def emit_moe(P, nc, sbp, psp, io):
    ldq = io.get("ldq", "gpsimd")
    sb, ps = sbp, psp
    gw, sel = io["gw_t"], io["sel_t"]
    w1, w3, w2 = io["w1_t"], io["w3_t"], io["w2_t"]
    xn_src = io["xn_src"]
    part_ap = io["part_ap"]

    TBK = 2048
    NBK = T // TBK
    SUP = (8, 8, 8, 8, 8, 4)

    gws = sb("gws", [128, 32], F32)
    xng = sb("xng", [128, 8 * 128], F32)
    sels = sb("sels", [4, 256], F32)
    xnB = sb("xnB", [128, 8 * TBK], BF16)
    lg4 = sb("lg4", [128, 4], F32)
    cw4 = sb("cw4", [128, 4], F32)
    cwT = sb("cwT", [4, T], F32)
    pl = [sb(f"pl{i}", [128, TBK], F32) for i in range(2)]
    sil = [sb(f"sil{i}", [128, 512], F32) for i in range(2)]
    hstr = sb("hstr", [128, 8 * TBK], BF16)
    oac = sb("oac", [128, 8 * TBK], F32)
    tmp5 = [sb(f"tmp5{i}", [128, 512], F32) for i in range(2)]
    pevac = [sb(f"pevac{i}", [128, 512], BF16) for i in range(2)]
    w13s = [sb(f"w13s{i}", [128, 1024], BF16) for i in range(4)]
    w2s = [sb(f"w2s{i}", [128, 1024], BF16) for i in range(2)]
    stm = sb("stm", [128, 10], F32)
    cmpb = sb("cmpb", [128, 4], F32)
    identF = sb("identF", [128, 128], F32)
    ps_ = [ps(f"ps{i}", [128, 512], F32) for i in range(8)]

    P.dma("sync", gws[:], gw[:, :], writes=["gws"])
    P.dma("sync", sels[:], sel[:, :], writes=["sels"])
    P.add("gpsimd", lambda e: e.memset(identF[:], 0.0), writes=["identF"])
    P.add("gpsimd", lambda e: e.affine_select(
        out=identF[:], in_=identF[:], compare_op=ALU.not_equal, fill=1.0,
        base=0, pattern=[[-1, 128]], channel_multiplier=1),
        reads=["identF"], writes=["identF"])

    for tt in range(T // 128):
        t0 = tt * 128
        pp = ps_[0]
        for dc in range(8):
            P.dma("gpsimd", xng[:, dc * 128:(dc + 1) * 128],
                  xn_src(dc, t0, 128), writes=["xng"])
        for dc in range(8):
            P.add("tensor", lambda e, dc=dc, pp=pp: e.matmul(
                pp[0:128, 0:4], xng[:, dc * 128:(dc + 1) * 128],
                gws[:, dc * 4:(dc + 1) * 4],
                start=(dc == 0), stop=(dc == 7)),
                reads=["xng", "gws"], writes=["ps0"])
        P.add("vector", lambda e, pp=pp: e.tensor_copy(lg4[:], pp[0:128, 0:4]),
              reads=["ps0"], writes=["lg4"])
        P.add("gpsimd", lambda e: e.memset(cw4[:], 0.0), writes=["cw4"])
        for e_ in range(4):
            for j in range(4):
                if j == e_:
                    continue
                op = ALU.is_ge if j < e_ else ALU.is_gt
                P.add("vector", lambda e, e_=e_, j=j, op=op: e.tensor_tensor(
                    cmpb[:, e_:e_ + 1], lg4[:, j:j + 1], lg4[:, e_:e_ + 1], op),
                    reads=["lg4"], writes=[f"cmpb@{e_}"])
                P.add("vector", lambda e, e_=e_: e.tensor_add(
                    cw4[:, e_:e_ + 1], cw4[:, e_:e_ + 1], cmpb[:, e_:e_ + 1]),
                    reads=[f"cmpb@{e_}"], writes=["cw4"])
        P.add("vector", lambda e: e.tensor_scalar(
            cw4[:], cw4[:], 1.5, None, ALU.is_lt),
            reads=["cw4"], writes=["cw4"])
        P.add("vector", lambda e: e.reduce_max(
            stm[:, 0:1], lg4[:], axis=AX.X, negate=True),
            reads=["lg4"], writes=["stm@a"])
        P.add("scalar", lambda e: e.activation(
            cmpb[:], lg4[:], AF.Exp, bias=stm[:, 0:1]),
            reads=["lg4", "stm@a"], writes=["cmpb"])
        P.add("vector", lambda e: e.tensor_mul(cw4[:], cw4[:], cmpb[:]),
              reads=["cw4", "cmpb"], writes=["cw4"])
        P.add("vector", lambda e: e.reduce_sum(
            stm[:, 1:2], cw4[:], axis=AX.X),
            reads=["cw4"], writes=["stm@b"])
        P.add("vector", lambda e: e.reciprocal(stm[:, 2:3], stm[:, 1:2]),
              reads=["stm@b"], writes=["stm@c"])
        P.add("vector", lambda e: e.tensor_scalar(
            cw4[:], cw4[:], stm[:, 2:3], None, ALU.mult),
            reads=["cw4", "stm@c"], writes=["cw4"])
        P.add("tensor", lambda e, pp=pp: e.transpose(
            pp[0:4, 0:128], cw4[:], identF[:]),
            reads=["cw4", "identF"], writes=["ps0"])
        P.add("vector", lambda e, t0=t0, pp=pp: e.tensor_copy(
            cwT[:, t0:t0 + 128], pp[0:4, 0:128]),
            reads=["ps0"], writes=["cwT"])

    for bk in range(NBK):
        t0 = bk * TBK
        for dc in range(8):
            P.dma("sync", xnB[:, dc * TBK:(dc + 1) * TBK],
                  xn_src(dc, t0, TBK), writes=["xnB"])
        for le in range(2):
            for t5 in range(4):
                pp = ps_[5]
                P.add("tensor", lambda e, le=le, t5=t5, t0=t0, pp=pp:
                      e.matmul(pp[:],
                               sels[:, le * 128:(le + 1) * 128],
                               cwT[:, t0 + t5 * 512: t0 + t5 * 512 + 512],
                               start=True, stop=True),
                      reads=["sels", "cwT"], writes=["ps5"])
                P.add("vector", lambda e, le=le, t5=t5, pp=pp: e.tensor_copy(
                    pl[le][:, t5 * 512: t5 * 512 + 512], pp[:]),
                    reads=["ps5"], writes=[f"pl{le}"])
        first = True
        for le in range(2):
            hbase = 0
            for si, sn_ in enumerate(SUP):
                for hi in range(sn_):
                    ht = hbase + hi
                    qa, qb = (ht % 2) * 2, (ht % 2) * 2 + 1
                    P.dma(ldq, w13s[qa][:], w1[le, ht, :, :],
                          writes=[f"w13s{qa}"])
                    P.dma(ldq, w13s[qb][:], w3[le, ht, :, :],
                          writes=[f"w13s{qb}"])
                    for t5 in range(4):
                        gi, ui = t5 % 2, 2 + t5 % 2
                        for dc in range(8):
                            P.add("tensor",
                                  lambda e, qa=qa, dc=dc, t5=t5, gi=gi:
                                  e.matmul(ps_[gi][:],
                                           w13s[qa][:, dc * 128:(dc + 1) * 128],
                                           xnB[:, dc * TBK + t5 * 512:
                                               dc * TBK + t5 * 512 + 512],
                                           start=(dc == 0), stop=(dc == 7)),
                                  reads=[f"w13s{qa}", "xnB"],
                                  writes=[f"ps{gi}"])
                            P.add("tensor",
                                  lambda e, qb=qb, dc=dc, t5=t5, ui=ui:
                                  e.matmul(ps_[ui][:],
                                           w13s[qb][:, dc * 128:(dc + 1) * 128],
                                           xnB[:, dc * TBK + t5 * 512:
                                               dc * TBK + t5 * 512 + 512],
                                           start=(dc == 0), stop=(dc == 7)),
                                  reads=[f"w13s{qb}", "xnB"],
                                  writes=[f"ps{ui}"])
                        P.add("scalar", lambda e, t5=t5, gi=gi: e.activation(
                            sil[t5 % 2][:], ps_[gi][:], AF.Silu),
                            reads=[f"ps{gi}"], writes=[f"sil{t5 % 2}"])
                        P.add("vector",
                              lambda e, hi=hi, t5=t5, ui=ui: e.tensor_mul(
                                  hstr[:, hi * TBK + t5 * 512:
                                       hi * TBK + t5 * 512 + 512],
                                  sil[t5 % 2][:], ps_[ui][:]),
                              reads=[f"sil{t5 % 2}", f"ps{ui}"],
                              writes=["hstr"])
                for do in range(8):
                    qd = do % 2
                    P.dma(ldq, w2s[qd][:, 0:sn_ * 128],
                          w2[le, do, :, hbase * 128:(hbase + sn_) * 128],
                          writes=[f"w2s{qd}"])
                    for t5 in range(4):
                        pi = 4 + (qd * 2 + t5 % 2) % 4
                        for hc in range(sn_):
                            P.add("tensor",
                                  lambda e, qd=qd, hc=hc, t5=t5, pi=pi:
                                  e.matmul(ps_[pi][:],
                                           w2s[qd][:, hc * 128:(hc + 1) * 128],
                                           hstr[:, hc * TBK + t5 * 512:
                                                hc * TBK + t5 * 512 + 512],
                                           start=(hc == 0), stop=(hc == sn_ - 1)),
                                  reads=[f"w2s{qd}", "hstr"],
                                  writes=[f"ps{pi}"])
                        P.add("vector",
                              lambda e, le=le, t5=t5, pi=pi: e.tensor_mul(
                                  tmp5[t5 % 2][:], ps_[pi][:],
                                  pl[le][:, t5 * 512: t5 * 512 + 512]),
                              reads=[f"ps{pi}", f"pl{le}"],
                              writes=[f"tmp5{t5 % 2}"])
                        if first:
                            P.add("gpsimd",
                                  lambda e, do=do, t5=t5: e.tensor_copy(
                                      oac[:, do * TBK + t5 * 512:
                                          do * TBK + t5 * 512 + 512],
                                      tmp5[t5 % 2][:]),
                                  reads=[f"tmp5{t5 % 2}"],
                                  writes=[f"oac@{do}_{t5}"])
                        else:
                            P.add("gpsimd",
                                  lambda e, do=do, t5=t5: e.tensor_add(
                                      oac[:, do * TBK + t5 * 512:
                                          do * TBK + t5 * 512 + 512],
                                      oac[:, do * TBK + t5 * 512:
                                          do * TBK + t5 * 512 + 512],
                                      tmp5[t5 % 2][:]),
                                  reads=[f"tmp5{t5 % 2}", f"oac@{do}_{t5}"],
                                  writes=[f"oac@{do}_{t5}"])
                hbase += sn_
                first = False
        for do in range(8):
            for t5 in range(4):
                P.add("vector", lambda e, do=do, t5=t5: e.tensor_copy(
                    pevac[t5 % 2][:], oac[:, do * TBK + t5 * 512:
                                          do * TBK + t5 * 512 + 512]),
                    reads=[f"oac@{do}_{t5}"], writes=[f"pevac{t5 % 2}"])
                P.dma("sync", part_ap(do, t0 + t5 * 512),
                      pevac[t5 % 2][:], reads=[f"pevac{t5 % 2}"],
                      writes=["mp_o"])





def build_attn():
    nc = bass.Bass("TRN2", target_bir_lowering=False, debug=False)
    P = Prog(nc)
    xna = nc.dram_tensor("xna", [D, T], BF16, kind="ExternalInput")
    ap_o = nc.dram_tensor("apart", [2, D, T // 2], BF16, kind="ExternalOutput")
    io = {
        "oscr": nc.dram_tensor("oscr", [4, 128, T], BF16, kind="Internal"),
        "wq_t": nc.dram_tensor("wq_t", [4, 128, 1024], F32, kind="ExternalInput"),
        "wk_t": nc.dram_tensor("wk_t", [4, 128, 1024], F32, kind="ExternalInput"),
        "wv_t": nc.dram_tensor("wv_t", [4, 128, 1024], F32, kind="ExternalInput"),
        "wo_t": nc.dram_tensor("wo_t", [8, 128, 512], F32, kind="ExternalInput"),
        "cos_t": nc.dram_tensor("cos_t", [32, T], F32, kind="ExternalInput"),
        "sin_t": nc.dram_tensor("sin_t", [32, T], F32, kind="ExternalInput"),
        "mask_t": nc.dram_tensor("mask_t", [128, 4 * 512], BF16, kind="ExternalInput"),
        "csc_t": nc.dram_tensor("csc_t", [128, 1], F32, kind="ExternalInput"),
        "xn_src": lambda dc, t0, w: xna[dc * 128:(dc + 1) * 128, t0:t0 + w],
        "part_ap": lambda do, t0: ap_o[t0 // 4096, do * 128:(do + 1) * 128,
                                       t0 % 4096: t0 % 4096 + 512],
    }
    emit_attn(P, nc, nc.alloc_sbuf_tensor, nc.alloc_psum_tensor, io)
    P.emit()
    return nc


def emit_attn(P, nc, sbp, psp, io):
    ldq = io.get("ldq", "gpsimd")
    oscr = io["oscr"]
    sb, ps = sbp, psp
    wq, wk, wv, wo = io["wq_t"], io["wk_t"], io["wv_t"], io["wo_t"]
    cosd, sind = io["cos_t"], io["sin_t"]
    maskd, cscd = io["mask_t"], io["csc_t"]
    xn_src = io["xn_src"]
    part_ap = io["part_ap"]

    QT = sb("QT", [128, T], BF16)
    KT = sb("KT", [128, T], BF16)
    VSB = sb("VSB", [128, T], BF16)
    oT4 = [sb(f"oT4_{i}", [128, 512], BF16) for i in range(2)]
    Sstr = [sb(f"Sstr{i}", [128, T], F32) for i in range(2)]
    Pstr = [sb("Pstr0", [128, T], BF16)] * 2
    xnb = sb("xnb", [128, 8 * 512], BF16)
    wqs = sb("wqs", [128, 4 * 1024], BF16)
    wks = sb("wks", [128, 4 * 1024], BF16)
    wvs = sb("wvs", [128, 4 * 1024], BF16)
    wos = sb("wos", [128, 8 * 512], BF16)
    coss = sb("coss", [32, 512], F32)
    sins = sb("sins", [32, 512], F32)
    masks = sb("masks", [128, 4 * 512], BF16)
    cscs = sb("cscs", [128, 1], F32)
    identB = sb("identB", [128, 128], BF16)
    identF = sb("identF", [128, 128], F32)
    rtmp = [sb(f"rtmp{i}", [32, 512], F32) for i in range(4)]
    PTs = [sb(f"PTs{i}", [128, 1024], BF16) for i in range(2)]
    lac = [sb(f"lac{i}", [128, 4], F32) for i in range(2)]
    rowred = [sb(f"rowred{i}", [128, 1], F32) for i in range(2)]
    lrs = sb("lrs", [1, 128], F32)
    nmrow = [sb(f"nmrow{i}", [128, 64], F32) for i in range(2)]
    lpls = sb("lpls", [128, 128], F32)
    ones1 = sb("ones1", [1, 128], F32)
    oev = sb("oev", [128, 512], BF16)
    psS = ps("psS", [128, 512], F32)
    psS2 = ps("psS2", [128, 512], F32)
    psPV_ = [ps(f"psPV{i}", [128, 512], F32) for i in range(2)]
    psAux = ps("psAux", [128, 512], F32)
    psQ = ps("psQ", [128, 512], F32)
    psTb = [ps(f"psTb{i}", [128, 1024], BF16) for i in range(2)]

    for tens, name, src_, n_ in ((wqs, "wqs", wq, 4), (wks, "wks", wk, 4),
                                 (wvs, "wvs", wv, 4)):
        for h in range(n_):
            P.dma(ldq, tens[:, h * 1024:(h + 1) * 1024], src_[h, :, :],
                  writes=[name])
    for do in range(8):
        P.dma(ldq, wos[:, do * 512:(do + 1) * 512], wo[do, :, :],
              writes=["wos"])
    P.dma("sync", masks[:], maskd[:, :], writes=["masks"])
    P.dma("sync", cscs[:], cscd[:, :], writes=["cscs"])
    P.add("gpsimd", lambda e: e.memset(identB[:], 0.0), writes=["identB"])
    P.add("gpsimd", lambda e: e.affine_select(
        out=identB[:], in_=identB[:], compare_op=ALU.not_equal, fill=1.0,
        base=0, pattern=[[-1, 128]], channel_multiplier=1),
        reads=["identB"], writes=["identB"])
    P.add("gpsimd", lambda e: e.memset(identF[:], 0.0), writes=["identF"])
    P.add("gpsimd", lambda e: e.affine_select(
        out=identF[:], in_=identF[:], compare_op=ALU.not_equal, fill=1.0,
        base=0, pattern=[[-1, 128]], channel_multiplier=1),
        reads=["identF"], writes=["identF"])
    P.add("gpsimd", lambda e: e.memset(ones1[:], 1.0), writes=["ones1"])

    def rope_evac(head, t5, pp, ppname, dst, dname):
        c0 = t5 * 512
        P.add("vector", lambda e: e.tensor_mul(
            rtmp[0][:], pp[0:32, :], coss[:, :]),
            reads=[ppname, "coss"], writes=["rtmp0"])
        P.add("vector", lambda e: e.tensor_mul(
            rtmp[1][:], pp[32:64, :], sins[:, :]),
            reads=[ppname, "sins"], writes=["rtmp1"])
        P.add("vector", lambda e: e.tensor_mul(
            rtmp[2][:], pp[0:32, :], sins[:, :]),
            reads=[ppname, "sins"], writes=["rtmp2"])
        P.add("vector", lambda e: e.tensor_mul(
            rtmp[3][:], pp[32:64, :], coss[:, :]),
            reads=[ppname, "coss"], writes=["rtmp3"])
        P.add("vector", lambda e: e.tensor_sub(
            dst[0:32, c0:c0 + 512], rtmp[0][:], rtmp[1][:]),
            reads=["rtmp0", "rtmp1"], writes=[dname])
        P.add("vector", lambda e: e.tensor_add(
            dst[32:64, c0:c0 + 512], rtmp[2][:], rtmp[3][:]),
            reads=["rtmp2", "rtmp3"], writes=[dname])
        P.add("vector", lambda e: e.tensor_copy(
            dst[64:128, c0:c0 + 512], pp[64:128, :]),
            reads=[ppname], writes=[dname])

    NKT = T // 512
    for h in range(4):
        for t5 in range(16):
            t0 = t5 * 512
            P.dma("sync", coss[:], cosd[:, t0:t0 + 512], writes=["coss"])
            P.dma("sync", sins[:], sind[:, t0:t0 + 512], writes=["sins"])
            for dc in range(8):
                P.dma("sync", xnb[:, dc * 512:(dc + 1) * 512],
                      xn_src(dc, t0, 512),
                      writes=["xnb"])
            for wset, wname, pp, ppname in ((wks, "wks", psQ, "psQ"),):
                for dc in range(8):
                    P.add("tensor", lambda e, h=h, dc=dc, wset=wset, pp=pp:
                          e.matmul(pp[:],
                                   wset[:, h * 1024 + dc * 128:
                                        h * 1024 + (dc + 1) * 128],
                                   xnb[:, dc * 512:(dc + 1) * 512],
                                   start=(dc == 0), stop=(dc == 7)),
                          reads=[wname, "xnb"], writes=[ppname])
                rope_evac(h, t5, pp, ppname, KT, "KT")
            for dc in range(8):
                P.add("tensor", lambda e, h=h, dc=dc: e.matmul(
                    psQ[:], wqs[:, h * 1024 + dc * 128:
                                h * 1024 + (dc + 1) * 128],
                    xnb[:, dc * 512:(dc + 1) * 512],
                    start=(dc == 0), stop=(dc == 7)),
                    reads=["wqs", "xnb"], writes=["psQ"])
            rope_evac(h, t5, psQ, "psQ", QT, "QT")
            for tt in range(4):
                for dc in range(8):
                    P.add("tensor", lambda e, h=h, dc=dc, tt=tt: e.matmul(
                        psAux[:, 0:128],
                        xnb[:, dc * 512 + tt * 128: dc * 512 + tt * 128 + 128],
                        wvs[:, h * 1024 + dc * 128: h * 1024 + (dc + 1) * 128],
                        start=(dc == 0), stop=(dc == 7)),
                        reads=["wvs", "xnb"], writes=["psAux"])
                P.add("vector", lambda e, t5=t5, tt=tt: e.tensor_copy(
                    VSB[:, (t5 * 4 + tt) * 128: (t5 * 4 + tt) * 128 + 128],
                    psAux[:, 0:128]),
                    reads=["psAux"], writes=["VSB"])

        for qt in range(T // 128):
            ktmax = qt // 4
            nkt = ktmax + 1
            qq = qt % 2
            lacq = lac[qq]
            psPV, psPVn = psPV_[qq], f"psPV{qq}"
            Ss, Ssn = Sstr[qq], f"Sstr{qq}"
            Ps, Psn = Pstr[0], "Pstr0"
            SB4 = [(psS, "psS"), (psQ, "psQ"), (psS2, "psS2")]
            for kt in range(nkt):
                pS, pn = SB4[kt % 3]
                P.add("tensor", lambda e, qt=qt, kt=kt, pS=pS: e.matmul(
                    pS[:], QT[:, qt * 128:(qt + 1) * 128],
                    KT[:, kt * 512:(kt + 1) * 512], start=True, stop=True),
                    reads=["QT", "KT"], writes=[pn])
                if kt == ktmax:
                    v = qt % 4
                    P.add("vector", lambda e, v=v, pS=pS, kt=kt, Ss=Ss:
                          e.tensor_add(Ss[:, kt * 512:(kt + 1) * 512],
                                       pS[:], masks[:, v * 512:(v + 1) * 512]),
                          reads=[pn, "masks"], writes=[Ssn])
                elif kt % 2 == 0:
                    P.add("vector", lambda e, pS=pS, kt=kt, Ss=Ss:
                          e.tensor_copy(Ss[:, kt * 512:(kt + 1) * 512], pS[:]),
                          reads=[pn], writes=[Ssn])
                else:
                    P.add("scalar", lambda e, pS=pS, kt=kt, Ss=Ss:
                          e.activation(Ss[:, kt * 512:(kt + 1) * 512], pS[:],
                                       AF.Copy),
                          reads=[pn], writes=[Ssn])
            P.add("vector", lambda e, Ss=Ss, nkt=nkt, lacq=lacq: e.reduce_max(
                lacq[:, 1:2], Ss[:, 0:nkt * 512], axis=AX.X, negate=True),
                reads=[Ssn], writes=[f"lac{qq}@nm"])
            P.add("vector", lambda e, lacq=lacq: e.tensor_scalar(
                lacq[:, 2:3], lacq[:, 1:2], cscs[:, 0:1], None, ALU.mult),
                reads=[f"lac{qq}@nm", "cscs"], writes=[f"lac{qq}@bias"])
            P.add("scalar", lambda e, Ss=Ss, Ps=Ps, nkt=nkt, lacq=lacq:
                  e.activation(Ps[:, 0:nkt * 512], Ss[:, 0:nkt * 512], AF.Exp,
                               scale=cscs[:, 0:1], bias=lacq[:, 2:3],
                               accum_out=lacq[:, 0:1]),
                  reads=[Ssn, "cscs", f"lac{qq}@bias"],
                  writes=[Psn, f"lac{qq}@l"])
            for k0 in range(0, nkt, 2):
                pq2 = (k0 // 2) % 2
                kts = [kt for kt in (k0, k0 + 1) if kt < nkt]
                for kt in kts:
                    half = (kt % 2) * 512
                    for j in range(4):
                        P.add("tensor", lambda e, j=j, kt=kt, pq2=pq2,
                              half=half, Ps=Ps: e.transpose(
                                  psTb[pq2][:, half + j * 128:
                                             half + (j + 1) * 128],
                                  Ps[:, kt * 512 + j * 128:
                                     kt * 512 + (j + 1) * 128],
                                  identB[:]),
                              reads=[Psn, "identB"], writes=[f"psTb{pq2}"])
                w = len(kts) * 512
                P.add("vector", lambda e, pq2=pq2, w=w: e.tensor_copy(
                    PTs[pq2][:, 0:w], psTb[pq2][:, 0:w]),
                    reads=[f"psTb{pq2}"], writes=[f"PTs{pq2}"])
                for kt in kts:
                    half = (kt % 2) * 512
                    for j in range(4):
                        P.add("tensor", lambda e, qt=qt, kt=kt, j=j, pq2=pq2,
                              half=half, psPV=psPV:
                              e.matmul(
                            psPV[:, 0:128],
                            VSB[:, (kt * 4 + j) * 128:
                                (kt * 4 + j) * 128 + 128],
                            PTs[pq2][:, half + j * 128: half + (j + 1) * 128],
                            start=(kt == 0 and j == 0),
                            stop=(kt == qt // 4 and j == 3)),
                            reads=["VSB", f"PTs{pq2}"], writes=[psPVn])
            P.add("vector", lambda e, lacq=lacq: e.reciprocal(
                lacq[:, 3:4], lacq[:, 0:1]),
                reads=[f"lac{qq}@l"], writes=[f"lac{qq}@linv"])
            P.add("tensor", lambda e, lacq=lacq: e.transpose(
                psAux[0:1, 0:128], lacq[:, 3:4], identF[:]),
                reads=[f"lac{qq}@linv", "identF"], writes=["psAux"])
            P.add("vector", lambda e: e.tensor_copy(lrs[:], psAux[0:1, 0:128]),
                  reads=["psAux"], writes=["lrs"])
            P.add("tensor", lambda e: e.matmul(
                psAux[:, 0:128], ones1[:], lrs[:], start=True, stop=True),
                reads=["ones1", "lrs"], writes=["psAux"])
            P.add("vector", lambda e: e.tensor_copy(lpls[:], psAux[:, 0:128]),
                  reads=["psAux"], writes=["lpls"])
            og = (qt // 4) % 2
            P.add("vector", lambda e, qt=qt, og=og, psPV=psPV: e.tensor_mul(
                oT4[og][:, (qt % 4) * 128: (qt % 4) * 128 + 128],
                psPV[:, 0:128], lpls[:]),
                reads=[psPVn, "lpls"], writes=[f"oT4_{og}@{qt % 4}"])
            if qt % 4 == 3:
                P.dma("sync", oscr[h, :, (qt - 3) * 128:(qt + 1) * 128],
                      oT4[og][:],
                      reads=[f"oT4_{og}"], writes=["oscr"])

    oTk = [sb(f"oTk{i}", [128, 4 * 512], BF16) for i in range(2)]
    for t5 in range(16):
        t0 = t5 * 512
        tq5 = t5 % 2
        for kc in range(4):
            P.dma("sync", oTk[tq5][:, kc * 512:(kc + 1) * 512],
                  oscr[kc, :, t0:t0 + 512], writes=[f"oTk{tq5}"])
        for do in range(8):
            pp = psS if do % 2 == 0 else psQ
            ppn = "psS" if do % 2 == 0 else "psQ"
            for kc in range(4):
                P.add("tensor", lambda e, do=do, kc=kc, t0=t0, pp=pp, tq5=tq5:
                      e.matmul(pp[:],
                               wos[:, do * 512 + kc * 128:
                                   do * 512 + (kc + 1) * 128],
                               oTk[tq5][:, kc * 512:(kc + 1) * 512],
                               start=(kc == 0), stop=(kc == 3)),
                      reads=["wos", f"oTk{tq5}"], writes=[ppn])
            P.add("vector", lambda e, pp=pp, ppn=ppn: e.tensor_copy(
                oev[:], pp[:]), reads=[ppn], writes=["oev"])
            P.dma("sync", part_ap(do, t0),
                  oev[:], reads=["oev"], writes=["ap_o"])


def stage_attn_inputs(inputs, cid):
    b, p = cid >> 1, cid & 1
    hs = slice(p * 4, (p + 1) * 4)
    wqt = np.stack([tile_lhsT(np.asarray(inputs["L1_attn_wq"])[h]).reshape(128, 1024)
                    for h in range(p * 4, p * 4 + 4)])
    wkt = np.stack([tile_lhsT(np.asarray(inputs["L1_attn_wk"])[h]).reshape(128, 1024)
                    for h in range(p * 4, p * 4 + 4)])
    wvt = np.stack([tile_lhsT(np.asarray(inputs["L1_attn_wv"])[h]).reshape(128, 1024)
                    for h in range(p * 4, p * 4 + 4)])
    wot = tile_lhsT(np.asarray(inputs["L1_attn_wo"])[p * 512:(p + 1) * 512])
    rad = np.arange(T)[None, :] * (10000.0 ** (-np.arange(32) / 32.0))[:, None]
    mask = np.zeros((128, 4, 512), np.float32)
    for v in range(4):
        i = np.arange(128)[:, None]
        j = np.arange(512)[None, :]
        mask[:, v, :] = np.where(v * 128 + i >= j, 0.0, -1e30)
    return {
        "xna": None,
        "wq_t": wqt, "wk_t": wkt, "wv_t": wvt, "wo_t": wot,
        "cos_t": np.cos(rad).astype(np.float32),
        "sin_t": np.sin(rad).astype(np.float32),
        "mask_t": mask.reshape(128, 2048).astype(BF),
        "csc_t": np.full((128, 1), 1.0 / 128.0, np.float32),
    }




def build_moe():
    nc = bass.Bass("TRN2", target_bir_lowering=False, debug=False)
    P = Prog(nc)
    xnm = nc.dram_tensor("xnm", [D, T], BF16, kind="ExternalInput")
    mp_o = nc.dram_tensor("mpart", [2, D, T // 2], BF16, kind="ExternalOutput")
    io = {
        "gw_t": nc.dram_tensor("gw_t", [128, 32], F32, kind="ExternalInput"),
        "sel_t": nc.dram_tensor("sel_t", [4, 256], F32, kind="ExternalInput"),
        "w1_t": nc.dram_tensor("w1_t", [2, 44, 128, 1024], F32, kind="ExternalInput"),
        "w3_t": nc.dram_tensor("w3_t", [2, 44, 128, 1024], F32, kind="ExternalInput"),
        "w2_t": nc.dram_tensor("w2_t", [2, 8, 128, 5632], F32, kind="ExternalInput"),
        "xn_src": lambda dc, t0, w: xnm[dc * 128:(dc + 1) * 128, t0:t0 + w],
        "part_ap": lambda do, t0: mp_o[t0 // 4096, do * 128:(do + 1) * 128,
                                       t0 % 4096: t0 % 4096 + 512],
    }
    emit_moe(P, nc, nc.alloc_sbuf_tensor, nc.alloc_psum_tensor, io)
    P.emit()
    return nc





def build_ffn1():
    nc = bass.Bass("TRN2", target_bir_lowering=False, debug=False)
    P = Prog(nc)
    xnf = nc.dram_tensor("xnf", [D, T], BF16, kind="ExternalInput")
    fp_o = nc.dram_tensor("fpart", [2, D, T // 2], BF16, kind="ExternalOutput")
    io = {
        "wg_t": nc.dram_tensor("wg_t", [12, 128, 1024], I32, kind="ExternalInput"),
        "wu_t": nc.dram_tensor("wu_t", [12, 128, 1024], I32, kind="ExternalInput"),
        "wdf_t": nc.dram_tensor("wd_t", [8, 128, 1536], I32, kind="ExternalInput"),
        "c127_t": nc.dram_tensor("c127_t", [128, 1], F32, kind="ExternalInput"),
        "xn_src": lambda dc, t0, w: xnf[dc * 128:(dc + 1) * 128, t0:t0 + w],
        "part_ap": lambda do, t0: fp_o[t0 // 4096, do * 128:(do + 1) * 128,
                                       t0 % 4096: t0 % 4096 + 512],
    }
    emit_ffn1(P, nc, nc.alloc_sbuf_tensor, nc.alloc_psum_tensor, io)
    P.emit()
    return nc


def emit_ffn1(P, nc, sbp, psp, io):
    ldq = io.get("ldq", "gpsimd")
    sb, ps = sbp, psp
    wg, wu, wdn, c127 = io["wg_t"], io["wu_t"], io["wdf_t"], io["c127_t"]
    xn_src = io["xn_src"]
    part_ap = io["part_ap"]

    wgs = sb("wgs", [128, 12 * 1024], BF16)
    wus = sb("wus", [128, 12 * 1024], BF16)
    wds = sb("wds", [128, 8 * 1536], BF16)
    c127s = sb("c127s", [128, 1], F32)
    xnb = sb("xnb", [128, 8 * 512], BF16)
    sil = sb("sil", [128, 512], F32)
    hstr = sb("hstr", [128, 12 * 512], BF16)
    pevac = sb("pevac", [128, 512], BF16)
    ps_ = [ps(f"ps{i}", [128, 512], F32) for i in range(6)]

    for i in range(12):
        P.dma(ldq, wgs[:, i * 1024:(i + 1) * 1024], wg[i, :, :],
              writes=["wgs"])
        P.dma(ldq, wus[:, i * 1024:(i + 1) * 1024], wu[i, :, :],
              writes=["wus"])
    for i in range(8):
        P.dma(ldq, wds[:, i * 1536:(i + 1) * 1536], wdn[i, :, :],
              writes=["wds"])
    P.dma("sync", c127s[:], c127[:, :], writes=["c127s"])

    for t5 in range(16):
        t0 = t5 * 512
        for dc in range(8):
            P.dma("sync", xnb[:, dc * 512:(dc + 1) * 512],
                  xn_src(dc, t0, 512), writes=["xnb"])
        for ht in range(12):
            pg, pu = ps_[ht % 2], ps_[2 + ht % 2]
            for dc in range(8):
                P.add("tensor", lambda e, ht=ht, dc=dc, pg=pg: e.matmul(
                    pg[:], wgs[:, ht * 1024 + dc * 128:
                               ht * 1024 + (dc + 1) * 128],
                    xnb[:, dc * 512:(dc + 1) * 512],
                    start=(dc == 0), stop=(dc == 7)),
                    reads=["wgs", "xnb"], writes=[f"ps{ht % 2}"])
                P.add("tensor", lambda e, ht=ht, dc=dc, pu=pu: e.matmul(
                    pu[:], wus[:, ht * 1024 + dc * 128:
                               ht * 1024 + (dc + 1) * 128],
                    xnb[:, dc * 512:(dc + 1) * 512],
                    start=(dc == 0), stop=(dc == 7)),
                    reads=["wus", "xnb"], writes=[f"ps{2 + ht % 2}"])
            P.add("scalar", lambda e, ht=ht, pg=pg: e.activation(
                sil[:], pg[:], AF.Silu, scale=c127s[:, 0:1]),
                reads=[f"ps{ht % 2}", "c127s"], writes=["sil"])
            P.add("vector", lambda e, ht=ht, pu=pu: e.scalar_tensor_tensor(
                hstr[:, ht * 512:(ht + 1) * 512], pu[:], c127s[:, 0:1],
                sil[:], ALU.mult, ALU.mult),
                reads=[f"ps{2 + ht % 2}", "c127s", "sil"], writes=["hstr"])
        for do in range(8):
            pp = ps_[4 + do % 2]
            for hc in range(12):
                P.add("tensor", lambda e, do=do, hc=hc, pp=pp: e.matmul(
                    pp[:], wds[:, do * 1536 + hc * 128:
                               do * 1536 + (hc + 1) * 128],
                    hstr[:, hc * 512:(hc + 1) * 512],
                    start=(hc == 0), stop=(hc == 11)),
                    reads=["wds", "hstr"], writes=[f"ps{4 + do % 2}"])
            P.add("vector", lambda e, pp=pp: e.tensor_scalar(
                pevac[:], pp[:], c127s[:, 0:1], None, ALU.mult),
                reads=[f"ps{4 + do % 2}", "c127s"], writes=["pevac"])
            P.dma("sync", part_ap(do, t0),
                  pevac[:], reads=["pevac"], writes=["fp_o"])


def stage_ffn1_inputs(inputs, cid):
    b, p = cid >> 1, cid & 1
    own = slice(p * 1536, (p + 1) * 1536)
    wgT = np.ascontiguousarray(np.asarray(inputs["L1_ffn_w_gate"]).T[:, own])
    wuT = np.ascontiguousarray(np.asarray(inputs["L1_ffn_w_up"]).T[:, own])
    wdT = np.ascontiguousarray(np.asarray(inputs["L1_ffn_w_down"]).T[own, :])
    return {
        "wg_t": tile_lhsT(wgT), "wu_t": tile_lhsT(wuT),
        "wd_t": tile_lhsT(wdT),
        "c127_t": np.full((128, 1), 1.0 / 127.0, np.float32),
    }


def stage_moe_inputs(inputs, cid):
    b, p = cid >> 1, cid & 1
    gw4 = np.asarray(inputs["L0_ffn_gate_w"])[0:4]
    gwt = np.ascontiguousarray(
        gw4.T.reshape(8, 128, 4).transpose(1, 0, 2).reshape(128, 32))
    sel = np.zeros((4, 256), np.float32)
    sel[p * 2, 0:128] = 1.0
    sel[p * 2 + 1, 128:256] = 1.0
    w1 = np.asarray(inputs["L0_ffn_w1"])
    w3 = np.asarray(inputs["L0_ffn_w3"])
    w2 = np.asarray(inputs["L0_ffn_w2"])
    w1t = np.stack([tile_lhsT(w1[p * 2 + i]) for i in range(2)])
    w3t = np.stack([tile_lhsT(w3[p * 2 + i]) for i in range(2)])
    w2t = np.stack([tile_lhsT(w2[p * 2 + i]) for i in range(2)])
    return {"gw_t": gwt, "sel_t": sel, "w1_t": w1t, "w3_t": w3t, "w2_t": w2t}


def tile_lhsT(w, blk_out=128):
    Din, Dout = w.shape
    dcn, ocn = Din // 128, Dout // blk_out
    return np.ascontiguousarray(
        w.reshape(dcn, 128, ocn, blk_out).transpose(2, 1, 0, 3)
        .reshape(ocn, 128, dcn * blk_out))


def vec_t(v):
    return np.ascontiguousarray(v.reshape(-1, 128).T)


def stage_s1_inputs(inputs, cid):
    b, p = cid >> 1, cid & 1
    x = inputs["L0_attn_x"][b]
    W_in = np.asarray(inputs["L0_attn_W_in"])
    cols = np.r_[p * 1024:(p + 1) * 1024, 2048 + p * 1024: 2048 + (p + 1) * 1024]
    w_half = W_in[:, cols]
    cwo = np.asarray(inputs["L0_attn_conv_w"])[p * 1024:(p + 1) * 1024, 0, :]
    return {
        "x": np.ascontiguousarray(x),
        "w_in_t": tile_lhsT(w_half),
        "lg_t": vec_t(np.asarray(inputs["L0_attn_ln_g"])),
        "lb_t": vec_t(np.asarray(inputs["L0_attn_ln_b"])),
        "cw_t": np.ascontiguousarray(
            cwo.reshape(8, 128, 4).transpose(1, 0, 2).reshape(128, 32)),
        "cb_t": vec_t(np.asarray(inputs["L0_attn_conv_b"])[p * 1024:(p + 1) * 1024]),
    }


PAIRS = [[0, 1], [2, 3], [4, 5], [6, 7]]
_FUSED = True


def build_fused():
    from contextlib import ExitStack
    nc = bass.Bass("TRN2", target_bir_lowering=False, debug=False)
    P = Prog(nc)
    dt = nc.dram_tensor
    TO = T // 2

    ext = {
        "x": dt("x", [T, D], F32, kind="ExternalInput"),
        "rp0": dt("rp0", [TO, D], F32, kind="ExternalInput"),
        "w_in_t": dt("w_in_t", [16, 128, 1024], F32, kind="ExternalInput"),
        "lg_t": dt("lg_t", [128, 8], F32, kind="ExternalInput"),
        "lb_t": dt("lb_t", [128, 8], F32, kind="ExternalInput"),
        "cw_t": dt("cw_t", [128, 32], F32, kind="ExternalInput"),
        "cb_t": dt("cb_t", [128, 8], F32, kind="ExternalInput"),
        "wd_t": dt("wd_t", [8, 128, 2048], F32, kind="ExternalInput"),
        "bd_t": dt("bd_t", [128, 8], F32, kind="ExternalInput"),
        "wb_t": dt("wb_t", [128, 256], F32, kind="ExternalInput"),
        "wc_t": dt("wc_t", [128, 256], F32, kind="ExternalInput"),
        "la_t": dt("la_t", [128, 128], F32, kind="ExternalInput"),
        "dp_t": dt("dp_t", [128, 8], F32, kind="ExternalInput"),
        "wo_t": dt("wo_t", [8, 128, 1024], F32, kind="ExternalInput"),
        "gw_t": dt("gw_t", [128, 32], F32, kind="ExternalInput"),
        "sel_t": dt("sel_t", [4, 256], F32, kind="ExternalInput"),
        "w1_t": dt("w1_t", [2, 44, 128, 1024], F32, kind="ExternalInput"),
        "w3_t": dt("w3_t", [2, 44, 128, 1024], F32, kind="ExternalInput"),
        "w2_t": dt("w2_t", [2, 8, 128, 5632], F32, kind="ExternalInput"),
        "wq_t": dt("wq_t", [4, 128, 1024], F32, kind="ExternalInput"),
        "wk_t": dt("wk_t", [4, 128, 1024], F32, kind="ExternalInput"),
        "wv_t": dt("wv_t", [4, 128, 1024], F32, kind="ExternalInput"),
        "woA_t": dt("woA_t", [8, 128, 512], F32, kind="ExternalInput"),
        "cos_t": dt("cos_t", [32, T], F32, kind="ExternalInput"),
        "sin_t": dt("sin_t", [32, T], F32, kind="ExternalInput"),
        "mask_t": dt("mask_t", [128, 4 * 512], BF16, kind="ExternalInput"),
        "csc_t": dt("csc_t", [128, 1], F32, kind="ExternalInput"),
        "wg_t": dt("wg_t", [12, 128, 1024], I32, kind="ExternalInput"),
        "wu_t": dt("wu_t", [12, 128, 1024], I32, kind="ExternalInput"),
        "wdf_t": dt("wdf_t", [8, 128, 1536], I32, kind="ExternalInput"),
        "c127_t": dt("c127_t", [128, 1], F32, kind="ExternalInput"),
    }
    xb_own = dt("xb_own", [DI // 2, T], BF16, kind="Internal")
    sz_own = dt("sz_own", [DI // 2, T], BF16, kind="Internal")
    xn0T = dt("xn0T", [D, T], BF16, kind="Internal")
    ag_xb = dt("ag_xb", [DI, T], BF16, kind="Internal")
    rs_in = dt("rs_in", [2 * D, TO], BF16, kind="Internal")
    rs_out = dt("rs_out", [D, TO], BF16, kind="Internal")
    ag_xn_in = dt("ag_xn_in", [D, TO], BF16, kind="Internal")
    ag_xn_out = dt("ag_xn_out", [2 * D, TO], BF16, kind="Internal")
    r1 = dt("r1", [TO, D], F32, kind="Internal")
    r2 = dt("r2", [TO, D], F32, kind="Internal")
    r3 = dt("r3", [TO, D], F32, kind="Internal")
    bscr = dt("bscr", [2, 16, 512], BF16, kind="Internal")
    oscr = dt("oscr", [4, 128, T], BF16, kind="Internal")
    rn_o = dt("rn", [TO, D], F32, kind="ExternalOutput")

    def part_ap(do, t0):
        return rs_in[(t0 // 4096) * D + do * 128:
                     (t0 // 4096) * D + (do + 1) * 128,
                     t0 % 4096: t0 % 4096 + 512]

    def xn_src(dc, t0, w):
        r0 = (t0 // 4096) * D + dc * 128
        c0 = t0 % 4096
        return ag_xn_out[r0:r0 + 128, c0:c0 + w]

    def xn_ap(dc, t0):
        return ag_xn_in[dc * 128:(dc + 1) * 128, t0:t0 + 128]

    stage_n = [0]

    def stage(emit_fn, io, name=None):
        pre = f"s{stage_n[0]}_"
        stage_n[0] += 1
        es = ExitStack()
        sbp = lambda n, sh, dty: es.enter_context(
            nc.sbuf_tensor(pre + n, sh, dty))
        psp = lambda n, sh, dty: es.enter_context(
            nc.psum_tensor(pre + n, sh, dty))
        with nc.named_scope(name or pre):
            emit_fn(P, nc, sbp, psp, io)
        es.close()
        P.barrier()

    def cc(kind, op, ins, outs):
        P.add("gpsimd", lambda e: e.collective_compute(
            kind, op, PAIRS, ins, outs), kind="cc")
        P.barrier()

    twin = {
        "wd_t": dt("wd_b", [8, 128, 2048], BF16, kind="Internal"),
        "wo_t": dt("wo_b", [8, 128, 1024], BF16, kind="Internal"),
        "wb_t": dt("wb_b", [128, 256], BF16, kind="Internal"),
        "wc_t": dt("wc_b", [128, 256], BF16, kind="Internal"),
        "w1_t": dt("w1_b", [2, 44, 128, 1024], BF16, kind="Internal"),
        "w3_t": dt("w3_b", [2, 44, 128, 1024], BF16, kind="Internal"),
        "w2_t": dt("w2_b", [2, 8, 128, 5632], BF16, kind="Internal"),
        "wq_t": dt("wq_b", [4, 128, 1024], BF16, kind="Internal"),
        "wk_t": dt("wk_b", [4, 128, 1024], BF16, kind="Internal"),
        "wv_t": dt("wv_b", [4, 128, 1024], BF16, kind="Internal"),
        "woA_t": dt("woA_b", [8, 128, 512], BF16, kind="Internal"),
        "wg_t": dt("wg_b", [12, 128, 1024], BF16, kind="Internal"),
        "wu_t": dt("wu_b", [12, 128, 1024], BF16, kind="Internal"),
        "wdf_t": dt("wdf_b", [8, 128, 1536], BF16, kind="Internal"),
    }

    stage(emit_s1, {"x": ext["x"], "w_in_t": ext["w_in_t"],
                    "lg_t": ext["lg_t"], "lb_t": ext["lb_t"],
                    "cw_t": ext["cw_t"], "cb_t": ext["cb_t"],
                    "xb": xb_own, "sz": sz_own, "xn0T": xn0T}, "S1")
    def conv(dst, srct, slices):
        for sl in slices:
            P.dma("gpsimd", dst[sl], srct[sl], writes=["twin"])
    conv(twin["wd_t"], ext["wd_t"], [(i,) for i in range(8)])
    conv(twin["wo_t"], ext["wo_t"], [(i,) for i in range(8)])
    conv(twin["wb_t"], ext["wb_t"], [slice(None)])
    conv(twin["wc_t"], ext["wc_t"], [slice(None)])
    for le in range(2):
        for a in range(4):
            conv(twin["w1_t"], ext["w1_t"], [(le, slice(a * 11, a * 11 + 11))])
            conv(twin["w3_t"], ext["w3_t"], [(le, slice(a * 11, a * 11 + 11))])
        for do_ in range(8):
            conv(twin["w2_t"], ext["w2_t"], [(le, do_)])
    for k in ("wq_t", "wk_t", "wv_t", "woA_t", "wg_t", "wu_t", "wdf_t"):
        conv(twin[k], ext[k], [(i,) for i in range(ext[k].shape[0])])
    cc("AllGather", ALU.bypass, [xb_own[:, :]], [ag_xb[:, :]])
    stage(emit_scan, {"xbF": ag_xb, "xbO": xb_own, "szO": sz_own,
                      "xn0T": xn0T, "wd_t": twin["wd_t"], "bd_t": ext["bd_t"],
                      "wb_t": twin["wb_t"], "wc_t": twin["wc_t"],
                      "la_t": ext["la_t"], "dp_t": ext["dp_t"],
                      "wo_t": twin["wo_t"], "bscr": bscr,
                      "part_ap": part_ap, "ldq": "sync"}, "SCAN")
    cc("ReduceScatter", ALU.add, [rs_in[:, :]], [rs_out[:, :]])
    stage(lambda P_, nc_, sbp, psp, io: emit_combine(
        P_, nc_, sbp, psp, io, True, True),
        {"psum": rs_out, "rp": ext["rp0"], "rn": r1, "xn_ap": xn_ap}, "C1")
    cc("AllGather", ALU.bypass, [ag_xn_in[:, :]], [ag_xn_out[:, :]])
    stage(emit_moe, {"gw_t": ext["gw_t"], "sel_t": ext["sel_t"],
                     "w1_t": twin["w1_t"], "w3_t": twin["w3_t"],
                     "w2_t": twin["w2_t"], "xn_src": xn_src,
                     "part_ap": part_ap, "ldq": "sync"}, "MOE")
    cc("ReduceScatter", ALU.add, [rs_in[:, :]], [rs_out[:, :]])
    stage(lambda P_, nc_, sbp, psp, io: emit_combine(
        P_, nc_, sbp, psp, io, True, True),
        {"psum": rs_out, "rp": r1, "rn": r2, "xn_ap": xn_ap}, "C2")
    cc("AllGather", ALU.bypass, [ag_xn_in[:, :]], [ag_xn_out[:, :]])
    stage(emit_attn, {"wq_t": twin["wq_t"], "wk_t": twin["wk_t"],
                      "wv_t": twin["wv_t"], "wo_t": twin["woA_t"],
                      "cos_t": ext["cos_t"], "sin_t": ext["sin_t"],
                      "mask_t": ext["mask_t"], "csc_t": ext["csc_t"],
                      "xn_src": xn_src, "part_ap": part_ap,
                      "oscr": oscr, "ldq": "sync"}, "ATTN")
    cc("ReduceScatter", ALU.add, [rs_in[:, :]], [rs_out[:, :]])
    stage(lambda P_, nc_, sbp, psp, io: emit_combine(
        P_, nc_, sbp, psp, io, True, True),
        {"psum": rs_out, "rp": r2, "rn": r3, "xn_ap": xn_ap}, "C3")
    cc("AllGather", ALU.bypass, [ag_xn_in[:, :]], [ag_xn_out[:, :]])
    stage(emit_ffn1, {"wg_t": twin["wg_t"], "wu_t": twin["wu_t"],
                      "wdf_t": twin["wdf_t"], "c127_t": ext["c127_t"],
                      "xn_src": xn_src, "part_ap": part_ap,
                      "ldq": "sync"}, "FFN")
    cc("ReduceScatter", ALU.add, [rs_in[:, :]], [rs_out[:, :]])
    stage(lambda P_, nc_, sbp, psp, io: emit_combine(
        P_, nc_, sbp, psp, io, False, True),
        {"psum": rs_out, "rp": r3, "rn": rn_o}, "C4")
    P.emit()
    return nc


def stage_fused_inputs(inputs, cid):
    b, p = cid >> 1, cid & 1
    own = slice(p * 4096, (p + 1) * 4096)
    m = {}
    m.update(stage_s1_inputs(inputs, cid))
    m.update(stage_scan_inputs(inputs, cid, None, None, None, None))
    for k in ("xbF", "xbO", "sz", "xn0"):
        m.pop(k, None)
    m.update(stage_moe_inputs(inputs, cid))
    att = stage_attn_inputs(inputs, cid)
    att.pop("xna", None)
    att["woA_t"] = att.pop("wo_t")
    m.update(att)
    ffn = stage_ffn1_inputs(inputs, cid)
    ffn["wdf_t"] = ffn.pop("wd_t")
    m.update(ffn)
    m["rp0"] = np.ascontiguousarray(np.asarray(inputs["L0_attn_x"])[b, own])
    return m


def kernel(**inputs):
    cores = list(range(8))
    if _FUSED:
        nc = build_fused()
        in_maps = [stage_fused_inputs(inputs, c) for c in cores]
        res = run_bass_kernel_spmd(nc, in_maps, core_ids=cores)
        out = np.zeros((4, 8192, 1024), np.float32)
        for c in cores:
            b, p = c >> 1, c & 1
            out[b, p * 4096:(p + 1) * 4096] = res.results[c]["rn"]
        return out
    return _kernel_unfused(inputs, cores)


def _kernel_unfused(inputs, cores):
    nc = build_s1()
    res = run_bass_kernel_spmd(nc, [stage_s1_inputs(inputs, c) for c in cores],
                               core_ids=cores)
    l1 = res.results

    nc = build_scan()
    in_maps = []
    for c in cores:
        ev, od = (c >> 1) * 2, (c >> 1) * 2 + 1
        xbf = np.ascontiguousarray(np.concatenate(
            [l1[ev]["xb"], l1[od]["xb"]], axis=0))
        in_maps.append(stage_scan_inputs(inputs, c, xbf, l1[c]["xb"],
                                         l1[c]["sz"], l1[c]["xn0"]))
    res = run_bass_kernel_spmd(nc, in_maps, core_ids=cores)
    l2 = res.results

    def combine(nc_c, parts, rps, want_xn):
        in_maps = []
        for c in cores:
            b, p = c >> 1, c & 1
            m = {"pa": np.ascontiguousarray(parts[2 * b][p]),
                 "pb": np.ascontiguousarray(parts[2 * b + 1][p]),
                 "rp": rps[c]}
            in_maps.append(m)
        return run_bass_kernel_spmd(nc_c, in_maps, core_ids=cores).results

    nc_cx = build_combine(True, False)
    nc_cl = build_combine(False, False)

    xs = np.asarray(inputs["L0_attn_x"])
    rp0 = [np.ascontiguousarray(xs[c >> 1, (c & 1) * 4096:(c & 1) * 4096 + 4096])
           for c in cores]
    l3 = combine(nc_cx, [l2[c]["part"] for c in cores], rp0, True)

    nc = build_moe()
    in_maps = []
    for c in cores:
        b = c >> 1
        m = stage_moe_inputs(inputs, c)
        m["xnm"] = np.ascontiguousarray(np.concatenate(
            [l3[2 * b]["xnx"], l3[2 * b + 1]["xnx"]], axis=1))
        in_maps.append(m)
    res = run_bass_kernel_spmd(nc, in_maps, core_ids=cores)
    l4 = res.results

    l5 = combine(nc_cx, [l4[c]["mpart"] for c in cores],
                 [l3[c]["rn"] for c in cores], True)

    nc = build_attn()
    in_maps = []
    for c in cores:
        b = c >> 1
        m = stage_attn_inputs(inputs, c)
        m["xna"] = np.ascontiguousarray(np.concatenate(
            [l5[2 * b]["xnx"], l5[2 * b + 1]["xnx"]], axis=1))
        in_maps.append(m)
    res = run_bass_kernel_spmd(nc, in_maps, core_ids=cores)
    l6 = res.results

    l7 = combine(nc_cx, [l6[c]["apart"] for c in cores],
                 [l5[c]["rn"] for c in cores], True)

    nc = build_ffn1()
    in_maps = []
    for c in cores:
        b = c >> 1
        m = stage_ffn1_inputs(inputs, c)
        m["xnf"] = np.ascontiguousarray(np.concatenate(
            [l7[2 * b]["xnx"], l7[2 * b + 1]["xnx"]], axis=1))
        in_maps.append(m)
    res = run_bass_kernel_spmd(nc, in_maps, core_ids=cores)
    l8 = res.results

    l9 = combine(nc_cl, [l8[c]["fpart"] for c in cores],
                 [l7[c]["rn"] for c in cores], False)

    out = np.zeros((4, 8192, 1024), np.float32)
    for c in cores:
        b, p = c >> 1, c & 1
        out[b, p * 4096:(p + 1) * 4096] = l9[c]["rn"]
    return out


# version 17
# speedup vs baseline: 1.1905x; 1.0279x over previous
import numpy as np
import ml_dtypes
import concourse.bass as bass
import concourse.mybir as mybir
from concourse.bass_utils import run_bass_kernel_spmd

F32 = mybir.dt.float32
BF16 = mybir.dt.bfloat16
I32 = mybir.dt.int32
AF = mybir.ActivationFunctionType
ALU = mybir.AluOpType
AX = mybir.AxisListType
BF = ml_dtypes.bfloat16

D = 1024
T = 8192
DI = 2048
NS = 16
HM = 5632
NE = 4
HF = 3072
NH = 8
DH = 128
EPS = 1e-5

ENGINES = ("sync", "tensor", "vector", "scalar", "gpsimd")


class _Op:
    __slots__ = ("engine", "fn", "reads", "writes", "kind", "idx", "waits",
                 "inc_needed", "pre_wait")

    def __init__(self, engine, fn, reads, writes, kind, idx):
        self.engine = engine
        self.fn = fn
        self.reads = list(reads)
        self.writes = list(writes)
        self.kind = kind
        self.idx = idx
        self.waits = []
        self.inc_needed = kind != "c"
        self.pre_wait = None


def _split(res):
    if "@" in res:
        name, tag = res.split("@", 1)
        return name, tag
    return res, "*"


class Prog:

    def __init__(self, nc):
        self.nc = nc
        self.ops = []
        self.seq = {}
        self.state = {}
        self.bar_waits = []
        self.dirty = {}

    def barrier(self):
        last = {}
        for op in self.ops:
            last[(op.engine, op.kind)] = op
        self.bar_waits = []
        for (e, k), op in last.items():
            op.inc_needed = True
            if k in ("d", "cc"):
                self.bar_waits.append((e, k, op.idx, self.seq[(e, k)]))
                self.dirty[(e, k)] = True
            else:
                self.bar_waits.append((e, k, op.idx, None))

    def add(self, engine, fn, reads=(), writes=(), kind="c"):
        key = (engine, kind)
        idx = self.seq.get(key, 0)
        self.seq[key] = idx + 1
        op = _Op(engine, fn, reads, writes, kind, idx)
        deps = set()

        def lookup(res):
            name, tag_ = _split(res)
            st = self.state.setdefault(name, {})
            if tag_ == "*":
                return st, list(st.keys())
            keys = [tag_]
            if "*" in st:
                keys.append("*")
            return st, keys

        war = set()
        for res in op.reads:
            st, keys = lookup(res)
            for k in keys:
                ent = st.get(k)
                if ent and ent["w"] is not None:
                    deps.add(ent["w"])
        for res in op.writes:
            st, keys = lookup(res)
            for k in keys:
                ent = st.get(k)
                if ent:
                    if ent["w"] is not None:
                        war.add(ent["w"])
                    for r in ent["r"]:
                        war.add(r)
        for res in op.reads:
            name, tag_ = _split(res)
            ent = self.state.setdefault(name, {}).setdefault(
                tag_, {"w": None, "r": []})
            ent["r"].append(op)
        for res in op.writes:
            name, tag_ = _split(res)
            st = self.state.setdefault(name, {})
            if tag_ == "*":
                st.clear()
            ent = st.setdefault(tag_, {"w": None, "r": []})
            ent["w"] = op
            ent["r"] = []

        if op.kind in ("d", "cc"):
            kk = (engine, op.kind)
            if self.dirty.get(kk):
                op.pre_wait = idx
                self.dirty[kk] = False
        for dep in war:
            if dep in deps:
                continue
            if (dep.engine == op.engine and dep.kind == "c"
                    and op.kind == "c"):
                continue
            deps.add(dep)
        for dep in deps:
            if dep is op:
                continue
            if dep.engine == op.engine and dep.kind == "d" and op.kind == "d":
                continue
            if (dep.engine == "tensor" and op.engine == "tensor"
                    and dep.kind == "c" and op.kind == "c"):
                continue
            dep.inc_needed = True
            if dep.kind in ("d", "cc"):
                kk = (dep.engine, dep.kind)
                op.waits.append((dep.engine, dep.kind, dep.idx,
                                 self.seq[kk]))
                self.dirty[kk] = True
            else:
                op.waits.append((dep.engine, dep.kind, dep.idx, None))
        for bw in self.bar_waits:
            if not (bw[0] == op.engine and bw[1] == "c" == op.kind):
                op.waits.append(bw)
        self.ops.append(op)
        return op

    def dma(self, engine, out, in_, reads=(), writes=()):
        return self.add(engine, lambda e: e.dma_start(out=out, in_=in_),
                        reads=reads, writes=writes, kind="d")

    def emit(self):
        import bisect
        nc = self.nc
        targets = {}
        for op in self.ops:
            if op.inc_needed:
                targets.setdefault((op.engine, op.kind), []).append(op.idx)
        for v in targets.values():
            v.sort()
        sem_cms, sems = [], {}
        for key in targets:
            cm = nc.semaphore(f"s_{key[0]}_{key[1]}")
            sems[key] = cm.__enter__()
            sem_cms.append(cm)

        def sig(engine, kind, idx, override=None):
            if override is not None:
                step = 16 if kind == "d" else 1
                return sems[(engine, kind)], override * step
            lst = targets[(engine, kind)]
            pos = bisect.bisect_left(lst, idx)
            rank = pos + 1
            step = 16 if kind == "d" else 1
            return sems[(engine, kind)], rank * step

        per_engine = {e: [] for e in ENGINES}
        for op in self.ops:
            per_engine[op.engine].append(op)

        inc_step = {"c": 1, "d": 16, "cc": 1}
        blk_cm = nc.Block()
        block = blk_cm.__enter__()
        for engine in ENGINES:
            ops = per_engine[engine]
            if not ops:
                continue

            def make_section(ops):
                def section(eng):
                    hw = {}
                    my_tot = {}
                    for op in ops:
                        if op.pre_wait:
                            kk = (op.engine, op.kind)
                            if kk in sems:
                                step = 16 if op.kind == "d" else 1
                                val = step * op.pre_wait
                                if hw.get(sems[kk].name, 0) < val:
                                    eng.wait_ge(sems[kk], val)
                                    hw[sems[kk].name] = val
                        for dep in op.waits:
                            sem, val = sig(*dep)
                            if hw.get(sem.name, 0) < val:
                                eng.wait_ge(sem, val)
                                hw[sem.name] = val
                        inst = op.fn(eng)
                        if op.inc_needed:
                            sem = sems[(op.engine, op.kind)]
                            step = inc_step[op.kind]
                            if op.kind == "cc":
                                inst.then_inc(sem)
                            else:
                                inst.then_inc(sem, step)
                            k = (op.engine, op.kind)
                            my_tot[k] = my_tot.get(k, 0) + step
                    for (e, k), total in my_tot.items():
                        if k in ("d", "cc"):
                            eng.wait_ge(sems[(e, k)], total)
                return section

            getattr(block, engine)(make_section(ops))
        blk_cm.__exit__(None, None, None)
        for cm in sem_cms:
            cm.__exit__(None, None, None)



def emit_s1(P, nc, sbp, psp, io):
    ldq = io.get("ldq", "gpsimd")
    x, w_in = io["x"], io["w_in_t"]
    lg, lb, cw, cb = io["lg_t"], io["lb_t"], io["cw_t"], io["cb_t"]
    xb_o, sz_o, xn0_o = io["xb"], io["sz"], io["xn0T"]

    NTH = 4
    TH = T // NTH
    sb, ps = sbp, psp
    xt = [sb(f"xt{i}", [128, D], F32) for i in range(2)]
    xmu = [sb(f"xmu{i}", [128, D], F32) for i in range(2)]
    sqb = [sb(f"sqb{i}", [128, D], F32) for i in range(2)]
    xn1 = [sb(f"xn1{i}", [128, D], F32) for i in range(2)]
    xn0b = [sb(f"xn0b{i}", [128, D], BF16) for i in range(2)]
    xn2b = [sb(f"xn2b{i}", [128, D], BF16) for i in range(2)]
    st = [sb(f"st{i}", [128, 12], F32) for i in range(2)]
    lgs = sb("lgs", [128, 8], F32)
    lbs = sb("lbs", [128, 8], F32)
    cws = sb("cws", [128, 32], F32)
    cbs = sb("cbs", [128, 8], F32)
    identB = sb("identB", [128, 128], BF16)
    xnT = sb("xnT", [128, 8 * TH], BF16)
    x0T = sb("x0T", [128, 8 * TH], BF16)
    wt = [sb(f"wt{i}", [128, 1024], BF16) for i in range(2)]
    strip = sb("strip", [128, 3 + TH], F32)
    carry = sb("carry", [128, 8 * 3], F32)
    acc = [sb(f"acc{i}", [128, TH], F32) for i in range(2)]
    xbb = sb("xbb", [128, TH], BF16)
    zbb = sb("zbb", [128, TH], BF16)
    psT = [ps(f"psT{i}", [128, 128], BF16) for i in range(2)]
    psM = [ps(f"psM{i}", [128, 512], F32) for i in range(2)]

    for t_, name, src in ((lgs, "lgs", lg), (lbs, "lbs", lb),
                          (cws, "cws", cw), (cbs, "cbs", cb)):
        P.dma("sync", t_[:], src[:, :], reads=[], writes=[name])
    P.add("gpsimd", lambda e: e.memset(identB[:], 0.0), writes=["identB"])
    P.add("gpsimd", lambda e: e.affine_select(
        out=identB[:], in_=identB[:], compare_op=ALU.not_equal, fill=1.0,
        base=0, pattern=[[-1, 128]], channel_multiplier=1),
        reads=["identB"], writes=["identB"])
    P.add("gpsimd", lambda e: e.memset(carry[:], 0.0), writes=["carry"])

    def ln_plain(q, xin, xname, xout, oname):
        s, sn = st[q], f"st{q}"
        P.add("vector", lambda e: e.reduce_sum(s[:, 0:1], xin[:], axis=AX.X),
              reads=[xname], writes=[sn + "@a"])
        P.add("vector", lambda e: e.tensor_scalar(
            s[:, 1:2], s[:, 0:1], 1.0 / D, None, ALU.mult),
            reads=[sn + "@a"], writes=[sn + "@b"])
        P.add("vector", lambda e: e.tensor_scalar(
            xmu[q][:], xin[:], s[:, 1:2], None, ALU.subtract),
            reads=[xname, sn + "@b"], writes=[f"xmu{q}"])
        P.add("scalar", lambda e: e.activation(
            sqb[q][:], xmu[q][:], AF.Square, accum_out=s[:, 2:3]),
            reads=[f"xmu{q}"], writes=[f"sqb{q}", sn + "@c"])
        P.add("vector", lambda e: e.tensor_scalar(
            s[:, 5:6], s[:, 2:3], 1.0 / D, EPS, ALU.mult, ALU.add),
            reads=[sn + "@c"], writes=[sn + "@f"])
        P.add("scalar", lambda e: e.sqrt(s[:, 3:4], s[:, 5:6]),
              reads=[sn + "@f"], writes=[sn + "@d"])
        P.add("vector", lambda e: e.reciprocal(s[:, 4:5], s[:, 3:4]),
              reads=[sn + "@d"], writes=[sn + "@e"])
        P.add("vector", lambda e: e.tensor_scalar(
            xout[:], xmu[q][:], s[:, 4:5], None, ALU.mult),
            reads=[f"xmu{q}", sn + "@e"], writes=[oname])

    for th in range(NTH):
        for tt in range(TH // 128):
            q = tt % 2
            tg = th * TH + tt * 128
            P.dma("sync", xt[q][:], x[tg:tg + 128, :],
                  reads=[], writes=[f"xt{q}"])
            ln_plain(q, xt[q], f"xt{q}", xn1[q], f"xn1{q}")
            P.add("vector", lambda e, q=q: e.tensor_copy(
                xn0b[q][:], xn1[q][:]),
                reads=[f"xn1{q}"], writes=[f"xn0b{q}"])
            ln_plain(q, xn1[q], f"xn1{q}", xn2b[q], f"xn2b{q}")
            for dc in range(8):
                col = dc * TH + tt * 128
                P.add("tensor", lambda e, q=q, dc=dc: e.transpose(
                    psT[q][:], xn2b[q][:, dc * 128:(dc + 1) * 128],
                    identB[:]),
                    reads=[f"xn2b{q}", "identB"], writes=[f"psT{q}"])
                P.add("vector", lambda e, q=q, dc=dc, col=col: e.tensor_scalar(
                    xnT[:, col:col + 128], psT[q][:],
                    lgs[:, dc:dc + 1], lbs[:, dc:dc + 1], ALU.mult, ALU.add),
                    reads=[f"psT{q}", "lgs", "lbs"], writes=["xnT"])
                P.add("tensor", lambda e, q=q, dc=dc: e.transpose(
                    psT[q][:], xn0b[q][:, dc * 128:(dc + 1) * 128],
                    identB[:]),
                    reads=[f"xn0b{q}", "identB"], writes=[f"psT{q}"])
                P.add("vector", lambda e, q=q, col=col: e.tensor_copy(
                    x0T[:, col:col + 128], psT[q][:]),
                    reads=[f"psT{q}"], writes=["x0T"])
        for dc in range(8):
            P.dma("sync", xn0_o[dc * 128:(dc + 1) * 128,
                                th * TH:(th + 1) * TH],
                  x0T[:, dc * TH:(dc + 1) * TH],
                  reads=["x0T"], writes=["xn0_o"])

        for oc in range(16):
            q = oc % 2
            P.dma(ldq, wt[q][:], w_in[oc, :, :],
                  reads=[], writes=[f"wt{q}"])
            for t5 in range(TH // 512):
                pq = t5 % 2
                for dc in range(8):
                    P.add("tensor",
                          lambda e, q=q, pq=pq, dc=dc, t5=t5: e.matmul(
                              psM[pq][:], wt[q][:, dc * 128:(dc + 1) * 128],
                              xnT[:, dc * TH + t5 * 512: dc * TH + t5 * 512 + 512],
                              start=(dc == 0), stop=(dc == 7)),
                          reads=[f"wt{q}", "xnT"], writes=[f"psM{pq}"])
                if oc < 8:
                    P.add("vector", lambda e, pq=pq, t5=t5: e.tensor_copy(
                        strip[:, 3 + t5 * 512: 3 + t5 * 512 + 512],
                        psM[pq][:]),
                        reads=[f"psM{pq}"], writes=["strip"])
                else:
                    P.add("scalar", lambda e, pq=pq, t5=t5: e.activation(
                        zbb[:, t5 * 512: t5 * 512 + 512], psM[pq][:],
                        AF.Silu),
                        reads=[f"psM{pq}"], writes=["zbb"])
            if oc < 8:
                if th == 0:
                    P.add("gpsimd", lambda e: e.memset(strip[:, 0:3], 0.0),
                          reads=[], writes=["strip"])
                else:
                    P.add("vector", lambda e, oc=oc: e.tensor_copy(
                        strip[:, 0:3], carry[:, oc * 3: oc * 3 + 3]),
                        reads=["carry"], writes=["strip"])
                P.add("vector", lambda e, oc=oc: e.tensor_copy(
                    carry[:, oc * 3: oc * 3 + 3], strip[:, TH: TH + 3]),
                    reads=["strip"], writes=["carry"])
                P.add("vector", lambda e, oc=oc: e.tensor_scalar(
                    acc[0][:], strip[:, 0: TH], cws[:, oc * 4: oc * 4 + 1],
                    None, ALU.mult),
                    reads=["strip", "cws"], writes=["acc0"])
                cur = 0
                for j in range(1, 4):
                    nxt = 1 - cur
                    P.add("vector",
                          lambda e, oc=oc, j=j, cur=cur, nxt=nxt:
                          e.scalar_tensor_tensor(
                              acc[nxt][:], strip[:, j: j + TH],
                              cws[:, oc * 4 + j: oc * 4 + j + 1],
                              acc[cur][:], ALU.mult, ALU.add),
                          reads=["strip", "cws", f"acc{cur}"],
                          writes=[f"acc{nxt}"])
                    cur = nxt
                P.add("scalar", lambda e, oc=oc, cur=cur: e.activation(
                    xbb[:], acc[cur][:], AF.Silu, bias=cbs[:, oc:oc + 1]),
                    reads=[f"acc{cur}", "cbs"], writes=["xbb"])
                P.dma("sync", xb_o[oc * 128:(oc + 1) * 128,
                                   th * TH:(th + 1) * TH], xbb[:],
                      reads=["xbb"], writes=["xb_o"])
            else:
                P.dma("sync", sz_o[(oc - 8) * 128:(oc - 7) * 128,
                                   th * TH:(th + 1) * TH], zbb[:],
                      reads=["zbb"], writes=["sz_o"])


def build_s1(p=0):
    nc = bass.Bass("TRN2", target_bir_lowering=False, debug=False)
    P = Prog(nc)
    io = {
        "x": nc.dram_tensor("x", [T, D], F32, kind="ExternalInput"),
        "w_in_t": nc.dram_tensor("w_in_t", [16, 128, 1024], F32,
                                 kind="ExternalInput"),
        "lg_t": nc.dram_tensor("lg_t", [128, 8], F32, kind="ExternalInput"),
        "lb_t": nc.dram_tensor("lb_t", [128, 8], F32, kind="ExternalInput"),
        "cw_t": nc.dram_tensor("cw_t", [128, 32], F32, kind="ExternalInput"),
        "cb_t": nc.dram_tensor("cb_t", [128, 8], F32, kind="ExternalInput"),
        "xb": nc.dram_tensor("xb", [DI // 2, T], BF16, kind="ExternalOutput"),
        "sz": nc.dram_tensor("sz", [DI // 2, T], BF16, kind="ExternalOutput"),
        "xn0T": nc.dram_tensor("xn0", [D, T], BF16, kind="ExternalOutput"),
    }
    emit_s1(P, nc, nc.alloc_sbuf_tensor, nc.alloc_psum_tensor, io)
    P.emit()
    return nc



def emit_scan(P, nc, sbp, psp, io):
    ldq = io.get("ldq", "gpsimd")
    xbF, xbO, szO, x0T = io["xbF"], io["xbO"], io["szO"], io["xn0T"]
    wd, bd, wb, wc = io["wd_t"], io["bd_t"], io["wb_t"], io["wc_t"]
    la, dp, wo = io["la_t"], io["dp_t"], io["wo_t"]
    part_ap = io["part_ap"]
    bscr = io["bscr"]

    TB = 512
    NTB = T // TB
    sb, ps = sbp, psp
    xbB = [sb(f"xbB{i}", [128, 16 * TB], BF16) for i in range(2)]
    xboB = sb("xboB", [128, 8 * TB], BF16)
    szB = sb("szB", [128, 8 * TB], BF16)
    x0B = [sb(f"x0B{i}", [128, 512], BF16) for i in range(2)]
    dsb = [sb(f"dsb{i}", [128, 8 * TB], F32) for i in range(2)]
    dxb = [sb(f"dxb{i}", [128, 8 * TB], BF16) for i in range(2)]
    yB = sb("yB", [128, 8 * TB], F32)
    PBs = [sb(f"PBs{i}", [128, TB], BF16) for i in range(2)]
    PCs = [sb(f"PCs{i}", [128, TB], BF16) for i in range(2)]
    aT = [sb(f"aT{i}", [128, TB], F32) for i in range(2)] * 2
    bxT = [sb(f"bxT{i}", [128, TB], BF16) for i in range(4)]
    hT = [sb(f"hT{i}", [128, TB], BF16) for i in range(4)]
    hCT = [sb(f"hCT{i}", [128, TB], BF16) for i in range(4)]
    gB = sb("gB", [128, 8 * TB], BF16)
    pevac = [sb(f"pevac{i}", [128, 512], BF16) for i in range(2)]
    wds = sb("wds", [128, 8 * 2048], BF16)
    wos = sb("wos", [128, 8 * 1024], BF16)
    wbs = sb("wbs", [128, 256], BF16)
    wcs = sb("wcs", [128, 256], BF16)
    bds = sb("bds", [128, 8], F32)
    dps = sb("dps", [128, 8], F32)
    asb = sb("asb", [128, 128], F32)
    seeds = sb("seeds", [128, 128], F32)
    ones1 = sb("ones1", [1, 128], BF16)
    brow = [sb(f"brow{i}", [1, TB], BF16) for i in range(2)]
    bsb = sb("bsb", [16, TB], BF16)
    csb = sb("csb", [16, TB], BF16)
    spA = [sb(f"spA{i}", [128, 512], F32) for i in range(2)]
    spB = [sb(f"spB{i}", [128, 512], F32) for i in range(2)]
    ps_ = [ps(f"ps{i}", [128, 512], F32) for i in range(6)]

    for i in range(8):
        P.dma(ldq, wds[:, i * 2048:(i + 1) * 2048], wd[i, :, :],
              writes=["wds"])
        P.dma(ldq, wos[:, i * 1024:(i + 1) * 1024], wo[i, :, :],
              writes=["wos"])
    P.dma(ldq, wbs[:], wb[:, :], writes=["wbs"])
    P.dma(ldq, wcs[:], wc[:, :], writes=["wcs"])
    P.dma("sync", bds[:], bd[:, :], writes=["bds"])
    P.dma("sync", dps[:], dp[:, :], writes=["dps"])
    P.dma("sync", asb[:], la[:, :], writes=["asb"])
    P.add("scalar", lambda e: e.activation(asb[:], asb[:], AF.Exp),
          reads=["asb"], writes=["asb"])
    P.add("vector", lambda e: e.tensor_scalar(
        asb[:], asb[:], -1.0, None, ALU.mult), reads=["asb"], writes=["asb"])
    P.add("gpsimd", lambda e: e.memset(seeds[:], 0.0), writes=["seeds"])
    P.add("gpsimd", lambda e: e.memset(ones1[:], 1.0), writes=["ones1"])

    for tb in range(NTB):
        t0 = tb * TB
        tq = tb % 2
        xbBt, dsbt, dxbt = xbB[tq], dsb[tq], dxb[tq]
        for ch in range(16):
            P.dma("sync", xbBt[:, ch * TB:(ch + 1) * TB],
                  xbF[ch * 128:(ch + 1) * 128, t0:t0 + TB], writes=[f"xbB{tq}"])
        for ch in range(8):
            P.dma("sync", xboB[:, ch * TB:(ch + 1) * TB],
                  xbO[ch * 128:(ch + 1) * 128, t0:t0 + TB], writes=["xboB"])
            P.dma("sync", szB[:, ch * TB:(ch + 1) * TB],
                  szO[ch * 128:(ch + 1) * 128, t0:t0 + TB], writes=["szB"])

        for which, wsb, wname, osb, oname in (
                (0, wbs, "wbs", bsb, "bsb"), (1, wcs, "wcs", csb, "csb")):
            pp = ps_[4]
            for dc in range(16):
                P.add("tensor", lambda e, wsb=wsb, dc=dc, pp=pp, xbBt=xbBt, dsbt=dsbt, dxbt=dxbt, tq=tq:
                      e.matmul(pp[0:16, :], wsb[:, dc * 16:(dc + 1) * 16],
                               xbBt[:, dc * TB: dc * TB + TB],
                               start=(dc == 0), stop=(dc == 15)),
                      reads=[wname, f"xbB{tq}"], writes=["ps4"])
            P.add("vector", lambda e, osb=osb, pp=pp, xbBt=xbBt, dsbt=dsbt, dxbt=dxbt, tq=tq:
                  e.tensor_copy(osb[:], pp[0:16, :]),
                  reads=["ps4"], writes=[oname])
            P.dma("sync", bscr[which, :, :], osb[:], reads=[oname],
                  writes=[f"bscr{which}"])

        for oc in range(8):
            pp = ps_[oc % 2]
            for dc in range(16):
                P.add("tensor", lambda e, oc=oc, dc=dc, pp=pp, xbBt=xbBt, dsbt=dsbt, dxbt=dxbt, tq=tq:
                      e.matmul(pp[:],
                               wds[:, oc * 2048 + dc * 128:
                                   oc * 2048 + (dc + 1) * 128],
                               xbBt[:, dc * TB: dc * TB + TB],
                               start=(dc == 0), stop=(dc == 15)),
                      reads=["wds", f"xbB{tq}"], writes=[f"ps{oc % 2}"])
            q = oc % 2
            P.add("scalar", lambda e, oc=oc, pp=pp, q=q, xbBt=xbBt, dsbt=dsbt, dxbt=dxbt, tq=tq: e.activation(
                spA[q][:], pp[:], AF.Abs, bias=bds[:, oc:oc + 1]),
                reads=[f"ps{oc % 2}", "bds"], writes=[f"spA{q}"])
            P.add("scalar", lambda e, q=q, xbBt=xbBt, dsbt=dsbt, dxbt=dxbt, tq=tq: e.activation(
                spA[q][:], spA[q][:], AF.Exp, scale=-1.0),
                reads=[f"spA{q}"], writes=[f"spA{q}"])
            P.add("scalar", lambda e, q=q, xbBt=xbBt, dsbt=dsbt, dxbt=dxbt, tq=tq: e.activation(
                spA[q][:], spA[q][:], AF.Ln, bias=1.0),
                reads=[f"spA{q}"], writes=[f"spA{q}"])
            P.add("scalar", lambda e, oc=oc, pp=pp, q=q, xbBt=xbBt, dsbt=dsbt, dxbt=dxbt, tq=tq: e.activation(
                spB[q][:], pp[:], AF.Relu, bias=bds[:, oc:oc + 1]),
                reads=[f"ps{oc % 2}", "bds"], writes=[f"spB{q}"])
            P.add("vector", lambda e, oc=oc, q=q, xbBt=xbBt, dsbt=dsbt, dxbt=dxbt, tq=tq: e.tensor_add(
                dsbt[:, oc * TB:(oc + 1) * TB], spA[q][:], spB[q][:]),
                reads=[f"spA{q}", f"spB{q}"], writes=[f"dsb{tq}"])
            P.add("vector", lambda e, oc=oc, xbBt=xbBt, dsbt=dsbt, dxbt=dxbt, tq=tq: e.tensor_mul(
                dxbt[:, oc * TB:(oc + 1) * TB], dsbt[:, oc * TB:(oc + 1) * TB],
                xboB[:, oc * TB:(oc + 1) * TB]),
                reads=[f"dsb{tq}", "xboB"], writes=[f"dxb{tq}"])

        for n in range(NS):
            nq = n % 2
            for which, plane, pname in ((0, PBs[nq], f"PBs{nq}"),
                                        (1, PCs[nq], f"PCs{nq}")):
                P.dma("sync", brow[which][:], bscr[which, n:n + 1, :],
                      reads=[f"bscr{which}"], writes=[f"brow{which}"])
                pp = ps_[2 + which]
                P.add("tensor", lambda e, which=which, pp=pp, xbBt=xbBt, dsbt=dsbt, dxbt=dxbt, tq=tq:
                      e.matmul(pp[:], ones1[:], brow[which][0:1, :],
                               start=True, stop=True),
                      reads=[f"brow{which}", "ones1"],
                      writes=[f"ps{2 + which}"])
                P.add("scalar", lambda e, plane=plane, pp=pp, xbBt=xbBt, dsbt=dsbt, dxbt=dxbt, tq=tq:
                      e.activation(plane[:], pp[:], AF.Copy),
                      reads=[f"ps{2 + which}"], writes=[pname])
            for oc in range(8):
                q = (n * 8 + oc) % 4
                ncol = oc * 16 + n
                P.add("scalar", lambda e, oc=oc, ncol=ncol, q=q, xbBt=xbBt, dsbt=dsbt, dxbt=dxbt, tq=tq: e.activation(
                    aT[q][:], dsbt[:, oc * TB:(oc + 1) * TB], AF.Exp,
                    scale=asb[:, ncol:ncol + 1]),
                    reads=[f"dsb{tq}", "asb"], writes=[f"aT{q % 2}"])
                P.add("vector", lambda e, oc=oc, q=q, nq=nq, xbBt=xbBt, dsbt=dsbt, dxbt=dxbt, tq=tq: e.tensor_mul(
                    bxT[q][:], dxbt[:, oc * TB:(oc + 1) * TB], PBs[nq][:]),
                    reads=[f"dxb{tq}", f"PBs{nq}"], writes=[f"bxT{q}"])
                P.add("vector", lambda e, ncol=ncol, q=q: e.tensor_tensor_scan(
                    hT[q][:], aT[q][:], bxT[q][:], seeds[:, ncol:ncol + 1],
                    ALU.mult, ALU.add),
                    reads=[f"aT{q % 2}", f"bxT{q}", f"seeds@{ncol}"],
                    writes=[f"hT{q}"])
                P.add("scalar", lambda e, ncol=ncol, q=q, xbBt=xbBt, dsbt=dsbt, dxbt=dxbt, tq=tq: e.copy(
                    seeds[:, ncol:ncol + 1], hT[q][:, TB - 1:TB]),
                    reads=[f"hT{q}"], writes=[f"seeds@{ncol}"])
                if n == 0:
                    P.add("vector", lambda e, oc=oc, q=q, nq=nq, xbBt=xbBt, dsbt=dsbt, dxbt=dxbt, tq=tq: e.tensor_mul(
                        yB[:, oc * TB:(oc + 1) * TB], hT[q][:], PCs[nq][:]),
                        reads=[f"hT{q}", f"PCs{nq}"], writes=[f"yB@{oc}"])
                else:
                    P.add("vector", lambda e, q=q, nq=nq, xbBt=xbBt, dsbt=dsbt, dxbt=dxbt, tq=tq: e.tensor_mul(
                        hCT[q][:], hT[q][:], PCs[nq][:]),
                        reads=[f"hT{q}", f"PCs{nq}"], writes=[f"hCT{q}"])
                    P.add("gpsimd", lambda e, oc=oc, q=q: e.tensor_add(
                        yB[:, oc * TB:(oc + 1) * TB],
                        yB[:, oc * TB:(oc + 1) * TB], hCT[q][:]),
                        reads=[f"hCT{q}", f"yB@{oc}"], writes=[f"yB@{oc}"])

        for oc in range(8):
            q = oc % 2
            P.add("vector", lambda e, oc=oc, q=q, xbBt=xbBt, dsbt=dsbt, dxbt=dxbt, tq=tq: e.scalar_tensor_tensor(
                hCT[q][:], xboB[:, oc * TB:(oc + 1) * TB], dps[:, oc:oc + 1],
                yB[:, oc * TB:(oc + 1) * TB], ALU.mult, ALU.add),
                reads=["xboB", "dps", f"yB@{oc}"], writes=[f"hCT{q}"])
            P.add("vector", lambda e, oc=oc, q=q, xbBt=xbBt, dsbt=dsbt, dxbt=dxbt, tq=tq: e.tensor_mul(
                gB[:, oc * TB:(oc + 1) * TB], hCT[q][:],
                szB[:, oc * TB:(oc + 1) * TB]),
                reads=[f"hCT{q}", "szB"], writes=["gB"])

        for do in range(8):
            pp = ps_[4 + do % 2]
            P.dma("sync", x0B[do % 2][:],
                  x0T[do * 128:(do + 1) * 128, t0:t0 + TB],
                  writes=[f"x0B{do % 2}"])
            for dc in range(8):
                P.add("tensor", lambda e, do=do, dc=dc, pp=pp, xbBt=xbBt, dsbt=dsbt, dxbt=dxbt, tq=tq:
                      e.matmul(pp[:],
                               wos[:, do * 1024 + dc * 128:
                                   do * 1024 + (dc + 1) * 128],
                               gB[:, dc * TB: dc * TB + TB],
                               start=(dc == 0), stop=(dc == 7)),
                      reads=["wos", "gB"], writes=[f"ps{4 + do % 2}"])
            P.add("vector", lambda e, pp=pp, do=do, xbBt=xbBt, dsbt=dsbt, dxbt=dxbt, tq=tq: e.scalar_tensor_tensor(
                pevac[do % 2][:], x0B[do % 2][:], 0.5, pp[:],
                ALU.mult, ALU.add),
                reads=[f"ps{4 + do % 2}", f"x0B{do % 2}"],
                writes=[f"pevac{do % 2}"])
            P.dma("sync", part_ap(do, t0), pevac[do % 2][:],
                  reads=[f"pevac{do % 2}"], writes=["part_o"])


def build_scan():
    nc = bass.Bass("TRN2", target_bir_lowering=False, debug=False)
    P = Prog(nc)
    io = {
        "xbF": nc.dram_tensor("xbF", [DI, T], BF16, kind="ExternalInput"),
        "xbO": nc.dram_tensor("xbO", [DI // 2, T], BF16, kind="ExternalInput"),
        "szO": nc.dram_tensor("sz", [DI // 2, T], BF16, kind="ExternalInput"),
        "xn0T": nc.dram_tensor("xn0", [D, T], BF16, kind="ExternalInput"),
        "wd_t": nc.dram_tensor("wd_t", [8, 128, 2048], F32, kind="ExternalInput"),
        "bd_t": nc.dram_tensor("bd_t", [128, 8], F32, kind="ExternalInput"),
        "wb_t": nc.dram_tensor("wb_t", [128, 256], F32, kind="ExternalInput"),
        "wc_t": nc.dram_tensor("wc_t", [128, 256], F32, kind="ExternalInput"),
        "la_t": nc.dram_tensor("la_t", [128, 128], F32, kind="ExternalInput"),
        "dp_t": nc.dram_tensor("dp_t", [128, 8], F32, kind="ExternalInput"),
        "wo_t": nc.dram_tensor("wo_t", [8, 128, 1024], F32, kind="ExternalInput"),
        "bscr": nc.dram_tensor("bscr", [2, 16, 512], BF16, kind="Internal"),
    }
    part = nc.dram_tensor("part", [2, D, T // 2], BF16, kind="ExternalOutput")
    io["part_ap"] = lambda do, t0: part[t0 // 4096, do * 128:(do + 1) * 128,
                                        t0 % 4096: t0 % 4096 + 512]
    emit_scan(P, nc, nc.alloc_sbuf_tensor, nc.alloc_psum_tensor, io)
    P.emit()
    return nc


def stage_scan_inputs(inputs, cid, xb_even_first, xb_own, sz_own, xn0T):
    b, p = cid >> 1, cid & 1
    own = slice(p * 1024, (p + 1) * 1024)
    Wd = np.asarray(inputs["L0_attn_W_delta"])[:, own]
    Wo = np.asarray(inputs["L0_attn_W_out"])[own]
    return {
        "xbF": xb_even_first,
        "xbO": xb_own,
        "sz": sz_own,
        "xn0": xn0T,
        "wd_t": tile_lhsT(Wd).reshape(8, 128, 2048),
        "bd_t": vec_t(np.asarray(inputs["L0_attn_b_delta"])[own]),
        "wb_t": np.ascontiguousarray(
            np.asarray(inputs["L0_attn_W_b"]).reshape(16, 128, 16)
            .transpose(1, 0, 2).reshape(128, 256)),
        "wc_t": np.ascontiguousarray(
            np.asarray(inputs["L0_attn_W_c"]).reshape(16, 128, 16)
            .transpose(1, 0, 2).reshape(128, 256)),
        "la_t": np.ascontiguousarray(
            np.asarray(inputs["L0_attn_log_A"])[own].reshape(8, 128, 16)
            .transpose(1, 0, 2).reshape(128, 128)),
        "dp_t": vec_t(np.asarray(inputs["L0_attn_Dp"])[own]),
        "wo_t": tile_lhsT(Wo).reshape(8, 128, 1024),
    }



def emit_combine(P, nc, sbp, psp, io, want_xn, summed):
    TO = T // 2
    rp = io["rp"]
    rn_o = io["rn"]
    sb, ps = sbp, psp

    pat = [sb(f"pat{i}", [128, 128], BF16) for i in range(2)]
    pbt = [sb(f"pbt{i}", [128, 128], BF16) for i in range(2)]
    sumb = sb("sumb", [128, 128], BF16)
    mo = [sb(f"mo{i}", [128, D], F32) for i in range(2)]
    rt = [sb(f"rt{i}", [128, D], F32) for i in range(2)]
    xmu = [sb(f"xmu{i}", [128, D], F32) for i in range(2)]
    sqb = [sb(f"sqb{i}", [128, D], F32) for i in range(2)]
    lnv = [sb(f"lnv{i}", [128, D], F32) for i in range(2)]
    rnb = [sb(f"rnb{i}", [128, D], F32) for i in range(2)]
    xnb = [sb(f"xnb{i}", [128, D], BF16) for i in range(2)]
    st = [sb(f"st{i}", [128, 12], F32) for i in range(2)]
    identB = sb("identB", [128, 128], BF16)
    psT = [ps(f"psT{i}", [128, 128], BF16) for i in range(2)]

    P.add("gpsimd", lambda e: e.memset(identB[:], 0.0), writes=["identB"])
    P.add("gpsimd", lambda e: e.affine_select(
        out=identB[:], in_=identB[:], compare_op=ALU.not_equal, fill=1.0,
        base=0, pattern=[[-1, 128]], channel_multiplier=1),
        reads=["identB"], writes=["identB"])

    def ln_plain(q, xin, xname, xout, oname):
        s, sn = st[q], f"st{q}"
        P.add("vector", lambda e: e.reduce_sum(s[:, 0:1], xin[:], axis=AX.X),
              reads=[xname], writes=[sn + "@a"])
        P.add("vector", lambda e: e.tensor_scalar(
            s[:, 1:2], s[:, 0:1], 1.0 / D, None, ALU.mult),
            reads=[sn + "@a"], writes=[sn + "@b"])
        P.add("vector", lambda e: e.tensor_scalar(
            xmu[q][:], xin[:], s[:, 1:2], None, ALU.subtract),
            reads=[xname, sn + "@b"], writes=[f"xmu{q}"])
        P.add("scalar", lambda e: e.activation(
            sqb[q][:], xmu[q][:], AF.Square, accum_out=s[:, 2:3]),
            reads=[f"xmu{q}"], writes=[f"sqb{q}", sn + "@c"])
        P.add("vector", lambda e: e.tensor_scalar(
            s[:, 5:6], s[:, 2:3], 1.0 / D, EPS, ALU.mult, ALU.add),
            reads=[sn + "@c"], writes=[sn + "@f"])
        P.add("scalar", lambda e: e.sqrt(s[:, 3:4], s[:, 5:6]),
              reads=[sn + "@f"], writes=[sn + "@d"])
        P.add("vector", lambda e: e.reciprocal(s[:, 4:5], s[:, 3:4]),
              reads=[sn + "@d"], writes=[sn + "@e"])
        P.add("vector", lambda e: e.tensor_scalar(
            xout[:], xmu[q][:], s[:, 4:5], None, ALU.mult),
            reads=[f"xmu{q}", sn + "@e"], writes=[oname])

    for tt in range(TO // 128):
        q = tt % 2
        t0 = tt * 128
        for dc in range(8):
            dq = dc % 2
            if summed:
                pa = io["psum"]
                P.dma("sync", pat[dq][:],
                      pa[dc * 128:(dc + 1) * 128, t0:t0 + 128],
                      writes=[f"pat{dq}"])
                P.add("tensor", lambda e, q=q, dq=dq: e.transpose(
                    psT[q][:], pat[dq][:], identB[:]),
                    reads=[f"pat{dq}", "identB"], writes=[f"psT{q}"])
            else:
                P.dma("sync", pat[dq][:],
                      io["pa"][dc * 128:(dc + 1) * 128, t0:t0 + 128],
                      writes=[f"pat{dq}"])
                P.dma("sync", pbt[dq][:],
                      io["pb"][dc * 128:(dc + 1) * 128, t0:t0 + 128],
                      writes=[f"pbt{dq}"])
                P.add("vector", lambda e, dq=dq: e.tensor_add(
                    sumb[:], pat[dq][:], pbt[dq][:]),
                    reads=[f"pat{dq}", f"pbt{dq}"], writes=["sumb"])
                P.add("tensor", lambda e, q=q: e.transpose(
                    psT[q][:], sumb[:], identB[:]),
                    reads=["sumb", "identB"], writes=[f"psT{q}"])
            P.add("vector", lambda e, q=q, dc=dc: e.tensor_copy(
                mo[q][:, dc * 128:(dc + 1) * 128], psT[q][:]),
                reads=[f"psT{q}"], writes=[f"mo{q}"])
        ln_plain(q, mo[q], f"mo{q}", lnv[q], f"lnv{q}")
        P.dma("sync", rt[q][:], rp[t0:t0 + 128, :], writes=[f"rt{q}"])
        P.add("vector", lambda e, q=q: e.tensor_add(
            rnb[q][:], rt[q][:], lnv[q][:]),
            reads=[f"rt{q}", f"lnv{q}"], writes=[f"rnb{q}"])
        P.dma("sync", rn_o[t0:t0 + 128, :], rnb[q][:],
              reads=[f"rnb{q}"], writes=["rn_o"])
        if want_xn:
            xn_ap = io["xn_ap"]
            ln_plain(q, rnb[q], f"rnb{q}", xnb[q], f"xnb{q}")
            for dc in range(8):
                P.add("tensor", lambda e, q=q, dc=dc: e.transpose(
                    psT[q][:], xnb[q][:, dc * 128:(dc + 1) * 128], identB[:]),
                    reads=[f"xnb{q}", "identB"], writes=[f"psT{q}"])
                P.add("vector", lambda e, q=q: e.tensor_copy(
                    sumb[:], psT[q][:]),
                    reads=[f"psT{q}"], writes=["sumb"])
                P.dma("sync", xn_ap(dc, t0), sumb[:],
                      reads=["sumb"], writes=["xn_o"])


def build_combine(want_xn, summed=False):
    nc = bass.Bass("TRN2", target_bir_lowering=False, debug=False)
    P = Prog(nc)
    TO = T // 2
    io = {"rp": nc.dram_tensor("rp", [TO, D], F32, kind="ExternalInput"),
          "rn": nc.dram_tensor("rn", [TO, D], F32, kind="ExternalOutput")}
    if summed:
        io["psum"] = nc.dram_tensor("psm", [D, TO], BF16, kind="ExternalInput")
    else:
        io["pa"] = nc.dram_tensor("pa", [D, TO], BF16, kind="ExternalInput")
        io["pb"] = nc.dram_tensor("pb", [D, TO], BF16, kind="ExternalInput")
    if want_xn:
        xn_o = nc.dram_tensor("xnx", [D, TO], BF16, kind="ExternalOutput")
        io["xn_ap"] = lambda dc, t0: xn_o[dc * 128:(dc + 1) * 128, t0:t0 + 128]
    emit_combine(P, nc, nc.alloc_sbuf_tensor, nc.alloc_psum_tensor, io,
                 want_xn, summed)
    P.emit()
    return nc



def emit_moe(P, nc, sbp, psp, io):
    ldq = io.get("ldq", "gpsimd")
    sb, ps = sbp, psp
    gw, sel = io["gw_t"], io["sel_t"]
    w1, w3, w2 = io["w1_t"], io["w3_t"], io["w2_t"]
    xn_src = io["xn_src"]
    part_ap = io["part_ap"]

    TBK = 2048
    NBK = T // TBK
    SUP = (8, 8, 8, 8, 8, 4)

    gws = sb("gws", [128, 32], F32)
    xng = sb("xng", [128, 8 * 128], F32)
    sels = sb("sels", [4, 256], F32)
    xnB = sb("xnB", [128, 8 * TBK], BF16)
    lg4 = sb("lg4", [128, 4], F32)
    cw4 = sb("cw4", [128, 4], F32)
    cwT = sb("cwT", [4, T], F32)
    pl = [sb(f"pl{i}", [128, TBK], F32) for i in range(2)]
    sil = [sb(f"sil{i}", [128, 512], F32) for i in range(2)]
    hstr = sb("hstr", [128, 8 * TBK], BF16)
    oac = sb("oac", [128, 8 * TBK], F32)
    tmp5 = [sb(f"tmp5{i}", [128, 512], F32) for i in range(2)]
    pevac = [sb(f"pevac{i}", [128, 512], BF16) for i in range(2)]
    w13s = [sb(f"w13s{i}", [128, 1024], BF16) for i in range(4)]
    w2s = [sb(f"w2s{i}", [128, 1024], BF16) for i in range(2)]
    stm = sb("stm", [128, 10], F32)
    cmpb = sb("cmpb", [128, 4], F32)
    identF = sb("identF", [128, 128], F32)
    ps_ = [ps(f"ps{i}", [128, 512], F32) for i in range(8)]

    P.dma("sync", gws[:], gw[:, :], writes=["gws"])
    P.dma("sync", sels[:], sel[:, :], writes=["sels"])
    P.add("gpsimd", lambda e: e.memset(identF[:], 0.0), writes=["identF"])
    P.add("gpsimd", lambda e: e.affine_select(
        out=identF[:], in_=identF[:], compare_op=ALU.not_equal, fill=1.0,
        base=0, pattern=[[-1, 128]], channel_multiplier=1),
        reads=["identF"], writes=["identF"])

    for tt in range(T // 128):
        t0 = tt * 128
        pp = ps_[0]
        for dc in range(8):
            P.dma("gpsimd", xng[:, dc * 128:(dc + 1) * 128],
                  xn_src(dc, t0, 128), writes=["xng"])
        for dc in range(8):
            P.add("tensor", lambda e, dc=dc, pp=pp: e.matmul(
                pp[0:128, 0:4], xng[:, dc * 128:(dc + 1) * 128],
                gws[:, dc * 4:(dc + 1) * 4],
                start=(dc == 0), stop=(dc == 7)),
                reads=["xng", "gws"], writes=["ps0"])
        P.add("vector", lambda e, pp=pp: e.tensor_copy(lg4[:], pp[0:128, 0:4]),
              reads=["ps0"], writes=["lg4"])
        P.add("gpsimd", lambda e: e.memset(cw4[:], 0.0), writes=["cw4"])
        for e_ in range(4):
            for j in range(4):
                if j == e_:
                    continue
                op = ALU.is_ge if j < e_ else ALU.is_gt
                P.add("vector", lambda e, e_=e_, j=j, op=op: e.tensor_tensor(
                    cmpb[:, e_:e_ + 1], lg4[:, j:j + 1], lg4[:, e_:e_ + 1], op),
                    reads=["lg4"], writes=[f"cmpb@{e_}"])
                P.add("vector", lambda e, e_=e_: e.tensor_add(
                    cw4[:, e_:e_ + 1], cw4[:, e_:e_ + 1], cmpb[:, e_:e_ + 1]),
                    reads=[f"cmpb@{e_}"], writes=["cw4"])
        P.add("vector", lambda e: e.tensor_scalar(
            cw4[:], cw4[:], 1.5, None, ALU.is_lt),
            reads=["cw4"], writes=["cw4"])
        P.add("vector", lambda e: e.reduce_max(
            stm[:, 0:1], lg4[:], axis=AX.X, negate=True),
            reads=["lg4"], writes=["stm@a"])
        P.add("scalar", lambda e: e.activation(
            cmpb[:], lg4[:], AF.Exp, bias=stm[:, 0:1]),
            reads=["lg4", "stm@a"], writes=["cmpb"])
        P.add("vector", lambda e: e.tensor_mul(cw4[:], cw4[:], cmpb[:]),
              reads=["cw4", "cmpb"], writes=["cw4"])
        P.add("vector", lambda e: e.reduce_sum(
            stm[:, 1:2], cw4[:], axis=AX.X),
            reads=["cw4"], writes=["stm@b"])
        P.add("vector", lambda e: e.reciprocal(stm[:, 2:3], stm[:, 1:2]),
              reads=["stm@b"], writes=["stm@c"])
        P.add("vector", lambda e: e.tensor_scalar(
            cw4[:], cw4[:], stm[:, 2:3], None, ALU.mult),
            reads=["cw4", "stm@c"], writes=["cw4"])
        P.add("tensor", lambda e, pp=pp: e.transpose(
            pp[0:4, 0:128], cw4[:], identF[:]),
            reads=["cw4", "identF"], writes=["ps0"])
        P.add("vector", lambda e, t0=t0, pp=pp: e.tensor_copy(
            cwT[:, t0:t0 + 128], pp[0:4, 0:128]),
            reads=["ps0"], writes=["cwT"])

    for bk in range(NBK):
        t0 = bk * TBK
        for dc in range(8):
            P.dma("sync", xnB[:, dc * TBK:(dc + 1) * TBK],
                  xn_src(dc, t0, TBK), writes=["xnB"])
        for le in range(2):
            for t5 in range(4):
                pp = ps_[5]
                P.add("tensor", lambda e, le=le, t5=t5, t0=t0, pp=pp:
                      e.matmul(pp[:],
                               sels[:, le * 128:(le + 1) * 128],
                               cwT[:, t0 + t5 * 512: t0 + t5 * 512 + 512],
                               start=True, stop=True),
                      reads=["sels", "cwT"], writes=["ps5"])
                P.add("vector", lambda e, le=le, t5=t5, pp=pp: e.tensor_copy(
                    pl[le][:, t5 * 512: t5 * 512 + 512], pp[:]),
                    reads=["ps5"], writes=[f"pl{le}"])
        first = True
        for le in range(2):
            hbase = 0
            for si, sn_ in enumerate(SUP):
                for hi in range(sn_):
                    ht = hbase + hi
                    qa, qb = (ht % 2) * 2, (ht % 2) * 2 + 1
                    P.dma(ldq, w13s[qa][:], w1[le, ht, :, :],
                          writes=[f"w13s{qa}"])
                    P.dma(ldq, w13s[qb][:], w3[le, ht, :, :],
                          writes=[f"w13s{qb}"])
                    for t5 in range(4):
                        gi, ui = t5 % 2, 2 + t5 % 2
                        for dc in range(8):
                            P.add("tensor",
                                  lambda e, qa=qa, dc=dc, t5=t5, gi=gi:
                                  e.matmul(ps_[gi][:],
                                           w13s[qa][:, dc * 128:(dc + 1) * 128],
                                           xnB[:, dc * TBK + t5 * 512:
                                               dc * TBK + t5 * 512 + 512],
                                           start=(dc == 0), stop=(dc == 7)),
                                  reads=[f"w13s{qa}", "xnB"],
                                  writes=[f"ps{gi}"])
                            P.add("tensor",
                                  lambda e, qb=qb, dc=dc, t5=t5, ui=ui:
                                  e.matmul(ps_[ui][:],
                                           w13s[qb][:, dc * 128:(dc + 1) * 128],
                                           xnB[:, dc * TBK + t5 * 512:
                                               dc * TBK + t5 * 512 + 512],
                                           start=(dc == 0), stop=(dc == 7)),
                                  reads=[f"w13s{qb}", "xnB"],
                                  writes=[f"ps{ui}"])
                        P.add("scalar", lambda e, t5=t5, gi=gi: e.activation(
                            sil[t5 % 2][:], ps_[gi][:], AF.Silu),
                            reads=[f"ps{gi}"], writes=[f"sil{t5 % 2}"])
                        P.add("vector",
                              lambda e, hi=hi, t5=t5, ui=ui: e.tensor_mul(
                                  hstr[:, hi * TBK + t5 * 512:
                                       hi * TBK + t5 * 512 + 512],
                                  sil[t5 % 2][:], ps_[ui][:]),
                              reads=[f"sil{t5 % 2}", f"ps{ui}"],
                              writes=["hstr"])
                for do in range(8):
                    qd = do % 2
                    P.dma(ldq, w2s[qd][:, 0:sn_ * 128],
                          w2[le, do, :, hbase * 128:(hbase + sn_) * 128],
                          writes=[f"w2s{qd}"])
                    for t5 in range(4):
                        pi = 4 + (qd * 2 + t5 % 2) % 4
                        for hc in range(sn_):
                            P.add("tensor",
                                  lambda e, qd=qd, hc=hc, t5=t5, pi=pi:
                                  e.matmul(ps_[pi][:],
                                           w2s[qd][:, hc * 128:(hc + 1) * 128],
                                           hstr[:, hc * TBK + t5 * 512:
                                                hc * TBK + t5 * 512 + 512],
                                           start=(hc == 0), stop=(hc == sn_ - 1)),
                                  reads=[f"w2s{qd}", "hstr"],
                                  writes=[f"ps{pi}"])
                        P.add("vector",
                              lambda e, le=le, t5=t5, pi=pi: e.tensor_mul(
                                  tmp5[t5 % 2][:], ps_[pi][:],
                                  pl[le][:, t5 * 512: t5 * 512 + 512]),
                              reads=[f"ps{pi}", f"pl{le}"],
                              writes=[f"tmp5{t5 % 2}"])
                        if first:
                            P.add("gpsimd",
                                  lambda e, do=do, t5=t5: e.tensor_copy(
                                      oac[:, do * TBK + t5 * 512:
                                          do * TBK + t5 * 512 + 512],
                                      tmp5[t5 % 2][:]),
                                  reads=[f"tmp5{t5 % 2}"],
                                  writes=[f"oac@{do}_{t5}"])
                        else:
                            P.add("gpsimd",
                                  lambda e, do=do, t5=t5: e.tensor_add(
                                      oac[:, do * TBK + t5 * 512:
                                          do * TBK + t5 * 512 + 512],
                                      oac[:, do * TBK + t5 * 512:
                                          do * TBK + t5 * 512 + 512],
                                      tmp5[t5 % 2][:]),
                                  reads=[f"tmp5{t5 % 2}", f"oac@{do}_{t5}"],
                                  writes=[f"oac@{do}_{t5}"])
                hbase += sn_
                first = False
        for do in range(8):
            for t5 in range(4):
                P.add("vector", lambda e, do=do, t5=t5: e.tensor_copy(
                    pevac[t5 % 2][:], oac[:, do * TBK + t5 * 512:
                                          do * TBK + t5 * 512 + 512]),
                    reads=[f"oac@{do}_{t5}"], writes=[f"pevac{t5 % 2}"])
                P.dma("sync", part_ap(do, t0 + t5 * 512),
                      pevac[t5 % 2][:], reads=[f"pevac{t5 % 2}"],
                      writes=["mp_o"])





def build_attn():
    nc = bass.Bass("TRN2", target_bir_lowering=False, debug=False)
    P = Prog(nc)
    xna = nc.dram_tensor("xna", [D, T], BF16, kind="ExternalInput")
    ap_o = nc.dram_tensor("apart", [2, D, T // 2], BF16, kind="ExternalOutput")
    io = {
        "oscr": nc.dram_tensor("oscr", [4, 128, T], BF16, kind="Internal"),
        "wq_t": nc.dram_tensor("wq_t", [4, 128, 1024], F32, kind="ExternalInput"),
        "wk_t": nc.dram_tensor("wk_t", [4, 128, 1024], F32, kind="ExternalInput"),
        "wv_t": nc.dram_tensor("wv_t", [4, 128, 1024], F32, kind="ExternalInput"),
        "wo_t": nc.dram_tensor("wo_t", [8, 128, 512], F32, kind="ExternalInput"),
        "cos_t": nc.dram_tensor("cos_t", [32, T], F32, kind="ExternalInput"),
        "sin_t": nc.dram_tensor("sin_t", [32, T], F32, kind="ExternalInput"),
        "mask_t": nc.dram_tensor("mask_t", [128, 4 * 512], BF16, kind="ExternalInput"),
        "csc_t": nc.dram_tensor("csc_t", [128, 1], F32, kind="ExternalInput"),
        "xn_src": lambda dc, t0, w: xna[dc * 128:(dc + 1) * 128, t0:t0 + w],
        "part_ap": lambda do, t0: ap_o[t0 // 4096, do * 128:(do + 1) * 128,
                                       t0 % 4096: t0 % 4096 + 512],
    }
    emit_attn(P, nc, nc.alloc_sbuf_tensor, nc.alloc_psum_tensor, io)
    P.emit()
    return nc


def emit_attn(P, nc, sbp, psp, io):
    ldq = io.get("ldq", "gpsimd")
    oscr = io["oscr"]
    sb, ps = sbp, psp
    wq, wk, wv, wo = io["wq_t"], io["wk_t"], io["wv_t"], io["wo_t"]
    cosd, sind = io["cos_t"], io["sin_t"]
    maskd, cscd = io["mask_t"], io["csc_t"]
    xn_src = io["xn_src"]
    part_ap = io["part_ap"]

    QT = sb("QT", [128, T], BF16)
    KT = sb("KT", [128, T], BF16)
    VSB = sb("VSB", [128, T], BF16)
    oT4 = [sb(f"oT4_{i}", [128, 512], BF16) for i in range(2)]
    Sstr = [sb(f"Sstr{i}", [128, T], F32) for i in range(2)]
    Pstr = [sb("Pstr0", [128, T], BF16)] * 2
    xnb = sb("xnb", [128, 8 * 512], BF16)
    wqs = sb("wqs", [128, 4 * 1024], BF16)
    wks = sb("wks", [128, 4 * 1024], BF16)
    wvs = sb("wvs", [128, 4 * 1024], BF16)
    wos = sb("wos", [128, 8 * 512], BF16)
    coss = sb("coss", [32, 512], F32)
    sins = sb("sins", [32, 512], F32)
    masks = sb("masks", [128, 4 * 512], BF16)
    cscs = sb("cscs", [128, 1], F32)
    identB = sb("identB", [128, 128], BF16)
    identF = sb("identF", [128, 128], F32)
    rtmp = [sb(f"rtmp{i}", [32, 512], F32) for i in range(4)]
    PTs = [sb(f"PTs{i}", [128, 1024], BF16) for i in range(2)]
    lac = [sb(f"lac{i}", [128, 4], F32) for i in range(2)]
    rowred = [sb(f"rowred{i}", [128, 1], F32) for i in range(2)]
    lrs = sb("lrs", [1, 128], F32)
    nmrow = [sb(f"nmrow{i}", [128, 64], F32) for i in range(2)]
    lpls = sb("lpls", [128, 128], F32)
    ones1 = sb("ones1", [1, 128], F32)
    oev = sb("oev", [128, 512], BF16)
    psS = ps("psS", [128, 512], F32)
    psS2 = ps("psS2", [128, 512], F32)
    psPV_ = [ps(f"psPV{i}", [128, 512], F32) for i in range(2)]
    psAux = ps("psAux", [128, 512], F32)
    psQ = ps("psQ", [128, 512], F32)
    psTb = [ps(f"psTb{i}", [128, 1024], BF16) for i in range(2)]

    for tens, name, src_, n_ in ((wqs, "wqs", wq, 4), (wks, "wks", wk, 4),
                                 (wvs, "wvs", wv, 4)):
        for h in range(n_):
            P.dma(ldq, tens[:, h * 1024:(h + 1) * 1024], src_[h, :, :],
                  writes=[name])
    for do in range(8):
        P.dma(ldq, wos[:, do * 512:(do + 1) * 512], wo[do, :, :],
              writes=["wos"])
    P.dma("sync", masks[:], maskd[:, :], writes=["masks"])
    P.dma("sync", cscs[:], cscd[:, :], writes=["cscs"])
    P.add("gpsimd", lambda e: e.memset(identB[:], 0.0), writes=["identB"])
    P.add("gpsimd", lambda e: e.affine_select(
        out=identB[:], in_=identB[:], compare_op=ALU.not_equal, fill=1.0,
        base=0, pattern=[[-1, 128]], channel_multiplier=1),
        reads=["identB"], writes=["identB"])
    P.add("gpsimd", lambda e: e.memset(identF[:], 0.0), writes=["identF"])
    P.add("gpsimd", lambda e: e.affine_select(
        out=identF[:], in_=identF[:], compare_op=ALU.not_equal, fill=1.0,
        base=0, pattern=[[-1, 128]], channel_multiplier=1),
        reads=["identF"], writes=["identF"])
    P.add("gpsimd", lambda e: e.memset(ones1[:], 1.0), writes=["ones1"])

    def rope_evac(head, t5, pp, ppname, dst, dname):
        c0 = t5 * 512
        P.add("vector", lambda e: e.tensor_mul(
            rtmp[0][:], pp[0:32, :], coss[:, :]),
            reads=[ppname, "coss"], writes=["rtmp0"])
        P.add("vector", lambda e: e.tensor_mul(
            rtmp[1][:], pp[32:64, :], sins[:, :]),
            reads=[ppname, "sins"], writes=["rtmp1"])
        P.add("vector", lambda e: e.tensor_mul(
            rtmp[2][:], pp[0:32, :], sins[:, :]),
            reads=[ppname, "sins"], writes=["rtmp2"])
        P.add("vector", lambda e: e.tensor_mul(
            rtmp[3][:], pp[32:64, :], coss[:, :]),
            reads=[ppname, "coss"], writes=["rtmp3"])
        P.add("vector", lambda e: e.tensor_sub(
            dst[0:32, c0:c0 + 512], rtmp[0][:], rtmp[1][:]),
            reads=["rtmp0", "rtmp1"], writes=[dname])
        P.add("vector", lambda e: e.tensor_add(
            dst[32:64, c0:c0 + 512], rtmp[2][:], rtmp[3][:]),
            reads=["rtmp2", "rtmp3"], writes=[dname])
        P.add("vector", lambda e: e.tensor_copy(
            dst[64:128, c0:c0 + 512], pp[64:128, :]),
            reads=[ppname], writes=[dname])

    NKT = T // 512
    for h in range(4):
        for t5 in range(16):
            t0 = t5 * 512
            P.dma("sync", coss[:], cosd[:, t0:t0 + 512], writes=["coss"])
            P.dma("sync", sins[:], sind[:, t0:t0 + 512], writes=["sins"])
            for dc in range(8):
                P.dma("sync", xnb[:, dc * 512:(dc + 1) * 512],
                      xn_src(dc, t0, 512),
                      writes=["xnb"])
            for wset, wname, pp, ppname in ((wks, "wks", psQ, "psQ"),):
                for dc in range(8):
                    P.add("tensor", lambda e, h=h, dc=dc, wset=wset, pp=pp:
                          e.matmul(pp[:],
                                   wset[:, h * 1024 + dc * 128:
                                        h * 1024 + (dc + 1) * 128],
                                   xnb[:, dc * 512:(dc + 1) * 512],
                                   start=(dc == 0), stop=(dc == 7)),
                          reads=[wname, "xnb"], writes=[ppname])
                rope_evac(h, t5, pp, ppname, KT, "KT")
            for dc in range(8):
                P.add("tensor", lambda e, h=h, dc=dc: e.matmul(
                    psQ[:], wqs[:, h * 1024 + dc * 128:
                                h * 1024 + (dc + 1) * 128],
                    xnb[:, dc * 512:(dc + 1) * 512],
                    start=(dc == 0), stop=(dc == 7)),
                    reads=["wqs", "xnb"], writes=["psQ"])
            rope_evac(h, t5, psQ, "psQ", QT, "QT")
            for tt in range(4):
                for dc in range(8):
                    P.add("tensor", lambda e, h=h, dc=dc, tt=tt: e.matmul(
                        psAux[:, 0:128],
                        xnb[:, dc * 512 + tt * 128: dc * 512 + tt * 128 + 128],
                        wvs[:, h * 1024 + dc * 128: h * 1024 + (dc + 1) * 128],
                        start=(dc == 0), stop=(dc == 7)),
                        reads=["wvs", "xnb"], writes=["psAux"])
                P.add("vector", lambda e, t5=t5, tt=tt: e.tensor_copy(
                    VSB[:, (t5 * 4 + tt) * 128: (t5 * 4 + tt) * 128 + 128],
                    psAux[:, 0:128]),
                    reads=["psAux"], writes=["VSB"])

        for qt in range(T // 128):
            ktmax = qt // 4
            nkt = ktmax + 1
            qq = qt % 2
            lacq = lac[qq]
            psPV, psPVn = psPV_[qq], f"psPV{qq}"
            Ss, Ssn = Sstr[qq], f"Sstr{qq}"
            Ps, Psn = Pstr[0], "Pstr0"
            SB4 = [(psS, "psS"), (psQ, "psQ"), (psS2, "psS2")]
            for kt in range(nkt):
                pS, pn = SB4[kt % 3]
                P.add("tensor", lambda e, qt=qt, kt=kt, pS=pS: e.matmul(
                    pS[:], QT[:, qt * 128:(qt + 1) * 128],
                    KT[:, kt * 512:(kt + 1) * 512], start=True, stop=True),
                    reads=["QT", "KT"], writes=[pn])
                if kt == ktmax:
                    v = qt % 4
                    P.add("vector", lambda e, v=v, pS=pS, kt=kt, Ss=Ss:
                          e.tensor_add(Ss[:, kt * 512:(kt + 1) * 512],
                                       pS[:], masks[:, v * 512:(v + 1) * 512]),
                          reads=[pn, "masks"], writes=[Ssn])
                elif kt % 2 == 0:
                    P.add("vector", lambda e, pS=pS, kt=kt, Ss=Ss:
                          e.tensor_copy(Ss[:, kt * 512:(kt + 1) * 512], pS[:]),
                          reads=[pn], writes=[Ssn])
                else:
                    P.add("scalar", lambda e, pS=pS, kt=kt, Ss=Ss:
                          e.activation(Ss[:, kt * 512:(kt + 1) * 512], pS[:],
                                       AF.Copy),
                          reads=[pn], writes=[Ssn])
            P.add("vector", lambda e, Ss=Ss, nkt=nkt, lacq=lacq: e.reduce_max(
                lacq[:, 1:2], Ss[:, 0:nkt * 512], axis=AX.X, negate=True),
                reads=[Ssn], writes=[f"lac{qq}@nm"])
            P.add("vector", lambda e, lacq=lacq: e.tensor_scalar(
                lacq[:, 2:3], lacq[:, 1:2], cscs[:, 0:1], None, ALU.mult),
                reads=[f"lac{qq}@nm", "cscs"], writes=[f"lac{qq}@bias"])
            P.add("scalar", lambda e, Ss=Ss, Ps=Ps, nkt=nkt, lacq=lacq:
                  e.activation(Ps[:, 0:nkt * 512], Ss[:, 0:nkt * 512], AF.Exp,
                               scale=cscs[:, 0:1], bias=lacq[:, 2:3],
                               accum_out=lacq[:, 0:1]),
                  reads=[Ssn, "cscs", f"lac{qq}@bias"],
                  writes=[Psn, f"lac{qq}@l"])
            for k0 in range(0, nkt, 2):
                pq2 = (k0 // 2) % 2
                kts = [kt for kt in (k0, k0 + 1) if kt < nkt]
                for kt in kts:
                    half = (kt % 2) * 512
                    for j in range(4):
                        P.add("tensor", lambda e, j=j, kt=kt, pq2=pq2,
                              half=half, Ps=Ps: e.transpose(
                                  psTb[pq2][:, half + j * 128:
                                             half + (j + 1) * 128],
                                  Ps[:, kt * 512 + j * 128:
                                     kt * 512 + (j + 1) * 128],
                                  identB[:]),
                              reads=[Psn, "identB"], writes=[f"psTb{pq2}"])
                w = len(kts) * 512
                P.add("vector", lambda e, pq2=pq2, w=w: e.tensor_copy(
                    PTs[pq2][:, 0:w], psTb[pq2][:, 0:w]),
                    reads=[f"psTb{pq2}"], writes=[f"PTs{pq2}"])
                for kt in kts:
                    half = (kt % 2) * 512
                    for j in range(4):
                        P.add("tensor", lambda e, qt=qt, kt=kt, j=j, pq2=pq2,
                              half=half, psPV=psPV:
                              e.matmul(
                            psPV[:, 0:128],
                            VSB[:, (kt * 4 + j) * 128:
                                (kt * 4 + j) * 128 + 128],
                            PTs[pq2][:, half + j * 128: half + (j + 1) * 128],
                            start=(kt == 0 and j == 0),
                            stop=(kt == qt // 4 and j == 3)),
                            reads=["VSB", f"PTs{pq2}"], writes=[psPVn])
            P.add("vector", lambda e, lacq=lacq: e.reciprocal(
                lacq[:, 3:4], lacq[:, 0:1]),
                reads=[f"lac{qq}@l"], writes=[f"lac{qq}@linv"])
            P.add("tensor", lambda e, lacq=lacq: e.transpose(
                psAux[0:1, 0:128], lacq[:, 3:4], identF[:]),
                reads=[f"lac{qq}@linv", "identF"], writes=["psAux"])
            P.add("vector", lambda e: e.tensor_copy(lrs[:], psAux[0:1, 0:128]),
                  reads=["psAux"], writes=["lrs"])
            P.add("tensor", lambda e: e.matmul(
                psAux[:, 0:128], ones1[:], lrs[:], start=True, stop=True),
                reads=["ones1", "lrs"], writes=["psAux"])
            P.add("vector", lambda e: e.tensor_copy(lpls[:], psAux[:, 0:128]),
                  reads=["psAux"], writes=["lpls"])
            og = (qt // 4) % 2
            P.add("vector", lambda e, qt=qt, og=og, psPV=psPV: e.tensor_mul(
                oT4[og][:, (qt % 4) * 128: (qt % 4) * 128 + 128],
                psPV[:, 0:128], lpls[:]),
                reads=[psPVn, "lpls"], writes=[f"oT4_{og}@{qt % 4}"])
            if qt % 4 == 3:
                P.dma("sync", oscr[h, :, (qt - 3) * 128:(qt + 1) * 128],
                      oT4[og][:],
                      reads=[f"oT4_{og}"], writes=["oscr"])

    oTk = [sb(f"oTk{i}", [128, 4 * 512], BF16) for i in range(2)]
    for t5 in range(16):
        t0 = t5 * 512
        tq5 = t5 % 2
        for kc in range(4):
            P.dma("sync", oTk[tq5][:, kc * 512:(kc + 1) * 512],
                  oscr[kc, :, t0:t0 + 512], writes=[f"oTk{tq5}"])
        for do in range(8):
            pp = psS if do % 2 == 0 else psQ
            ppn = "psS" if do % 2 == 0 else "psQ"
            for kc in range(4):
                P.add("tensor", lambda e, do=do, kc=kc, t0=t0, pp=pp, tq5=tq5:
                      e.matmul(pp[:],
                               wos[:, do * 512 + kc * 128:
                                   do * 512 + (kc + 1) * 128],
                               oTk[tq5][:, kc * 512:(kc + 1) * 512],
                               start=(kc == 0), stop=(kc == 3)),
                      reads=["wos", f"oTk{tq5}"], writes=[ppn])
            P.add("vector", lambda e, pp=pp, ppn=ppn: e.tensor_copy(
                oev[:], pp[:]), reads=[ppn], writes=["oev"])
            P.dma("sync", part_ap(do, t0),
                  oev[:], reads=["oev"], writes=["ap_o"])


def stage_attn_inputs(inputs, cid):
    b, p = cid >> 1, cid & 1
    hs = slice(p * 4, (p + 1) * 4)
    wqt = np.stack([tile_lhsT(np.asarray(inputs["L1_attn_wq"])[h]).reshape(128, 1024)
                    for h in range(p * 4, p * 4 + 4)])
    wkt = np.stack([tile_lhsT(np.asarray(inputs["L1_attn_wk"])[h]).reshape(128, 1024)
                    for h in range(p * 4, p * 4 + 4)])
    wvt = np.stack([tile_lhsT(np.asarray(inputs["L1_attn_wv"])[h]).reshape(128, 1024)
                    for h in range(p * 4, p * 4 + 4)])
    wot = tile_lhsT(np.asarray(inputs["L1_attn_wo"])[p * 512:(p + 1) * 512])
    rad = np.arange(T)[None, :] * (10000.0 ** (-np.arange(32) / 32.0))[:, None]
    mask = np.zeros((128, 4, 512), np.float32)
    for v in range(4):
        i = np.arange(128)[:, None]
        j = np.arange(512)[None, :]
        mask[:, v, :] = np.where(v * 128 + i >= j, 0.0, -1e30)
    return {
        "xna": None,
        "wq_t": wqt, "wk_t": wkt, "wv_t": wvt, "wo_t": wot,
        "cos_t": np.cos(rad).astype(np.float32),
        "sin_t": np.sin(rad).astype(np.float32),
        "mask_t": mask.reshape(128, 2048).astype(BF),
        "csc_t": np.full((128, 1), 1.0 / 128.0, np.float32),
    }




def build_moe():
    nc = bass.Bass("TRN2", target_bir_lowering=False, debug=False)
    P = Prog(nc)
    xnm = nc.dram_tensor("xnm", [D, T], BF16, kind="ExternalInput")
    mp_o = nc.dram_tensor("mpart", [2, D, T // 2], BF16, kind="ExternalOutput")
    io = {
        "gw_t": nc.dram_tensor("gw_t", [128, 32], F32, kind="ExternalInput"),
        "sel_t": nc.dram_tensor("sel_t", [4, 256], F32, kind="ExternalInput"),
        "w1_t": nc.dram_tensor("w1_t", [2, 44, 128, 1024], F32, kind="ExternalInput"),
        "w3_t": nc.dram_tensor("w3_t", [2, 44, 128, 1024], F32, kind="ExternalInput"),
        "w2_t": nc.dram_tensor("w2_t", [2, 8, 128, 5632], F32, kind="ExternalInput"),
        "xn_src": lambda dc, t0, w: xnm[dc * 128:(dc + 1) * 128, t0:t0 + w],
        "part_ap": lambda do, t0: mp_o[t0 // 4096, do * 128:(do + 1) * 128,
                                       t0 % 4096: t0 % 4096 + 512],
    }
    emit_moe(P, nc, nc.alloc_sbuf_tensor, nc.alloc_psum_tensor, io)
    P.emit()
    return nc





def build_ffn1():
    nc = bass.Bass("TRN2", target_bir_lowering=False, debug=False)
    P = Prog(nc)
    xnf = nc.dram_tensor("xnf", [D, T], BF16, kind="ExternalInput")
    fp_o = nc.dram_tensor("fpart", [2, D, T // 2], BF16, kind="ExternalOutput")
    io = {
        "wg_t": nc.dram_tensor("wg_t", [12, 128, 1024], I32, kind="ExternalInput"),
        "wu_t": nc.dram_tensor("wu_t", [12, 128, 1024], I32, kind="ExternalInput"),
        "wdf_t": nc.dram_tensor("wd_t", [8, 128, 1536], I32, kind="ExternalInput"),
        "c127_t": nc.dram_tensor("c127_t", [128, 1], F32, kind="ExternalInput"),
        "xn_src": lambda dc, t0, w: xnf[dc * 128:(dc + 1) * 128, t0:t0 + w],
        "part_ap": lambda do, t0: fp_o[t0 // 4096, do * 128:(do + 1) * 128,
                                       t0 % 4096: t0 % 4096 + 512],
    }
    emit_ffn1(P, nc, nc.alloc_sbuf_tensor, nc.alloc_psum_tensor, io)
    P.emit()
    return nc


def emit_ffn1(P, nc, sbp, psp, io):
    ldq = io.get("ldq", "gpsimd")
    sb, ps = sbp, psp
    wg, wu, wdn, c127 = io["wg_t"], io["wu_t"], io["wdf_t"], io["c127_t"]
    xn_src = io["xn_src"]
    part_ap = io["part_ap"]

    wgs = sb("wgs", [128, 12 * 1024], BF16)
    wus = sb("wus", [128, 12 * 1024], BF16)
    wds = sb("wds", [128, 8 * 1536], BF16)
    c127s = sb("c127s", [128, 1], F32)
    xnb = sb("xnb", [128, 8 * 512], BF16)
    sil = sb("sil", [128, 512], F32)
    hstr = sb("hstr", [128, 12 * 512], BF16)
    pevac = sb("pevac", [128, 512], BF16)
    ps_ = [ps(f"ps{i}", [128, 512], F32) for i in range(6)]

    for i in range(12):
        P.dma(ldq, wgs[:, i * 1024:(i + 1) * 1024], wg[i, :, :],
              writes=["wgs"])
        P.dma(ldq, wus[:, i * 1024:(i + 1) * 1024], wu[i, :, :],
              writes=["wus"])
    for i in range(8):
        P.dma(ldq, wds[:, i * 1536:(i + 1) * 1536], wdn[i, :, :],
              writes=["wds"])
    P.dma("sync", c127s[:], c127[:, :], writes=["c127s"])

    for t5 in range(16):
        t0 = t5 * 512
        for dc in range(8):
            P.dma("sync", xnb[:, dc * 512:(dc + 1) * 512],
                  xn_src(dc, t0, 512), writes=["xnb"])
        for ht in range(12):
            pg, pu = ps_[ht % 2], ps_[2 + ht % 2]
            for dc in range(8):
                P.add("tensor", lambda e, ht=ht, dc=dc, pg=pg: e.matmul(
                    pg[:], wgs[:, ht * 1024 + dc * 128:
                               ht * 1024 + (dc + 1) * 128],
                    xnb[:, dc * 512:(dc + 1) * 512],
                    start=(dc == 0), stop=(dc == 7)),
                    reads=["wgs", "xnb"], writes=[f"ps{ht % 2}"])
                P.add("tensor", lambda e, ht=ht, dc=dc, pu=pu: e.matmul(
                    pu[:], wus[:, ht * 1024 + dc * 128:
                               ht * 1024 + (dc + 1) * 128],
                    xnb[:, dc * 512:(dc + 1) * 512],
                    start=(dc == 0), stop=(dc == 7)),
                    reads=["wus", "xnb"], writes=[f"ps{2 + ht % 2}"])
            P.add("scalar", lambda e, ht=ht, pg=pg: e.activation(
                sil[:], pg[:], AF.Silu, scale=c127s[:, 0:1]),
                reads=[f"ps{ht % 2}", "c127s"], writes=["sil"])
            P.add("vector", lambda e, ht=ht, pu=pu: e.scalar_tensor_tensor(
                hstr[:, ht * 512:(ht + 1) * 512], pu[:], c127s[:, 0:1],
                sil[:], ALU.mult, ALU.mult),
                reads=[f"ps{2 + ht % 2}", "c127s", "sil"], writes=["hstr"])
        for do in range(8):
            pp = ps_[4 + do % 2]
            for hc in range(12):
                P.add("tensor", lambda e, do=do, hc=hc, pp=pp: e.matmul(
                    pp[:], wds[:, do * 1536 + hc * 128:
                               do * 1536 + (hc + 1) * 128],
                    hstr[:, hc * 512:(hc + 1) * 512],
                    start=(hc == 0), stop=(hc == 11)),
                    reads=["wds", "hstr"], writes=[f"ps{4 + do % 2}"])
            P.add("vector", lambda e, pp=pp: e.tensor_scalar(
                pevac[:], pp[:], c127s[:, 0:1], None, ALU.mult),
                reads=[f"ps{4 + do % 2}", "c127s"], writes=["pevac"])
            P.dma("sync", part_ap(do, t0),
                  pevac[:], reads=["pevac"], writes=["fp_o"])


def stage_ffn1_inputs(inputs, cid):
    b, p = cid >> 1, cid & 1
    own = slice(p * 1536, (p + 1) * 1536)
    wgT = np.ascontiguousarray(np.asarray(inputs["L1_ffn_w_gate"]).T[:, own])
    wuT = np.ascontiguousarray(np.asarray(inputs["L1_ffn_w_up"]).T[:, own])
    wdT = np.ascontiguousarray(np.asarray(inputs["L1_ffn_w_down"]).T[own, :])
    return {
        "wg_t": tile_lhsT(wgT), "wu_t": tile_lhsT(wuT),
        "wd_t": tile_lhsT(wdT),
        "c127_t": np.full((128, 1), 1.0 / 127.0, np.float32),
    }


def stage_moe_inputs(inputs, cid):
    b, p = cid >> 1, cid & 1
    gw4 = np.asarray(inputs["L0_ffn_gate_w"])[0:4]
    gwt = np.ascontiguousarray(
        gw4.T.reshape(8, 128, 4).transpose(1, 0, 2).reshape(128, 32))
    sel = np.zeros((4, 256), np.float32)
    sel[p * 2, 0:128] = 1.0
    sel[p * 2 + 1, 128:256] = 1.0
    w1 = np.asarray(inputs["L0_ffn_w1"])
    w3 = np.asarray(inputs["L0_ffn_w3"])
    w2 = np.asarray(inputs["L0_ffn_w2"])
    w1t = np.stack([tile_lhsT(w1[p * 2 + i]) for i in range(2)])
    w3t = np.stack([tile_lhsT(w3[p * 2 + i]) for i in range(2)])
    w2t = np.stack([tile_lhsT(w2[p * 2 + i]) for i in range(2)])
    return {"gw_t": gwt, "sel_t": sel, "w1_t": w1t, "w3_t": w3t, "w2_t": w2t}


def tile_lhsT(w, blk_out=128):
    Din, Dout = w.shape
    dcn, ocn = Din // 128, Dout // blk_out
    return np.ascontiguousarray(
        w.reshape(dcn, 128, ocn, blk_out).transpose(2, 1, 0, 3)
        .reshape(ocn, 128, dcn * blk_out))


def vec_t(v):
    return np.ascontiguousarray(v.reshape(-1, 128).T)


def stage_s1_inputs(inputs, cid):
    b, p = cid >> 1, cid & 1
    x = inputs["L0_attn_x"][b]
    W_in = np.asarray(inputs["L0_attn_W_in"])
    cols = np.r_[p * 1024:(p + 1) * 1024, 2048 + p * 1024: 2048 + (p + 1) * 1024]
    w_half = W_in[:, cols]
    cwo = np.asarray(inputs["L0_attn_conv_w"])[p * 1024:(p + 1) * 1024, 0, :]
    return {
        "x": np.ascontiguousarray(x),
        "w_in_t": tile_lhsT(w_half),
        "lg_t": vec_t(np.asarray(inputs["L0_attn_ln_g"])),
        "lb_t": vec_t(np.asarray(inputs["L0_attn_ln_b"])),
        "cw_t": np.ascontiguousarray(
            cwo.reshape(8, 128, 4).transpose(1, 0, 2).reshape(128, 32)),
        "cb_t": vec_t(np.asarray(inputs["L0_attn_conv_b"])[p * 1024:(p + 1) * 1024]),
    }


PAIRS = [[0, 1], [2, 3], [4, 5], [6, 7]]
_FUSED = True


def build_fused():
    from contextlib import ExitStack
    nc = bass.Bass("TRN2", target_bir_lowering=False, debug=False)
    P = Prog(nc)
    dt = nc.dram_tensor
    TO = T // 2

    ext = {
        "x": dt("x", [T, D], F32, kind="ExternalInput"),
        "rp0": dt("rp0", [TO, D], F32, kind="ExternalInput"),
        "w_in_t": dt("w_in_t", [16, 128, 1024], F32, kind="ExternalInput"),
        "lg_t": dt("lg_t", [128, 8], F32, kind="ExternalInput"),
        "lb_t": dt("lb_t", [128, 8], F32, kind="ExternalInput"),
        "cw_t": dt("cw_t", [128, 32], F32, kind="ExternalInput"),
        "cb_t": dt("cb_t", [128, 8], F32, kind="ExternalInput"),
        "wd_t": dt("wd_t", [8, 128, 2048], F32, kind="ExternalInput"),
        "bd_t": dt("bd_t", [128, 8], F32, kind="ExternalInput"),
        "wb_t": dt("wb_t", [128, 256], F32, kind="ExternalInput"),
        "wc_t": dt("wc_t", [128, 256], F32, kind="ExternalInput"),
        "la_t": dt("la_t", [128, 128], F32, kind="ExternalInput"),
        "dp_t": dt("dp_t", [128, 8], F32, kind="ExternalInput"),
        "wo_t": dt("wo_t", [8, 128, 1024], F32, kind="ExternalInput"),
        "gw_t": dt("gw_t", [128, 32], F32, kind="ExternalInput"),
        "sel_t": dt("sel_t", [4, 256], F32, kind="ExternalInput"),
        "w1_t": dt("w1_t", [2, 44, 128, 1024], F32, kind="ExternalInput"),
        "w3_t": dt("w3_t", [2, 44, 128, 1024], F32, kind="ExternalInput"),
        "w2_t": dt("w2_t", [2, 8, 128, 5632], F32, kind="ExternalInput"),
        "wq_t": dt("wq_t", [4, 128, 1024], F32, kind="ExternalInput"),
        "wk_t": dt("wk_t", [4, 128, 1024], F32, kind="ExternalInput"),
        "wv_t": dt("wv_t", [4, 128, 1024], F32, kind="ExternalInput"),
        "woA_t": dt("woA_t", [8, 128, 512], F32, kind="ExternalInput"),
        "cos_t": dt("cos_t", [32, T], F32, kind="ExternalInput"),
        "sin_t": dt("sin_t", [32, T], F32, kind="ExternalInput"),
        "mask_t": dt("mask_t", [128, 4 * 512], BF16, kind="ExternalInput"),
        "csc_t": dt("csc_t", [128, 1], F32, kind="ExternalInput"),
        "wg_t": dt("wg_t", [12, 128, 1024], I32, kind="ExternalInput"),
        "wu_t": dt("wu_t", [12, 128, 1024], I32, kind="ExternalInput"),
        "wdf_t": dt("wdf_t", [8, 128, 1536], I32, kind="ExternalInput"),
        "c127_t": dt("c127_t", [128, 1], F32, kind="ExternalInput"),
    }
    xb_own = dt("xb_own", [DI // 2, T], BF16, kind="Internal")
    sz_own = dt("sz_own", [DI // 2, T], BF16, kind="Internal")
    xn0T = dt("xn0T", [D, T], BF16, kind="Internal")
    ag_xb = dt("ag_xb", [DI, T], BF16, kind="Internal")
    rs_in = dt("rs_in", [2 * D, TO], BF16, kind="Internal")
    rs_out = dt("rs_out", [D, TO], BF16, kind="Internal")
    ag_xn_in = dt("ag_xn_in", [D, TO], BF16, kind="Internal")
    ag_xn_out = dt("ag_xn_out", [2 * D, TO], BF16, kind="Internal")
    r1 = dt("r1", [TO, D], F32, kind="Internal")
    r2 = dt("r2", [TO, D], F32, kind="Internal")
    r3 = dt("r3", [TO, D], F32, kind="Internal")
    bscr = dt("bscr", [2, 16, 512], BF16, kind="Internal")
    oscr = dt("oscr", [4, 128, T], BF16, kind="Internal")
    rn_o = dt("rn", [TO, D], F32, kind="ExternalOutput")

    def part_ap(do, t0):
        return rs_in[(t0 // 4096) * D + do * 128:
                     (t0 // 4096) * D + (do + 1) * 128,
                     t0 % 4096: t0 % 4096 + 512]

    def xn_src(dc, t0, w):
        r0 = (t0 // 4096) * D + dc * 128
        c0 = t0 % 4096
        return ag_xn_out[r0:r0 + 128, c0:c0 + w]

    def xn_ap(dc, t0):
        return ag_xn_in[dc * 128:(dc + 1) * 128, t0:t0 + 128]

    stage_n = [0]

    def stage(emit_fn, io, name=None):
        pre = f"s{stage_n[0]}_"
        stage_n[0] += 1
        es = ExitStack()
        sbp = lambda n, sh, dty: es.enter_context(
            nc.sbuf_tensor(pre + n, sh, dty))
        psp = lambda n, sh, dty: es.enter_context(
            nc.psum_tensor(pre + n, sh, dty))
        with nc.named_scope(name or pre):
            emit_fn(P, nc, sbp, psp, io)
        es.close()
        P.barrier()

    def cc(kind, op, ins, outs):
        P.add("gpsimd", lambda e: e.collective_compute(
            kind, op, PAIRS, ins, outs), kind="cc")
        P.barrier()

    twin = {
        "wd_t": dt("wd_b", [8, 128, 2048], BF16, kind="Internal"),
        "wo_t": dt("wo_b", [8, 128, 1024], BF16, kind="Internal"),
        "wb_t": dt("wb_b", [128, 256], BF16, kind="Internal"),
        "wc_t": dt("wc_b", [128, 256], BF16, kind="Internal"),
        "w1_t": dt("w1_b", [2, 44, 128, 1024], BF16, kind="Internal"),
        "w3_t": dt("w3_b", [2, 44, 128, 1024], BF16, kind="Internal"),
        "w2_t": dt("w2_b", [2, 8, 128, 5632], BF16, kind="Internal"),
        "wq_t": dt("wq_b", [4, 128, 1024], BF16, kind="Internal"),
        "wk_t": dt("wk_b", [4, 128, 1024], BF16, kind="Internal"),
        "wv_t": dt("wv_b", [4, 128, 1024], BF16, kind="Internal"),
        "woA_t": dt("woA_b", [8, 128, 512], BF16, kind="Internal"),
        "wg_t": dt("wg_b", [12, 128, 1024], BF16, kind="Internal"),
        "wu_t": dt("wu_b", [12, 128, 1024], BF16, kind="Internal"),
        "wdf_t": dt("wdf_b", [8, 128, 1536], BF16, kind="Internal"),
    }

    stage(emit_s1, {"x": ext["x"], "w_in_t": ext["w_in_t"],
                    "lg_t": ext["lg_t"], "lb_t": ext["lb_t"],
                    "cw_t": ext["cw_t"], "cb_t": ext["cb_t"],
                    "xb": xb_own, "sz": sz_own, "xn0T": xn0T}, "S1")
    def conv(dst, srct, slices):
        for sl in slices:
            P.dma("gpsimd", dst[sl], srct[sl], writes=["twin"])
    conv(twin["wd_t"], ext["wd_t"], [(i,) for i in range(8)])
    conv(twin["wo_t"], ext["wo_t"], [(i,) for i in range(8)])
    conv(twin["wb_t"], ext["wb_t"], [slice(None)])
    conv(twin["wc_t"], ext["wc_t"], [slice(None)])
    for le in range(2):
        for a in range(4):
            conv(twin["w1_t"], ext["w1_t"], [(le, slice(a * 11, a * 11 + 11))])
            conv(twin["w3_t"], ext["w3_t"], [(le, slice(a * 11, a * 11 + 11))])
        for do_ in range(8):
            conv(twin["w2_t"], ext["w2_t"], [(le, do_)])
    for k in ("wq_t", "wk_t", "wv_t", "woA_t", "wg_t", "wu_t", "wdf_t"):
        conv(twin[k], ext[k], [(i,) for i in range(ext[k].shape[0])])
    cc("AllGather", ALU.bypass, [xb_own[:, :]], [ag_xb[:, :]])
    stage(emit_scan, {"xbF": ag_xb, "xbO": xb_own, "szO": sz_own,
                      "xn0T": xn0T, "wd_t": twin["wd_t"], "bd_t": ext["bd_t"],
                      "wb_t": twin["wb_t"], "wc_t": twin["wc_t"],
                      "la_t": ext["la_t"], "dp_t": ext["dp_t"],
                      "wo_t": twin["wo_t"], "bscr": bscr,
                      "part_ap": part_ap, "ldq": "sync"}, "SCAN")
    cc("ReduceScatter", ALU.add, [rs_in[:, :]], [rs_out[:, :]])
    stage(lambda P_, nc_, sbp, psp, io: emit_combine(
        P_, nc_, sbp, psp, io, True, True),
        {"psum": rs_out, "rp": ext["rp0"], "rn": r1, "xn_ap": xn_ap}, "C1")
    cc("AllGather", ALU.bypass, [ag_xn_in[:, :]], [ag_xn_out[:, :]])
    stage(emit_moe, {"gw_t": ext["gw_t"], "sel_t": ext["sel_t"],
                     "w1_t": twin["w1_t"], "w3_t": twin["w3_t"],
                     "w2_t": twin["w2_t"], "xn_src": xn_src,
                     "part_ap": part_ap, "ldq": "sync"}, "MOE")
    cc("ReduceScatter", ALU.add, [rs_in[:, :]], [rs_out[:, :]])
    stage(lambda P_, nc_, sbp, psp, io: emit_combine(
        P_, nc_, sbp, psp, io, True, True),
        {"psum": rs_out, "rp": r1, "rn": r2, "xn_ap": xn_ap}, "C2")
    cc("AllGather", ALU.bypass, [ag_xn_in[:, :]], [ag_xn_out[:, :]])
    stage(emit_attn, {"wq_t": twin["wq_t"], "wk_t": twin["wk_t"],
                      "wv_t": twin["wv_t"], "wo_t": twin["woA_t"],
                      "cos_t": ext["cos_t"], "sin_t": ext["sin_t"],
                      "mask_t": ext["mask_t"], "csc_t": ext["csc_t"],
                      "xn_src": xn_src, "part_ap": part_ap,
                      "oscr": oscr, "ldq": "sync"}, "ATTN")
    cc("ReduceScatter", ALU.add, [rs_in[:, :]], [rs_out[:, :]])
    stage(lambda P_, nc_, sbp, psp, io: emit_combine(
        P_, nc_, sbp, psp, io, True, True),
        {"psum": rs_out, "rp": r2, "rn": r3, "xn_ap": xn_ap}, "C3")
    cc("AllGather", ALU.bypass, [ag_xn_in[:, :]], [ag_xn_out[:, :]])
    stage(emit_ffn1, {"wg_t": twin["wg_t"], "wu_t": twin["wu_t"],
                      "wdf_t": twin["wdf_t"], "c127_t": ext["c127_t"],
                      "xn_src": xn_src, "part_ap": part_ap,
                      "ldq": "sync"}, "FFN")
    cc("ReduceScatter", ALU.add, [rs_in[:, :]], [rs_out[:, :]])
    stage(lambda P_, nc_, sbp, psp, io: emit_combine(
        P_, nc_, sbp, psp, io, False, True),
        {"psum": rs_out, "rp": r3, "rn": rn_o}, "C4")
    P.emit()
    return nc


def stage_fused_inputs(inputs, cid):
    b, p = cid >> 1, cid & 1
    own = slice(p * 4096, (p + 1) * 4096)
    m = {}
    m.update(stage_s1_inputs(inputs, cid))
    m.update(stage_scan_inputs(inputs, cid, None, None, None, None))
    for k in ("xbF", "xbO", "sz", "xn0"):
        m.pop(k, None)
    m.update(stage_moe_inputs(inputs, cid))
    att = stage_attn_inputs(inputs, cid)
    att.pop("xna", None)
    att["woA_t"] = att.pop("wo_t")
    m.update(att)
    ffn = stage_ffn1_inputs(inputs, cid)
    ffn["wdf_t"] = ffn.pop("wd_t")
    m.update(ffn)
    m["rp0"] = np.ascontiguousarray(np.asarray(inputs["L0_attn_x"])[b, own])
    return m


def kernel(**inputs):
    cores = list(range(8))
    if _FUSED:
        nc = build_fused()
        in_maps = [stage_fused_inputs(inputs, c) for c in cores]
        res = run_bass_kernel_spmd(nc, in_maps, core_ids=cores)
        out = np.zeros((4, 8192, 1024), np.float32)
        for c in cores:
            b, p = c >> 1, c & 1
            out[b, p * 4096:(p + 1) * 4096] = res.results[c]["rn"]
        return out
    return _kernel_unfused(inputs, cores)


def _kernel_unfused(inputs, cores):
    nc = build_s1()
    res = run_bass_kernel_spmd(nc, [stage_s1_inputs(inputs, c) for c in cores],
                               core_ids=cores)
    l1 = res.results

    nc = build_scan()
    in_maps = []
    for c in cores:
        ev, od = (c >> 1) * 2, (c >> 1) * 2 + 1
        xbf = np.ascontiguousarray(np.concatenate(
            [l1[ev]["xb"], l1[od]["xb"]], axis=0))
        in_maps.append(stage_scan_inputs(inputs, c, xbf, l1[c]["xb"],
                                         l1[c]["sz"], l1[c]["xn0"]))
    res = run_bass_kernel_spmd(nc, in_maps, core_ids=cores)
    l2 = res.results

    def combine(nc_c, parts, rps, want_xn):
        in_maps = []
        for c in cores:
            b, p = c >> 1, c & 1
            m = {"pa": np.ascontiguousarray(parts[2 * b][p]),
                 "pb": np.ascontiguousarray(parts[2 * b + 1][p]),
                 "rp": rps[c]}
            in_maps.append(m)
        return run_bass_kernel_spmd(nc_c, in_maps, core_ids=cores).results

    nc_cx = build_combine(True, False)
    nc_cl = build_combine(False, False)

    xs = np.asarray(inputs["L0_attn_x"])
    rp0 = [np.ascontiguousarray(xs[c >> 1, (c & 1) * 4096:(c & 1) * 4096 + 4096])
           for c in cores]
    l3 = combine(nc_cx, [l2[c]["part"] for c in cores], rp0, True)

    nc = build_moe()
    in_maps = []
    for c in cores:
        b = c >> 1
        m = stage_moe_inputs(inputs, c)
        m["xnm"] = np.ascontiguousarray(np.concatenate(
            [l3[2 * b]["xnx"], l3[2 * b + 1]["xnx"]], axis=1))
        in_maps.append(m)
    res = run_bass_kernel_spmd(nc, in_maps, core_ids=cores)
    l4 = res.results

    l5 = combine(nc_cx, [l4[c]["mpart"] for c in cores],
                 [l3[c]["rn"] for c in cores], True)

    nc = build_attn()
    in_maps = []
    for c in cores:
        b = c >> 1
        m = stage_attn_inputs(inputs, c)
        m["xna"] = np.ascontiguousarray(np.concatenate(
            [l5[2 * b]["xnx"], l5[2 * b + 1]["xnx"]], axis=1))
        in_maps.append(m)
    res = run_bass_kernel_spmd(nc, in_maps, core_ids=cores)
    l6 = res.results

    l7 = combine(nc_cx, [l6[c]["apart"] for c in cores],
                 [l5[c]["rn"] for c in cores], True)

    nc = build_ffn1()
    in_maps = []
    for c in cores:
        b = c >> 1
        m = stage_ffn1_inputs(inputs, c)
        m["xnf"] = np.ascontiguousarray(np.concatenate(
            [l7[2 * b]["xnx"], l7[2 * b + 1]["xnx"]], axis=1))
        in_maps.append(m)
    res = run_bass_kernel_spmd(nc, in_maps, core_ids=cores)
    l8 = res.results

    l9 = combine(nc_cl, [l8[c]["fpart"] for c in cores],
                 [l7[c]["rn"] for c in cores], False)

    out = np.zeros((4, 8192, 1024), np.float32)
    for c in cores:
        b, p = c >> 1, c & 1
        out[b, p * 4096:(p + 1) * 4096] = l9[c]["rn"]
    return out


# version 18
# speedup vs baseline: 1.1912x; 1.0006x over previous
import numpy as np
import ml_dtypes
import concourse.bass as bass
import concourse.mybir as mybir
from concourse.bass_utils import run_bass_kernel_spmd

F32 = mybir.dt.float32
BF16 = mybir.dt.bfloat16
I32 = mybir.dt.int32
AF = mybir.ActivationFunctionType
ALU = mybir.AluOpType
AX = mybir.AxisListType
BF = ml_dtypes.bfloat16

D = 1024
T = 8192
DI = 2048
NS = 16
HM = 5632
NE = 4
HF = 3072
NH = 8
DH = 128
EPS = 1e-5

ENGINES = ("sync", "tensor", "vector", "scalar", "gpsimd")


class _Op:
    __slots__ = ("engine", "fn", "reads", "writes", "kind", "idx", "waits",
                 "inc_needed", "pre_wait")

    def __init__(self, engine, fn, reads, writes, kind, idx):
        self.engine = engine
        self.fn = fn
        self.reads = list(reads)
        self.writes = list(writes)
        self.kind = kind
        self.idx = idx
        self.waits = []
        self.inc_needed = kind != "c"
        self.pre_wait = None


def _split(res):
    if "@" in res:
        name, tag = res.split("@", 1)
        return name, tag
    return res, "*"


class Prog:

    def __init__(self, nc):
        self.nc = nc
        self.ops = []
        self.seq = {}
        self.state = {}
        self.bar_waits = []
        self.dirty = {}

    def barrier(self):
        last = {}
        for op in self.ops:
            last[(op.engine, op.kind)] = op
        self.bar_waits = []
        for (e, k), op in last.items():
            op.inc_needed = True
            if k in ("d", "cc"):
                self.bar_waits.append((e, k, op.idx, self.seq[(e, k)]))
                self.dirty[(e, k)] = True
            else:
                self.bar_waits.append((e, k, op.idx, None))

    def add(self, engine, fn, reads=(), writes=(), kind="c"):
        key = (engine, kind)
        idx = self.seq.get(key, 0)
        self.seq[key] = idx + 1
        op = _Op(engine, fn, reads, writes, kind, idx)
        deps = set()

        def lookup(res):
            name, tag_ = _split(res)
            st = self.state.setdefault(name, {})
            if tag_ == "*":
                return st, list(st.keys())
            keys = [tag_]
            if "*" in st:
                keys.append("*")
            return st, keys

        war = set()
        for res in op.reads:
            st, keys = lookup(res)
            for k in keys:
                ent = st.get(k)
                if ent and ent["w"] is not None:
                    deps.add(ent["w"])
        for res in op.writes:
            st, keys = lookup(res)
            for k in keys:
                ent = st.get(k)
                if ent:
                    if ent["w"] is not None:
                        war.add(ent["w"])
                    for r in ent["r"]:
                        war.add(r)
        for res in op.reads:
            name, tag_ = _split(res)
            ent = self.state.setdefault(name, {}).setdefault(
                tag_, {"w": None, "r": []})
            ent["r"].append(op)
        for res in op.writes:
            name, tag_ = _split(res)
            st = self.state.setdefault(name, {})
            if tag_ == "*":
                st.clear()
            ent = st.setdefault(tag_, {"w": None, "r": []})
            ent["w"] = op
            ent["r"] = []

        if op.kind in ("d", "cc"):
            kk = (engine, op.kind)
            if self.dirty.get(kk):
                op.pre_wait = idx
                self.dirty[kk] = False
        for dep in war:
            if dep in deps:
                continue
            if (dep.engine == op.engine and dep.kind == "c"
                    and op.kind == "c"):
                continue
            deps.add(dep)
        for dep in deps:
            if dep is op:
                continue
            if dep.engine == op.engine and dep.kind == "d" and op.kind == "d":
                continue
            if (dep.engine == "tensor" and op.engine == "tensor"
                    and dep.kind == "c" and op.kind == "c"):
                continue
            dep.inc_needed = True
            if dep.kind in ("d", "cc"):
                kk = (dep.engine, dep.kind)
                op.waits.append((dep.engine, dep.kind, dep.idx,
                                 self.seq[kk]))
                self.dirty[kk] = True
            else:
                op.waits.append((dep.engine, dep.kind, dep.idx, None))
        for bw in self.bar_waits:
            if not (bw[0] == op.engine and bw[1] == "c" == op.kind):
                op.waits.append(bw)
        self.ops.append(op)
        return op

    def dma(self, engine, out, in_, reads=(), writes=()):
        return self.add(engine, lambda e: e.dma_start(out=out, in_=in_),
                        reads=reads, writes=writes, kind="d")

    def emit(self):
        import bisect
        nc = self.nc
        targets = {}
        for op in self.ops:
            if op.inc_needed:
                targets.setdefault((op.engine, op.kind), []).append(op.idx)
        for v in targets.values():
            v.sort()
        sem_cms, sems = [], {}
        for key in targets:
            cm = nc.semaphore(f"s_{key[0]}_{key[1]}")
            sems[key] = cm.__enter__()
            sem_cms.append(cm)

        def sig(engine, kind, idx, override=None):
            if override is not None:
                step = 16 if kind == "d" else 1
                return sems[(engine, kind)], override * step
            lst = targets[(engine, kind)]
            pos = bisect.bisect_left(lst, idx)
            rank = pos + 1
            step = 16 if kind == "d" else 1
            return sems[(engine, kind)], rank * step

        per_engine = {e: [] for e in ENGINES}
        for op in self.ops:
            per_engine[op.engine].append(op)

        inc_step = {"c": 1, "d": 16, "cc": 1}
        blk_cm = nc.Block()
        block = blk_cm.__enter__()
        for engine in ENGINES:
            ops = per_engine[engine]
            if not ops:
                continue

            def make_section(ops):
                def section(eng):
                    hw = {}
                    my_tot = {}
                    for op in ops:
                        if op.pre_wait:
                            kk = (op.engine, op.kind)
                            if kk in sems:
                                step = 16 if op.kind == "d" else 1
                                val = step * op.pre_wait
                                if hw.get(sems[kk].name, 0) < val:
                                    eng.wait_ge(sems[kk], val)
                                    hw[sems[kk].name] = val
                        for dep in op.waits:
                            sem, val = sig(*dep)
                            if hw.get(sem.name, 0) < val:
                                eng.wait_ge(sem, val)
                                hw[sem.name] = val
                        inst = op.fn(eng)
                        if op.inc_needed:
                            sem = sems[(op.engine, op.kind)]
                            step = inc_step[op.kind]
                            if op.kind == "cc":
                                inst.then_inc(sem)
                            else:
                                inst.then_inc(sem, step)
                            k = (op.engine, op.kind)
                            my_tot[k] = my_tot.get(k, 0) + step
                    for (e, k), total in my_tot.items():
                        if k in ("d", "cc"):
                            eng.wait_ge(sems[(e, k)], total)
                return section

            getattr(block, engine)(make_section(ops))
        blk_cm.__exit__(None, None, None)
        for cm in sem_cms:
            cm.__exit__(None, None, None)



def emit_s1(P, nc, sbp, psp, io):
    ldq = io.get("ldq", "gpsimd")
    x, w_in = io["x"], io["w_in_t"]
    lg, lb, cw, cb = io["lg_t"], io["lb_t"], io["cw_t"], io["cb_t"]
    xb_o, sz_o, xn0_o = io["xb"], io["sz"], io["xn0T"]

    NTH = 4
    TH = T // NTH
    sb, ps = sbp, psp
    xt = [sb(f"xt{i}", [128, D], F32) for i in range(4)]
    xmu = [sb(f"xmu{i}", [128, D], F32) for i in range(4)]
    sqb = [sb(f"sqb{i}", [128, D], F32) for i in range(4)]
    xn1 = [sb(f"xn1{i}", [128, D], F32) for i in range(4)]
    xn0b = [sb(f"xn0b{i}", [128, D], BF16) for i in range(4)]
    xn2b = [sb(f"xn2b{i}", [128, D], BF16) for i in range(4)]
    st = [sb(f"st{i}", [128, 12], F32) for i in range(4)]
    lgs = sb("lgs", [128, 8], F32)
    lbs = sb("lbs", [128, 8], F32)
    cws = sb("cws", [128, 32], F32)
    cbs = sb("cbs", [128, 8], F32)
    identB = sb("identB", [128, 128], BF16)
    xnT = sb("xnT", [128, 8 * TH], BF16)
    x0T = sb("x0T", [128, 8 * TH], BF16)
    wt = [sb(f"wt{i}", [128, 1024], BF16) for i in range(2)]
    strip = sb("strip", [128, 3 + TH], F32)
    carry = sb("carry", [128, 8 * 3], F32)
    acc = [sb(f"acc{i}", [128, TH], F32) for i in range(2)]
    xbb = sb("xbb", [128, TH], BF16)
    zbb = sb("zbb", [128, TH], BF16)
    psT = [ps(f"psT{i}", [128, 128], BF16) for i in range(2)]
    psM = [ps(f"psM{i}", [128, 512], F32) for i in range(2)]

    for t_, name, src in ((lgs, "lgs", lg), (lbs, "lbs", lb),
                          (cws, "cws", cw), (cbs, "cbs", cb)):
        P.dma("sync", t_[:], src[:, :], reads=[], writes=[name])
    P.add("gpsimd", lambda e: e.memset(identB[:], 0.0), writes=["identB"])
    P.add("gpsimd", lambda e: e.affine_select(
        out=identB[:], in_=identB[:], compare_op=ALU.not_equal, fill=1.0,
        base=0, pattern=[[-1, 128]], channel_multiplier=1),
        reads=["identB"], writes=["identB"])
    P.add("gpsimd", lambda e: e.memset(carry[:], 0.0), writes=["carry"])

    def ln_plain(q, xin, xname, xout, oname):
        s, sn = st[q], f"st{q}"
        P.add("vector", lambda e: e.reduce_sum(s[:, 0:1], xin[:], axis=AX.X),
              reads=[xname], writes=[sn + "@a"])
        P.add("vector", lambda e: e.tensor_scalar(
            s[:, 1:2], s[:, 0:1], 1.0 / D, None, ALU.mult),
            reads=[sn + "@a"], writes=[sn + "@b"])
        P.add("vector", lambda e: e.tensor_scalar(
            xmu[q][:], xin[:], s[:, 1:2], None, ALU.subtract),
            reads=[xname, sn + "@b"], writes=[f"xmu{q}"])
        P.add("scalar", lambda e: e.activation(
            sqb[q][:], xmu[q][:], AF.Square, accum_out=s[:, 2:3]),
            reads=[f"xmu{q}"], writes=[f"sqb{q}", sn + "@c"])
        P.add("vector", lambda e: e.tensor_scalar(
            s[:, 5:6], s[:, 2:3], 1.0 / D, EPS, ALU.mult, ALU.add),
            reads=[sn + "@c"], writes=[sn + "@f"])
        P.add("scalar", lambda e: e.sqrt(s[:, 3:4], s[:, 5:6]),
              reads=[sn + "@f"], writes=[sn + "@d"])
        P.add("vector", lambda e: e.reciprocal(s[:, 4:5], s[:, 3:4]),
              reads=[sn + "@d"], writes=[sn + "@e"])
        P.add("vector", lambda e: e.tensor_scalar(
            xout[:], xmu[q][:], s[:, 4:5], None, ALU.mult),
            reads=[f"xmu{q}", sn + "@e"], writes=[oname])

    for th in range(NTH):
        for tt in range(TH // 128):
            q = tt % 4
            tg = th * TH + tt * 128
            P.dma("sync", xt[q][:], x[tg:tg + 128, :],
                  reads=[], writes=[f"xt{q}"])
            ln_plain(q, xt[q], f"xt{q}", xn1[q], f"xn1{q}")
            P.add("vector", lambda e, q=q: e.tensor_copy(
                xn0b[q][:], xn1[q][:]),
                reads=[f"xn1{q}"], writes=[f"xn0b{q}"])
            ln_plain(q, xn1[q], f"xn1{q}", xn2b[q], f"xn2b{q}")
            for dc in range(8):
                col = dc * TH + tt * 128
                P.add("tensor", lambda e, q=q, dc=dc: e.transpose(
                    psT[q % 2][:], xn2b[q][:, dc * 128:(dc + 1) * 128],
                    identB[:]),
                    reads=[f"xn2b{q}", "identB"], writes=[f"psT{q % 2}"])
                P.add("vector", lambda e, q=q, dc=dc, col=col: e.tensor_scalar(
                    xnT[:, col:col + 128], psT[q % 2][:],
                    lgs[:, dc:dc + 1], lbs[:, dc:dc + 1], ALU.mult, ALU.add),
                    reads=[f"psT{q % 2}", "lgs", "lbs"], writes=["xnT"])
                P.add("tensor", lambda e, q=q, dc=dc: e.transpose(
                    psT[q % 2][:], xn0b[q][:, dc * 128:(dc + 1) * 128],
                    identB[:]),
                    reads=[f"xn0b{q}", "identB"], writes=[f"psT{q % 2}"])
                P.add("vector", lambda e, q=q, col=col: e.tensor_copy(
                    x0T[:, col:col + 128], psT[q % 2][:]),
                    reads=[f"psT{q % 2}"], writes=["x0T"])
        for dc in range(8):
            P.dma("sync", xn0_o[dc * 128:(dc + 1) * 128,
                                th * TH:(th + 1) * TH],
                  x0T[:, dc * TH:(dc + 1) * TH],
                  reads=["x0T"], writes=["xn0_o"])

        for oc in range(16):
            q = oc % 2
            P.dma(ldq, wt[q][:], w_in[oc, :, :],
                  reads=[], writes=[f"wt{q}"])
            for t5 in range(TH // 512):
                pq = t5 % 2
                for dc in range(8):
                    P.add("tensor",
                          lambda e, q=q, pq=pq, dc=dc, t5=t5: e.matmul(
                              psM[pq][:], wt[q][:, dc * 128:(dc + 1) * 128],
                              xnT[:, dc * TH + t5 * 512: dc * TH + t5 * 512 + 512],
                              start=(dc == 0), stop=(dc == 7)),
                          reads=[f"wt{q}", "xnT"], writes=[f"psM{pq}"])
                if oc < 8:
                    P.add("vector", lambda e, pq=pq, t5=t5: e.tensor_copy(
                        strip[:, 3 + t5 * 512: 3 + t5 * 512 + 512],
                        psM[pq][:]),
                        reads=[f"psM{pq}"], writes=["strip"])
                else:
                    P.add("scalar", lambda e, pq=pq, t5=t5: e.activation(
                        zbb[:, t5 * 512: t5 * 512 + 512], psM[pq][:],
                        AF.Silu),
                        reads=[f"psM{pq}"], writes=["zbb"])
            if oc < 8:
                if th == 0:
                    P.add("gpsimd", lambda e: e.memset(strip[:, 0:3], 0.0),
                          reads=[], writes=["strip"])
                else:
                    P.add("vector", lambda e, oc=oc: e.tensor_copy(
                        strip[:, 0:3], carry[:, oc * 3: oc * 3 + 3]),
                        reads=["carry"], writes=["strip"])
                P.add("vector", lambda e, oc=oc: e.tensor_copy(
                    carry[:, oc * 3: oc * 3 + 3], strip[:, TH: TH + 3]),
                    reads=["strip"], writes=["carry"])
                P.add("vector", lambda e, oc=oc: e.tensor_scalar(
                    acc[0][:], strip[:, 0: TH], cws[:, oc * 4: oc * 4 + 1],
                    None, ALU.mult),
                    reads=["strip", "cws"], writes=["acc0"])
                cur = 0
                for j in range(1, 4):
                    nxt = 1 - cur
                    P.add("vector",
                          lambda e, oc=oc, j=j, cur=cur, nxt=nxt:
                          e.scalar_tensor_tensor(
                              acc[nxt][:], strip[:, j: j + TH],
                              cws[:, oc * 4 + j: oc * 4 + j + 1],
                              acc[cur][:], ALU.mult, ALU.add),
                          reads=["strip", "cws", f"acc{cur}"],
                          writes=[f"acc{nxt}"])
                    cur = nxt
                P.add("scalar", lambda e, oc=oc, cur=cur: e.activation(
                    xbb[:], acc[cur][:], AF.Silu, bias=cbs[:, oc:oc + 1]),
                    reads=[f"acc{cur}", "cbs"], writes=["xbb"])
                P.dma("sync", xb_o[oc * 128:(oc + 1) * 128,
                                   th * TH:(th + 1) * TH], xbb[:],
                      reads=["xbb"], writes=["xb_o"])
            else:
                P.dma("sync", sz_o[(oc - 8) * 128:(oc - 7) * 128,
                                   th * TH:(th + 1) * TH], zbb[:],
                      reads=["zbb"], writes=["sz_o"])


def build_s1(p=0):
    nc = bass.Bass("TRN2", target_bir_lowering=False, debug=False)
    P = Prog(nc)
    io = {
        "x": nc.dram_tensor("x", [T, D], F32, kind="ExternalInput"),
        "w_in_t": nc.dram_tensor("w_in_t", [16, 128, 1024], F32,
                                 kind="ExternalInput"),
        "lg_t": nc.dram_tensor("lg_t", [128, 8], F32, kind="ExternalInput"),
        "lb_t": nc.dram_tensor("lb_t", [128, 8], F32, kind="ExternalInput"),
        "cw_t": nc.dram_tensor("cw_t", [128, 32], F32, kind="ExternalInput"),
        "cb_t": nc.dram_tensor("cb_t", [128, 8], F32, kind="ExternalInput"),
        "xb": nc.dram_tensor("xb", [DI // 2, T], BF16, kind="ExternalOutput"),
        "sz": nc.dram_tensor("sz", [DI // 2, T], BF16, kind="ExternalOutput"),
        "xn0T": nc.dram_tensor("xn0", [D, T], BF16, kind="ExternalOutput"),
    }
    emit_s1(P, nc, nc.alloc_sbuf_tensor, nc.alloc_psum_tensor, io)
    P.emit()
    return nc



def emit_scan(P, nc, sbp, psp, io):
    ldq = io.get("ldq", "gpsimd")
    xbF, xbO, szO, x0T = io["xbF"], io["xbO"], io["szO"], io["xn0T"]
    wd, bd, wb, wc = io["wd_t"], io["bd_t"], io["wb_t"], io["wc_t"]
    la, dp, wo = io["la_t"], io["dp_t"], io["wo_t"]
    part_ap = io["part_ap"]
    bscr = io["bscr"]

    TB = 512
    NTB = T // TB
    sb, ps = sbp, psp
    xbB = [sb(f"xbB{i}", [128, 16 * TB], BF16) for i in range(2)]
    xboB = sb("xboB", [128, 8 * TB], BF16)
    szB = sb("szB", [128, 8 * TB], BF16)
    x0B = [sb(f"x0B{i}", [128, 512], BF16) for i in range(2)]
    dsb = [sb(f"dsb{i}", [128, 8 * TB], F32) for i in range(2)]
    dxb = [sb(f"dxb{i}", [128, 8 * TB], BF16) for i in range(2)]
    yB = sb("yB", [128, 8 * TB], F32)
    PBs = [sb(f"PBs{i}", [128, TB], BF16) for i in range(2)]
    PCs = [sb(f"PCs{i}", [128, TB], BF16) for i in range(2)]
    aT = [sb(f"aT{i}", [128, TB], F32) for i in range(2)] * 2
    bxT = [sb(f"bxT{i}", [128, TB], BF16) for i in range(4)]
    hT = [sb(f"hT{i}", [128, TB], BF16) for i in range(4)]
    hCT = [sb(f"hCT{i}", [128, TB], BF16) for i in range(4)]
    gB = sb("gB", [128, 8 * TB], BF16)
    pevac = [sb(f"pevac{i}", [128, 512], BF16) for i in range(2)]
    wds = sb("wds", [128, 8 * 2048], BF16)
    wos = sb("wos", [128, 8 * 1024], BF16)
    wbs = sb("wbs", [128, 256], BF16)
    wcs = sb("wcs", [128, 256], BF16)
    bds = sb("bds", [128, 8], F32)
    dps = sb("dps", [128, 8], F32)
    asb = sb("asb", [128, 128], F32)
    seeds = sb("seeds", [128, 128], F32)
    ones1 = sb("ones1", [1, 128], BF16)
    brow = [sb(f"brow{i}", [1, TB], BF16) for i in range(2)]
    bsb = sb("bsb", [16, TB], BF16)
    csb = sb("csb", [16, TB], BF16)
    spA = [sb(f"spA{i}", [128, 512], F32) for i in range(2)]
    spB = [sb(f"spB{i}", [128, 512], F32) for i in range(2)]
    ps_ = [ps(f"ps{i}", [128, 512], F32) for i in range(6)]

    for i in range(8):
        P.dma(ldq, wds[:, i * 2048:(i + 1) * 2048], wd[i, :, :],
              writes=["wds"])
        P.dma(ldq, wos[:, i * 1024:(i + 1) * 1024], wo[i, :, :],
              writes=["wos"])
    P.dma(ldq, wbs[:], wb[:, :], writes=["wbs"])
    P.dma(ldq, wcs[:], wc[:, :], writes=["wcs"])
    P.dma("sync", bds[:], bd[:, :], writes=["bds"])
    P.dma("sync", dps[:], dp[:, :], writes=["dps"])
    P.dma("sync", asb[:], la[:, :], writes=["asb"])
    P.add("scalar", lambda e: e.activation(asb[:], asb[:], AF.Exp),
          reads=["asb"], writes=["asb"])
    P.add("vector", lambda e: e.tensor_scalar(
        asb[:], asb[:], -1.0, None, ALU.mult), reads=["asb"], writes=["asb"])
    P.add("gpsimd", lambda e: e.memset(seeds[:], 0.0), writes=["seeds"])
    P.add("gpsimd", lambda e: e.memset(ones1[:], 1.0), writes=["ones1"])

    for tb in range(NTB):
        t0 = tb * TB
        tq = tb % 2
        xbBt, dsbt, dxbt = xbB[tq], dsb[tq], dxb[tq]
        for ch in range(16):
            P.dma("sync", xbBt[:, ch * TB:(ch + 1) * TB],
                  xbF[ch * 128:(ch + 1) * 128, t0:t0 + TB], writes=[f"xbB{tq}"])
        for ch in range(8):
            P.dma("sync", xboB[:, ch * TB:(ch + 1) * TB],
                  xbO[ch * 128:(ch + 1) * 128, t0:t0 + TB], writes=["xboB"])
            P.dma("sync", szB[:, ch * TB:(ch + 1) * TB],
                  szO[ch * 128:(ch + 1) * 128, t0:t0 + TB], writes=["szB"])

        for which, wsb, wname, osb, oname in (
                (0, wbs, "wbs", bsb, "bsb"), (1, wcs, "wcs", csb, "csb")):
            pp = ps_[4]
            for dc in range(16):
                P.add("tensor", lambda e, wsb=wsb, dc=dc, pp=pp, xbBt=xbBt, dsbt=dsbt, dxbt=dxbt, tq=tq:
                      e.matmul(pp[0:16, :], wsb[:, dc * 16:(dc + 1) * 16],
                               xbBt[:, dc * TB: dc * TB + TB],
                               start=(dc == 0), stop=(dc == 15)),
                      reads=[wname, f"xbB{tq}"], writes=["ps4"])
            P.add("vector", lambda e, osb=osb, pp=pp, xbBt=xbBt, dsbt=dsbt, dxbt=dxbt, tq=tq:
                  e.tensor_copy(osb[:], pp[0:16, :]),
                  reads=["ps4"], writes=[oname])
            P.dma("sync", bscr[which, :, :], osb[:], reads=[oname],
                  writes=[f"bscr{which}"])

        for oc in range(8):
            pp = ps_[oc % 2]
            for dc in range(16):
                P.add("tensor", lambda e, oc=oc, dc=dc, pp=pp, xbBt=xbBt, dsbt=dsbt, dxbt=dxbt, tq=tq:
                      e.matmul(pp[:],
                               wds[:, oc * 2048 + dc * 128:
                                   oc * 2048 + (dc + 1) * 128],
                               xbBt[:, dc * TB: dc * TB + TB],
                               start=(dc == 0), stop=(dc == 15)),
                      reads=["wds", f"xbB{tq}"], writes=[f"ps{oc % 2}"])
            q = oc % 2
            P.add("scalar", lambda e, oc=oc, pp=pp, q=q, xbBt=xbBt, dsbt=dsbt, dxbt=dxbt, tq=tq: e.activation(
                spA[q][:], pp[:], AF.Abs, bias=bds[:, oc:oc + 1]),
                reads=[f"ps{oc % 2}", "bds"], writes=[f"spA{q}"])
            P.add("scalar", lambda e, q=q, xbBt=xbBt, dsbt=dsbt, dxbt=dxbt, tq=tq: e.activation(
                spA[q][:], spA[q][:], AF.Exp, scale=-1.0),
                reads=[f"spA{q}"], writes=[f"spA{q}"])
            P.add("scalar", lambda e, q=q, xbBt=xbBt, dsbt=dsbt, dxbt=dxbt, tq=tq: e.activation(
                spA[q][:], spA[q][:], AF.Ln, bias=1.0),
                reads=[f"spA{q}"], writes=[f"spA{q}"])
            P.add("scalar", lambda e, oc=oc, pp=pp, q=q, xbBt=xbBt, dsbt=dsbt, dxbt=dxbt, tq=tq: e.activation(
                spB[q][:], pp[:], AF.Relu, bias=bds[:, oc:oc + 1]),
                reads=[f"ps{oc % 2}", "bds"], writes=[f"spB{q}"])
            P.add("vector", lambda e, oc=oc, q=q, xbBt=xbBt, dsbt=dsbt, dxbt=dxbt, tq=tq: e.tensor_add(
                dsbt[:, oc * TB:(oc + 1) * TB], spA[q][:], spB[q][:]),
                reads=[f"spA{q}", f"spB{q}"], writes=[f"dsb{tq}"])
            P.add("vector", lambda e, oc=oc, xbBt=xbBt, dsbt=dsbt, dxbt=dxbt, tq=tq: e.tensor_mul(
                dxbt[:, oc * TB:(oc + 1) * TB], dsbt[:, oc * TB:(oc + 1) * TB],
                xboB[:, oc * TB:(oc + 1) * TB]),
                reads=[f"dsb{tq}", "xboB"], writes=[f"dxb{tq}"])

        for n in range(NS):
            nq = n % 2
            for which, plane, pname in ((0, PBs[nq], f"PBs{nq}"),
                                        (1, PCs[nq], f"PCs{nq}")):
                P.dma("sync", brow[which][:], bscr[which, n:n + 1, :],
                      reads=[f"bscr{which}"], writes=[f"brow{which}"])
                pp = ps_[2 + which]
                P.add("tensor", lambda e, which=which, pp=pp, xbBt=xbBt, dsbt=dsbt, dxbt=dxbt, tq=tq:
                      e.matmul(pp[:], ones1[:], brow[which][0:1, :],
                               start=True, stop=True),
                      reads=[f"brow{which}", "ones1"],
                      writes=[f"ps{2 + which}"])
                P.add("scalar", lambda e, plane=plane, pp=pp, xbBt=xbBt, dsbt=dsbt, dxbt=dxbt, tq=tq:
                      e.activation(plane[:], pp[:], AF.Copy),
                      reads=[f"ps{2 + which}"], writes=[pname])
            for oc in range(8):
                q = (n * 8 + oc) % 4
                ncol = oc * 16 + n
                P.add("scalar", lambda e, oc=oc, ncol=ncol, q=q, xbBt=xbBt, dsbt=dsbt, dxbt=dxbt, tq=tq: e.activation(
                    aT[q][:], dsbt[:, oc * TB:(oc + 1) * TB], AF.Exp,
                    scale=asb[:, ncol:ncol + 1]),
                    reads=[f"dsb{tq}", "asb"], writes=[f"aT{q % 2}"])
                P.add("vector", lambda e, oc=oc, q=q, nq=nq, xbBt=xbBt, dsbt=dsbt, dxbt=dxbt, tq=tq: e.tensor_mul(
                    bxT[q][:], dxbt[:, oc * TB:(oc + 1) * TB], PBs[nq][:]),
                    reads=[f"dxb{tq}", f"PBs{nq}"], writes=[f"bxT{q}"])
                P.add("vector", lambda e, ncol=ncol, q=q: e.tensor_tensor_scan(
                    hT[q][:], aT[q][:], bxT[q][:], seeds[:, ncol:ncol + 1],
                    ALU.mult, ALU.add),
                    reads=[f"aT{q % 2}", f"bxT{q}", f"seeds@{ncol}"],
                    writes=[f"hT{q}"])
                P.add("scalar", lambda e, ncol=ncol, q=q, xbBt=xbBt, dsbt=dsbt, dxbt=dxbt, tq=tq: e.copy(
                    seeds[:, ncol:ncol + 1], hT[q][:, TB - 1:TB]),
                    reads=[f"hT{q}"], writes=[f"seeds@{ncol}"])
                if n == 0:
                    P.add("vector", lambda e, oc=oc, q=q, nq=nq, xbBt=xbBt, dsbt=dsbt, dxbt=dxbt, tq=tq: e.tensor_mul(
                        yB[:, oc * TB:(oc + 1) * TB], hT[q][:], PCs[nq][:]),
                        reads=[f"hT{q}", f"PCs{nq}"], writes=[f"yB@{oc}"])
                else:
                    P.add("vector", lambda e, q=q, nq=nq, xbBt=xbBt, dsbt=dsbt, dxbt=dxbt, tq=tq: e.tensor_mul(
                        hCT[q][:], hT[q][:], PCs[nq][:]),
                        reads=[f"hT{q}", f"PCs{nq}"], writes=[f"hCT{q}"])
                    P.add("gpsimd", lambda e, oc=oc, q=q: e.tensor_add(
                        yB[:, oc * TB:(oc + 1) * TB],
                        yB[:, oc * TB:(oc + 1) * TB], hCT[q][:]),
                        reads=[f"hCT{q}", f"yB@{oc}"], writes=[f"yB@{oc}"])

        for oc in range(8):
            q = oc % 2
            P.add("vector", lambda e, oc=oc, q=q, xbBt=xbBt, dsbt=dsbt, dxbt=dxbt, tq=tq: e.scalar_tensor_tensor(
                hCT[q][:], xboB[:, oc * TB:(oc + 1) * TB], dps[:, oc:oc + 1],
                yB[:, oc * TB:(oc + 1) * TB], ALU.mult, ALU.add),
                reads=["xboB", "dps", f"yB@{oc}"], writes=[f"hCT{q}"])
            P.add("vector", lambda e, oc=oc, q=q, xbBt=xbBt, dsbt=dsbt, dxbt=dxbt, tq=tq: e.tensor_mul(
                gB[:, oc * TB:(oc + 1) * TB], hCT[q][:],
                szB[:, oc * TB:(oc + 1) * TB]),
                reads=[f"hCT{q}", "szB"], writes=["gB"])

        for do in range(8):
            pp = ps_[4 + do % 2]
            P.dma("sync", x0B[do % 2][:],
                  x0T[do * 128:(do + 1) * 128, t0:t0 + TB],
                  writes=[f"x0B{do % 2}"])
            for dc in range(8):
                P.add("tensor", lambda e, do=do, dc=dc, pp=pp, xbBt=xbBt, dsbt=dsbt, dxbt=dxbt, tq=tq:
                      e.matmul(pp[:],
                               wos[:, do * 1024 + dc * 128:
                                   do * 1024 + (dc + 1) * 128],
                               gB[:, dc * TB: dc * TB + TB],
                               start=(dc == 0), stop=(dc == 7)),
                      reads=["wos", "gB"], writes=[f"ps{4 + do % 2}"])
            P.add("vector", lambda e, pp=pp, do=do, xbBt=xbBt, dsbt=dsbt, dxbt=dxbt, tq=tq: e.scalar_tensor_tensor(
                pevac[do % 2][:], x0B[do % 2][:], 0.5, pp[:],
                ALU.mult, ALU.add),
                reads=[f"ps{4 + do % 2}", f"x0B{do % 2}"],
                writes=[f"pevac{do % 2}"])
            P.dma("sync", part_ap(do, t0), pevac[do % 2][:],
                  reads=[f"pevac{do % 2}"], writes=["part_o"])


def build_scan():
    nc = bass.Bass("TRN2", target_bir_lowering=False, debug=False)
    P = Prog(nc)
    io = {
        "xbF": nc.dram_tensor("xbF", [DI, T], BF16, kind="ExternalInput"),
        "xbO": nc.dram_tensor("xbO", [DI // 2, T], BF16, kind="ExternalInput"),
        "szO": nc.dram_tensor("sz", [DI // 2, T], BF16, kind="ExternalInput"),
        "xn0T": nc.dram_tensor("xn0", [D, T], BF16, kind="ExternalInput"),
        "wd_t": nc.dram_tensor("wd_t", [8, 128, 2048], F32, kind="ExternalInput"),
        "bd_t": nc.dram_tensor("bd_t", [128, 8], F32, kind="ExternalInput"),
        "wb_t": nc.dram_tensor("wb_t", [128, 256], F32, kind="ExternalInput"),
        "wc_t": nc.dram_tensor("wc_t", [128, 256], F32, kind="ExternalInput"),
        "la_t": nc.dram_tensor("la_t", [128, 128], F32, kind="ExternalInput"),
        "dp_t": nc.dram_tensor("dp_t", [128, 8], F32, kind="ExternalInput"),
        "wo_t": nc.dram_tensor("wo_t", [8, 128, 1024], F32, kind="ExternalInput"),
        "bscr": nc.dram_tensor("bscr", [2, 16, 512], BF16, kind="Internal"),
    }
    part = nc.dram_tensor("part", [2, D, T // 2], BF16, kind="ExternalOutput")
    io["part_ap"] = lambda do, t0: part[t0 // 4096, do * 128:(do + 1) * 128,
                                        t0 % 4096: t0 % 4096 + 512]
    emit_scan(P, nc, nc.alloc_sbuf_tensor, nc.alloc_psum_tensor, io)
    P.emit()
    return nc


def stage_scan_inputs(inputs, cid, xb_even_first, xb_own, sz_own, xn0T):
    b, p = cid >> 1, cid & 1
    own = slice(p * 1024, (p + 1) * 1024)
    Wd = np.asarray(inputs["L0_attn_W_delta"])[:, own]
    Wo = np.asarray(inputs["L0_attn_W_out"])[own]
    return {
        "xbF": xb_even_first,
        "xbO": xb_own,
        "sz": sz_own,
        "xn0": xn0T,
        "wd_t": tile_lhsT(Wd).reshape(8, 128, 2048),
        "bd_t": vec_t(np.asarray(inputs["L0_attn_b_delta"])[own]),
        "wb_t": np.ascontiguousarray(
            np.asarray(inputs["L0_attn_W_b"]).reshape(16, 128, 16)
            .transpose(1, 0, 2).reshape(128, 256)),
        "wc_t": np.ascontiguousarray(
            np.asarray(inputs["L0_attn_W_c"]).reshape(16, 128, 16)
            .transpose(1, 0, 2).reshape(128, 256)),
        "la_t": np.ascontiguousarray(
            np.asarray(inputs["L0_attn_log_A"])[own].reshape(8, 128, 16)
            .transpose(1, 0, 2).reshape(128, 128)),
        "dp_t": vec_t(np.asarray(inputs["L0_attn_Dp"])[own]),
        "wo_t": tile_lhsT(Wo).reshape(8, 128, 1024),
    }



def emit_combine(P, nc, sbp, psp, io, want_xn, summed):
    TO = T // 2
    rp = io["rp"]
    rn_o = io["rn"]
    sb, ps = sbp, psp

    pat = [sb(f"pat{i}", [128, 128], BF16) for i in range(4)]
    pbt = [sb(f"pbt{i}", [128, 128], BF16) for i in range(4)]
    sumb = [sb(f"sumb{i}", [128, 128], BF16) for i in range(4)]
    mo = [sb(f"mo{i}", [128, D], F32) for i in range(4)]
    rt = [sb(f"rt{i}", [128, D], F32) for i in range(4)]
    xmu = [sb(f"xmu{i}", [128, D], F32) for i in range(4)]
    sqb = [sb(f"sqb{i}", [128, D], F32) for i in range(4)]
    lnv = [sb(f"lnv{i}", [128, D], F32) for i in range(4)]
    rnb = [sb(f"rnb{i}", [128, D], F32) for i in range(4)]
    xnb = [sb(f"xnb{i}", [128, D], BF16) for i in range(4)]
    st = [sb(f"st{i}", [128, 12], F32) for i in range(4)]
    identB = sb("identB", [128, 128], BF16)
    psT = [ps(f"psT{i}", [128, 128], BF16) for i in range(2)]

    P.add("gpsimd", lambda e: e.memset(identB[:], 0.0), writes=["identB"])
    P.add("gpsimd", lambda e: e.affine_select(
        out=identB[:], in_=identB[:], compare_op=ALU.not_equal, fill=1.0,
        base=0, pattern=[[-1, 128]], channel_multiplier=1),
        reads=["identB"], writes=["identB"])

    def ln_plain(q, xin, xname, xout, oname):
        s, sn = st[q], f"st{q}"
        P.add("vector", lambda e: e.reduce_sum(s[:, 0:1], xin[:], axis=AX.X),
              reads=[xname], writes=[sn + "@a"])
        P.add("vector", lambda e: e.tensor_scalar(
            s[:, 1:2], s[:, 0:1], 1.0 / D, None, ALU.mult),
            reads=[sn + "@a"], writes=[sn + "@b"])
        P.add("vector", lambda e: e.tensor_scalar(
            xmu[q][:], xin[:], s[:, 1:2], None, ALU.subtract),
            reads=[xname, sn + "@b"], writes=[f"xmu{q}"])
        P.add("scalar", lambda e: e.activation(
            sqb[q][:], xmu[q][:], AF.Square, accum_out=s[:, 2:3]),
            reads=[f"xmu{q}"], writes=[f"sqb{q}", sn + "@c"])
        P.add("vector", lambda e: e.tensor_scalar(
            s[:, 5:6], s[:, 2:3], 1.0 / D, EPS, ALU.mult, ALU.add),
            reads=[sn + "@c"], writes=[sn + "@f"])
        P.add("scalar", lambda e: e.sqrt(s[:, 3:4], s[:, 5:6]),
              reads=[sn + "@f"], writes=[sn + "@d"])
        P.add("vector", lambda e: e.reciprocal(s[:, 4:5], s[:, 3:4]),
              reads=[sn + "@d"], writes=[sn + "@e"])
        P.add("vector", lambda e: e.tensor_scalar(
            xout[:], xmu[q][:], s[:, 4:5], None, ALU.mult),
            reads=[f"xmu{q}", sn + "@e"], writes=[oname])

    for tt in range(TO // 128):
        q = tt % 4
        t0 = tt * 128
        for dc in range(8):
            dq = (tt * 8 + dc) % 4
            if summed:
                pa = io["psum"]
                P.dma("sync", pat[dq][:],
                      pa[dc * 128:(dc + 1) * 128, t0:t0 + 128],
                      writes=[f"pat{dq}"])
                P.add("tensor", lambda e, q=q, dq=dq: e.transpose(
                    psT[q % 2][:], pat[dq][:], identB[:]),
                    reads=[f"pat{dq}", "identB"], writes=[f"psT{q % 2}"])
            else:
                P.dma("sync", pat[dq][:],
                      io["pa"][dc * 128:(dc + 1) * 128, t0:t0 + 128],
                      writes=[f"pat{dq}"])
                P.dma("sync", pbt[dq][:],
                      io["pb"][dc * 128:(dc + 1) * 128, t0:t0 + 128],
                      writes=[f"pbt{dq}"])
                P.add("vector", lambda e, dq=dq: e.tensor_add(
                    sumb[dq][:], pat[dq][:], pbt[dq][:]),
                    reads=[f"pat{dq}", f"pbt{dq}"], writes=[f"sumb{dq}"])
                P.add("tensor", lambda e, q=q, dq=dq: e.transpose(
                    psT[q % 2][:], sumb[dq][:], identB[:]),
                    reads=[f"sumb{dq}", "identB"], writes=[f"psT{q % 2}"])
            P.add("vector", lambda e, q=q, dc=dc: e.tensor_copy(
                mo[q][:, dc * 128:(dc + 1) * 128], psT[q % 2][:]),
                reads=[f"psT{q % 2}"], writes=[f"mo{q}"])
        ln_plain(q, mo[q], f"mo{q}", lnv[q], f"lnv{q}")
        P.dma("sync", rt[q][:], rp[t0:t0 + 128, :], writes=[f"rt{q}"])
        P.add("vector", lambda e, q=q: e.tensor_add(
            rnb[q][:], rt[q][:], lnv[q][:]),
            reads=[f"rt{q}", f"lnv{q}"], writes=[f"rnb{q}"])
        P.dma("sync", rn_o[t0:t0 + 128, :], rnb[q][:],
              reads=[f"rnb{q}"], writes=["rn_o"])
        if want_xn:
            xn_ap = io["xn_ap"]
            ln_plain(q, rnb[q], f"rnb{q}", xnb[q], f"xnb{q}")
            for dc in range(8):
                dq2 = (tt * 8 + dc) % 4
                P.add("tensor", lambda e, q=q, dc=dc: e.transpose(
                    psT[q % 2][:], xnb[q][:, dc * 128:(dc + 1) * 128],
                    identB[:]),
                    reads=[f"xnb{q}", "identB"], writes=[f"psT{q % 2}"])
                P.add("vector", lambda e, q=q, dq2=dq2: e.tensor_copy(
                    sumb[dq2][:], psT[q % 2][:]),
                    reads=[f"psT{q % 2}"], writes=[f"sumb{dq2}"])
                P.dma("sync", xn_ap(dc, t0), sumb[dq2][:],
                      reads=[f"sumb{dq2}"], writes=["xn_o"])


def build_combine(want_xn, summed=False):
    nc = bass.Bass("TRN2", target_bir_lowering=False, debug=False)
    P = Prog(nc)
    TO = T // 2
    io = {"rp": nc.dram_tensor("rp", [TO, D], F32, kind="ExternalInput"),
          "rn": nc.dram_tensor("rn", [TO, D], F32, kind="ExternalOutput")}
    if summed:
        io["psum"] = nc.dram_tensor("psm", [D, TO], BF16, kind="ExternalInput")
    else:
        io["pa"] = nc.dram_tensor("pa", [D, TO], BF16, kind="ExternalInput")
        io["pb"] = nc.dram_tensor("pb", [D, TO], BF16, kind="ExternalInput")
    if want_xn:
        xn_o = nc.dram_tensor("xnx", [D, TO], BF16, kind="ExternalOutput")
        io["xn_ap"] = lambda dc, t0: xn_o[dc * 128:(dc + 1) * 128, t0:t0 + 128]
    emit_combine(P, nc, nc.alloc_sbuf_tensor, nc.alloc_psum_tensor, io,
                 want_xn, summed)
    P.emit()
    return nc



def emit_moe(P, nc, sbp, psp, io):
    ldq = io.get("ldq", "gpsimd")
    sb, ps = sbp, psp
    gw, sel = io["gw_t"], io["sel_t"]
    w1, w3, w2 = io["w1_t"], io["w3_t"], io["w2_t"]
    xn_src = io["xn_src"]
    part_ap = io["part_ap"]

    TBK = 2048
    NBK = T // TBK
    SUP = (8, 8, 8, 8, 8, 4)

    gws = sb("gws", [128, 32], F32)
    xng = sb("xng", [128, 8 * 128], F32)
    sels = sb("sels", [4, 256], F32)
    xnB = sb("xnB", [128, 8 * TBK], BF16)
    lg4 = sb("lg4", [128, 4], F32)
    cw4 = sb("cw4", [128, 4], F32)
    cwT = sb("cwT", [4, T], F32)
    pl = [sb(f"pl{i}", [128, TBK], F32) for i in range(2)]
    sil = [sb(f"sil{i}", [128, 512], F32) for i in range(2)]
    hstr = sb("hstr", [128, 8 * TBK], BF16)
    oac = sb("oac", [128, 8 * TBK], F32)
    tmp5 = [sb(f"tmp5{i}", [128, 512], F32) for i in range(2)]
    pevac = [sb(f"pevac{i}", [128, 512], BF16) for i in range(2)]
    w13s = [sb(f"w13s{i}", [128, 1024], BF16) for i in range(4)]
    w2s = [sb(f"w2s{i}", [128, 1024], BF16) for i in range(2)]
    stm = sb("stm", [128, 10], F32)
    cmpb = sb("cmpb", [128, 4], F32)
    identF = sb("identF", [128, 128], F32)
    ps_ = [ps(f"ps{i}", [128, 512], F32) for i in range(8)]

    P.dma("sync", gws[:], gw[:, :], writes=["gws"])
    P.dma("sync", sels[:], sel[:, :], writes=["sels"])
    P.add("gpsimd", lambda e: e.memset(identF[:], 0.0), writes=["identF"])
    P.add("gpsimd", lambda e: e.affine_select(
        out=identF[:], in_=identF[:], compare_op=ALU.not_equal, fill=1.0,
        base=0, pattern=[[-1, 128]], channel_multiplier=1),
        reads=["identF"], writes=["identF"])

    for tt in range(T // 128):
        t0 = tt * 128
        pp = ps_[0]
        for dc in range(8):
            P.dma("gpsimd", xng[:, dc * 128:(dc + 1) * 128],
                  xn_src(dc, t0, 128), writes=["xng"])
        for dc in range(8):
            P.add("tensor", lambda e, dc=dc, pp=pp: e.matmul(
                pp[0:128, 0:4], xng[:, dc * 128:(dc + 1) * 128],
                gws[:, dc * 4:(dc + 1) * 4],
                start=(dc == 0), stop=(dc == 7)),
                reads=["xng", "gws"], writes=["ps0"])
        P.add("vector", lambda e, pp=pp: e.tensor_copy(lg4[:], pp[0:128, 0:4]),
              reads=["ps0"], writes=["lg4"])
        P.add("gpsimd", lambda e: e.memset(cw4[:], 0.0), writes=["cw4"])
        for e_ in range(4):
            for j in range(4):
                if j == e_:
                    continue
                op = ALU.is_ge if j < e_ else ALU.is_gt
                P.add("vector", lambda e, e_=e_, j=j, op=op: e.tensor_tensor(
                    cmpb[:, e_:e_ + 1], lg4[:, j:j + 1], lg4[:, e_:e_ + 1], op),
                    reads=["lg4"], writes=[f"cmpb@{e_}"])
                P.add("vector", lambda e, e_=e_: e.tensor_add(
                    cw4[:, e_:e_ + 1], cw4[:, e_:e_ + 1], cmpb[:, e_:e_ + 1]),
                    reads=[f"cmpb@{e_}"], writes=["cw4"])
        P.add("vector", lambda e: e.tensor_scalar(
            cw4[:], cw4[:], 1.5, None, ALU.is_lt),
            reads=["cw4"], writes=["cw4"])
        P.add("vector", lambda e: e.reduce_max(
            stm[:, 0:1], lg4[:], axis=AX.X, negate=True),
            reads=["lg4"], writes=["stm@a"])
        P.add("scalar", lambda e: e.activation(
            cmpb[:], lg4[:], AF.Exp, bias=stm[:, 0:1]),
            reads=["lg4", "stm@a"], writes=["cmpb"])
        P.add("vector", lambda e: e.tensor_mul(cw4[:], cw4[:], cmpb[:]),
              reads=["cw4", "cmpb"], writes=["cw4"])
        P.add("vector", lambda e: e.reduce_sum(
            stm[:, 1:2], cw4[:], axis=AX.X),
            reads=["cw4"], writes=["stm@b"])
        P.add("vector", lambda e: e.reciprocal(stm[:, 2:3], stm[:, 1:2]),
              reads=["stm@b"], writes=["stm@c"])
        P.add("vector", lambda e: e.tensor_scalar(
            cw4[:], cw4[:], stm[:, 2:3], None, ALU.mult),
            reads=["cw4", "stm@c"], writes=["cw4"])
        P.add("tensor", lambda e, pp=pp: e.transpose(
            pp[0:4, 0:128], cw4[:], identF[:]),
            reads=["cw4", "identF"], writes=["ps0"])
        P.add("vector", lambda e, t0=t0, pp=pp: e.tensor_copy(
            cwT[:, t0:t0 + 128], pp[0:4, 0:128]),
            reads=["ps0"], writes=["cwT"])

    for bk in range(NBK):
        t0 = bk * TBK
        for dc in range(8):
            P.dma("sync", xnB[:, dc * TBK:(dc + 1) * TBK],
                  xn_src(dc, t0, TBK), writes=["xnB"])
        for le in range(2):
            for t5 in range(4):
                pp = ps_[5]
                P.add("tensor", lambda e, le=le, t5=t5, t0=t0, pp=pp:
                      e.matmul(pp[:],
                               sels[:, le * 128:(le + 1) * 128],
                               cwT[:, t0 + t5 * 512: t0 + t5 * 512 + 512],
                               start=True, stop=True),
                      reads=["sels", "cwT"], writes=["ps5"])
                P.add("vector", lambda e, le=le, t5=t5, pp=pp: e.tensor_copy(
                    pl[le][:, t5 * 512: t5 * 512 + 512], pp[:]),
                    reads=["ps5"], writes=[f"pl{le}"])
        first = True
        for le in range(2):
            hbase = 0
            for si, sn_ in enumerate(SUP):
                for hi in range(sn_):
                    ht = hbase + hi
                    qa, qb = (ht % 2) * 2, (ht % 2) * 2 + 1
                    P.dma(ldq, w13s[qa][:], w1[le, ht, :, :],
                          writes=[f"w13s{qa}"])
                    P.dma(ldq, w13s[qb][:], w3[le, ht, :, :],
                          writes=[f"w13s{qb}"])
                    for t5 in range(4):
                        gi, ui = t5 % 2, 2 + t5 % 2
                        for dc in range(8):
                            P.add("tensor",
                                  lambda e, qa=qa, dc=dc, t5=t5, gi=gi:
                                  e.matmul(ps_[gi][:],
                                           w13s[qa][:, dc * 128:(dc + 1) * 128],
                                           xnB[:, dc * TBK + t5 * 512:
                                               dc * TBK + t5 * 512 + 512],
                                           start=(dc == 0), stop=(dc == 7)),
                                  reads=[f"w13s{qa}", "xnB"],
                                  writes=[f"ps{gi}"])
                            P.add("tensor",
                                  lambda e, qb=qb, dc=dc, t5=t5, ui=ui:
                                  e.matmul(ps_[ui][:],
                                           w13s[qb][:, dc * 128:(dc + 1) * 128],
                                           xnB[:, dc * TBK + t5 * 512:
                                               dc * TBK + t5 * 512 + 512],
                                           start=(dc == 0), stop=(dc == 7)),
                                  reads=[f"w13s{qb}", "xnB"],
                                  writes=[f"ps{ui}"])
                        P.add("scalar", lambda e, t5=t5, gi=gi: e.activation(
                            sil[t5 % 2][:], ps_[gi][:], AF.Silu),
                            reads=[f"ps{gi}"], writes=[f"sil{t5 % 2}"])
                        P.add("vector",
                              lambda e, hi=hi, t5=t5, ui=ui: e.tensor_mul(
                                  hstr[:, hi * TBK + t5 * 512:
                                       hi * TBK + t5 * 512 + 512],
                                  sil[t5 % 2][:], ps_[ui][:]),
                              reads=[f"sil{t5 % 2}", f"ps{ui}"],
                              writes=["hstr"])
                for do in range(8):
                    qd = do % 2
                    P.dma(ldq, w2s[qd][:, 0:sn_ * 128],
                          w2[le, do, :, hbase * 128:(hbase + sn_) * 128],
                          writes=[f"w2s{qd}"])
                    for t5 in range(4):
                        pi = 4 + (qd * 2 + t5 % 2) % 4
                        for hc in range(sn_):
                            P.add("tensor",
                                  lambda e, qd=qd, hc=hc, t5=t5, pi=pi:
                                  e.matmul(ps_[pi][:],
                                           w2s[qd][:, hc * 128:(hc + 1) * 128],
                                           hstr[:, hc * TBK + t5 * 512:
                                                hc * TBK + t5 * 512 + 512],
                                           start=(hc == 0), stop=(hc == sn_ - 1)),
                                  reads=[f"w2s{qd}", "hstr"],
                                  writes=[f"ps{pi}"])
                        P.add("vector",
                              lambda e, le=le, t5=t5, pi=pi: e.tensor_mul(
                                  tmp5[t5 % 2][:], ps_[pi][:],
                                  pl[le][:, t5 * 512: t5 * 512 + 512]),
                              reads=[f"ps{pi}", f"pl{le}"],
                              writes=[f"tmp5{t5 % 2}"])
                        if first:
                            P.add("gpsimd",
                                  lambda e, do=do, t5=t5: e.tensor_copy(
                                      oac[:, do * TBK + t5 * 512:
                                          do * TBK + t5 * 512 + 512],
                                      tmp5[t5 % 2][:]),
                                  reads=[f"tmp5{t5 % 2}"],
                                  writes=[f"oac@{do}_{t5}"])
                        else:
                            P.add("gpsimd",
                                  lambda e, do=do, t5=t5: e.tensor_add(
                                      oac[:, do * TBK + t5 * 512:
                                          do * TBK + t5 * 512 + 512],
                                      oac[:, do * TBK + t5 * 512:
                                          do * TBK + t5 * 512 + 512],
                                      tmp5[t5 % 2][:]),
                                  reads=[f"tmp5{t5 % 2}", f"oac@{do}_{t5}"],
                                  writes=[f"oac@{do}_{t5}"])
                hbase += sn_
                first = False
        for do in range(8):
            for t5 in range(4):
                P.add("vector", lambda e, do=do, t5=t5: e.tensor_copy(
                    pevac[t5 % 2][:], oac[:, do * TBK + t5 * 512:
                                          do * TBK + t5 * 512 + 512]),
                    reads=[f"oac@{do}_{t5}"], writes=[f"pevac{t5 % 2}"])
                P.dma("sync", part_ap(do, t0 + t5 * 512),
                      pevac[t5 % 2][:], reads=[f"pevac{t5 % 2}"],
                      writes=["mp_o"])





def build_attn():
    nc = bass.Bass("TRN2", target_bir_lowering=False, debug=False)
    P = Prog(nc)
    xna = nc.dram_tensor("xna", [D, T], BF16, kind="ExternalInput")
    ap_o = nc.dram_tensor("apart", [2, D, T // 2], BF16, kind="ExternalOutput")
    io = {
        "oscr": nc.dram_tensor("oscr", [4, 128, T], BF16, kind="Internal"),
        "wq_t": nc.dram_tensor("wq_t", [4, 128, 1024], F32, kind="ExternalInput"),
        "wk_t": nc.dram_tensor("wk_t", [4, 128, 1024], F32, kind="ExternalInput"),
        "wv_t": nc.dram_tensor("wv_t", [4, 128, 1024], F32, kind="ExternalInput"),
        "wo_t": nc.dram_tensor("wo_t", [8, 128, 512], F32, kind="ExternalInput"),
        "cos_t": nc.dram_tensor("cos_t", [32, T], F32, kind="ExternalInput"),
        "sin_t": nc.dram_tensor("sin_t", [32, T], F32, kind="ExternalInput"),
        "mask_t": nc.dram_tensor("mask_t", [128, 4 * 512], BF16, kind="ExternalInput"),
        "csc_t": nc.dram_tensor("csc_t", [128, 1], F32, kind="ExternalInput"),
        "xn_src": lambda dc, t0, w: xna[dc * 128:(dc + 1) * 128, t0:t0 + w],
        "part_ap": lambda do, t0: ap_o[t0 // 4096, do * 128:(do + 1) * 128,
                                       t0 % 4096: t0 % 4096 + 512],
    }
    emit_attn(P, nc, nc.alloc_sbuf_tensor, nc.alloc_psum_tensor, io)
    P.emit()
    return nc


def emit_attn(P, nc, sbp, psp, io):
    ldq = io.get("ldq", "gpsimd")
    oscr = io["oscr"]
    sb, ps = sbp, psp
    wq, wk, wv, wo = io["wq_t"], io["wk_t"], io["wv_t"], io["wo_t"]
    cosd, sind = io["cos_t"], io["sin_t"]
    maskd, cscd = io["mask_t"], io["csc_t"]
    xn_src = io["xn_src"]
    part_ap = io["part_ap"]

    QT = sb("QT", [128, T], BF16)
    KT = sb("KT", [128, T], BF16)
    VSB = sb("VSB", [128, T], BF16)
    oT4 = [sb(f"oT4_{i}", [128, 512], BF16) for i in range(2)]
    Sstr = [sb(f"Sstr{i}", [128, T], F32) for i in range(2)]
    Pstr = [sb("Pstr0", [128, T], BF16)] * 2
    xnb = sb("xnb", [128, 8 * 512], BF16)
    wqs = sb("wqs", [128, 4 * 1024], BF16)
    wks = sb("wks", [128, 4 * 1024], BF16)
    wvs = sb("wvs", [128, 4 * 1024], BF16)
    wos = sb("wos", [128, 8 * 512], BF16)
    coss = sb("coss", [32, 512], F32)
    sins = sb("sins", [32, 512], F32)
    masks = sb("masks", [128, 4 * 512], BF16)
    cscs = sb("cscs", [128, 1], F32)
    identB = sb("identB", [128, 128], BF16)
    identF = sb("identF", [128, 128], F32)
    rtmp = [sb(f"rtmp{i}", [32, 512], F32) for i in range(4)]
    PTs = [sb(f"PTs{i}", [128, 1024], BF16) for i in range(2)]
    lac = [sb(f"lac{i}", [128, 4], F32) for i in range(2)]
    rowred = [sb(f"rowred{i}", [128, 1], F32) for i in range(2)]
    lrs = sb("lrs", [1, 128], F32)
    nmrow = [sb(f"nmrow{i}", [128, 64], F32) for i in range(2)]
    lpls = sb("lpls", [128, 128], F32)
    ones1 = sb("ones1", [1, 128], F32)
    oev = sb("oev", [128, 512], BF16)
    psS = ps("psS", [128, 512], F32)
    psS2 = ps("psS2", [128, 512], F32)
    psPV_ = [ps(f"psPV{i}", [128, 512], F32) for i in range(2)]
    psAux = ps("psAux", [128, 512], F32)
    psQ = ps("psQ", [128, 512], F32)
    psTb = [ps(f"psTb{i}", [128, 1024], BF16) for i in range(2)]

    for tens, name, src_, n_ in ((wqs, "wqs", wq, 4), (wks, "wks", wk, 4),
                                 (wvs, "wvs", wv, 4)):
        for h in range(n_):
            P.dma(ldq, tens[:, h * 1024:(h + 1) * 1024], src_[h, :, :],
                  writes=[name])
    for do in range(8):
        P.dma(ldq, wos[:, do * 512:(do + 1) * 512], wo[do, :, :],
              writes=["wos"])
    P.dma("sync", masks[:], maskd[:, :], writes=["masks"])
    P.dma("sync", cscs[:], cscd[:, :], writes=["cscs"])
    P.add("gpsimd", lambda e: e.memset(identB[:], 0.0), writes=["identB"])
    P.add("gpsimd", lambda e: e.affine_select(
        out=identB[:], in_=identB[:], compare_op=ALU.not_equal, fill=1.0,
        base=0, pattern=[[-1, 128]], channel_multiplier=1),
        reads=["identB"], writes=["identB"])
    P.add("gpsimd", lambda e: e.memset(identF[:], 0.0), writes=["identF"])
    P.add("gpsimd", lambda e: e.affine_select(
        out=identF[:], in_=identF[:], compare_op=ALU.not_equal, fill=1.0,
        base=0, pattern=[[-1, 128]], channel_multiplier=1),
        reads=["identF"], writes=["identF"])
    P.add("gpsimd", lambda e: e.memset(ones1[:], 1.0), writes=["ones1"])

    def rope_evac(head, t5, pp, ppname, dst, dname):
        c0 = t5 * 512
        P.add("vector", lambda e: e.tensor_mul(
            rtmp[0][:], pp[0:32, :], coss[:, :]),
            reads=[ppname, "coss"], writes=["rtmp0"])
        P.add("vector", lambda e: e.tensor_mul(
            rtmp[1][:], pp[32:64, :], sins[:, :]),
            reads=[ppname, "sins"], writes=["rtmp1"])
        P.add("vector", lambda e: e.tensor_mul(
            rtmp[2][:], pp[0:32, :], sins[:, :]),
            reads=[ppname, "sins"], writes=["rtmp2"])
        P.add("vector", lambda e: e.tensor_mul(
            rtmp[3][:], pp[32:64, :], coss[:, :]),
            reads=[ppname, "coss"], writes=["rtmp3"])
        P.add("vector", lambda e: e.tensor_sub(
            dst[0:32, c0:c0 + 512], rtmp[0][:], rtmp[1][:]),
            reads=["rtmp0", "rtmp1"], writes=[dname])
        P.add("vector", lambda e: e.tensor_add(
            dst[32:64, c0:c0 + 512], rtmp[2][:], rtmp[3][:]),
            reads=["rtmp2", "rtmp3"], writes=[dname])
        P.add("vector", lambda e: e.tensor_copy(
            dst[64:128, c0:c0 + 512], pp[64:128, :]),
            reads=[ppname], writes=[dname])

    NKT = T // 512
    for h in range(4):
        for t5 in range(16):
            t0 = t5 * 512
            P.dma("sync", coss[:], cosd[:, t0:t0 + 512], writes=["coss"])
            P.dma("sync", sins[:], sind[:, t0:t0 + 512], writes=["sins"])
            for dc in range(8):
                P.dma("sync", xnb[:, dc * 512:(dc + 1) * 512],
                      xn_src(dc, t0, 512),
                      writes=["xnb"])
            for wset, wname, pp, ppname in ((wks, "wks", psQ, "psQ"),):
                for dc in range(8):
                    P.add("tensor", lambda e, h=h, dc=dc, wset=wset, pp=pp:
                          e.matmul(pp[:],
                                   wset[:, h * 1024 + dc * 128:
                                        h * 1024 + (dc + 1) * 128],
                                   xnb[:, dc * 512:(dc + 1) * 512],
                                   start=(dc == 0), stop=(dc == 7)),
                          reads=[wname, "xnb"], writes=[ppname])
                rope_evac(h, t5, pp, ppname, KT, "KT")
            for dc in range(8):
                P.add("tensor", lambda e, h=h, dc=dc: e.matmul(
                    psQ[:], wqs[:, h * 1024 + dc * 128:
                                h * 1024 + (dc + 1) * 128],
                    xnb[:, dc * 512:(dc + 1) * 512],
                    start=(dc == 0), stop=(dc == 7)),
                    reads=["wqs", "xnb"], writes=["psQ"])
            rope_evac(h, t5, psQ, "psQ", QT, "QT")
            for tt in range(4):
                for dc in range(8):
                    P.add("tensor", lambda e, h=h, dc=dc, tt=tt: e.matmul(
                        psAux[:, 0:128],
                        xnb[:, dc * 512 + tt * 128: dc * 512 + tt * 128 + 128],
                        wvs[:, h * 1024 + dc * 128: h * 1024 + (dc + 1) * 128],
                        start=(dc == 0), stop=(dc == 7)),
                        reads=["wvs", "xnb"], writes=["psAux"])
                P.add("vector", lambda e, t5=t5, tt=tt: e.tensor_copy(
                    VSB[:, (t5 * 4 + tt) * 128: (t5 * 4 + tt) * 128 + 128],
                    psAux[:, 0:128]),
                    reads=["psAux"], writes=["VSB"])

        for qt in range(T // 128):
            ktmax = qt // 4
            nkt = ktmax + 1
            qq = qt % 2
            lacq = lac[qq]
            psPV, psPVn = psPV_[qq], f"psPV{qq}"
            Ss, Ssn = Sstr[qq], f"Sstr{qq}"
            Ps, Psn = Pstr[0], "Pstr0"
            SB4 = [(psS, "psS"), (psQ, "psQ"), (psS2, "psS2")]
            for kt in range(nkt):
                pS, pn = SB4[kt % 3]
                P.add("tensor", lambda e, qt=qt, kt=kt, pS=pS: e.matmul(
                    pS[:], QT[:, qt * 128:(qt + 1) * 128],
                    KT[:, kt * 512:(kt + 1) * 512], start=True, stop=True),
                    reads=["QT", "KT"], writes=[pn])
                if kt == ktmax:
                    v = qt % 4
                    P.add("vector", lambda e, v=v, pS=pS, kt=kt, Ss=Ss:
                          e.tensor_add(Ss[:, kt * 512:(kt + 1) * 512],
                                       pS[:], masks[:, v * 512:(v + 1) * 512]),
                          reads=[pn, "masks"], writes=[Ssn])
                elif kt % 2 == 0:
                    P.add("vector", lambda e, pS=pS, kt=kt, Ss=Ss:
                          e.tensor_copy(Ss[:, kt * 512:(kt + 1) * 512], pS[:]),
                          reads=[pn], writes=[Ssn])
                else:
                    P.add("scalar", lambda e, pS=pS, kt=kt, Ss=Ss:
                          e.activation(Ss[:, kt * 512:(kt + 1) * 512], pS[:],
                                       AF.Copy),
                          reads=[pn], writes=[Ssn])
            P.add("vector", lambda e, Ss=Ss, nkt=nkt, lacq=lacq: e.reduce_max(
                lacq[:, 1:2], Ss[:, 0:nkt * 512], axis=AX.X, negate=True),
                reads=[Ssn], writes=[f"lac{qq}@nm"])
            P.add("vector", lambda e, lacq=lacq: e.tensor_scalar(
                lacq[:, 2:3], lacq[:, 1:2], cscs[:, 0:1], None, ALU.mult),
                reads=[f"lac{qq}@nm", "cscs"], writes=[f"lac{qq}@bias"])
            P.add("scalar", lambda e, Ss=Ss, Ps=Ps, nkt=nkt, lacq=lacq:
                  e.activation(Ps[:, 0:nkt * 512], Ss[:, 0:nkt * 512], AF.Exp,
                               scale=cscs[:, 0:1], bias=lacq[:, 2:3],
                               accum_out=lacq[:, 0:1]),
                  reads=[Ssn, "cscs", f"lac{qq}@bias"],
                  writes=[Psn, f"lac{qq}@l"])
            for k0 in range(0, nkt, 2):
                pq2 = (k0 // 2) % 2
                kts = [kt for kt in (k0, k0 + 1) if kt < nkt]
                for kt in kts:
                    half = (kt % 2) * 512
                    for j in range(4):
                        P.add("tensor", lambda e, j=j, kt=kt, pq2=pq2,
                              half=half, Ps=Ps: e.transpose(
                                  psTb[pq2][:, half + j * 128:
                                             half + (j + 1) * 128],
                                  Ps[:, kt * 512 + j * 128:
                                     kt * 512 + (j + 1) * 128],
                                  identB[:]),
                              reads=[Psn, "identB"], writes=[f"psTb{pq2}"])
                w = len(kts) * 512
                P.add("vector", lambda e, pq2=pq2, w=w: e.tensor_copy(
                    PTs[pq2][:, 0:w], psTb[pq2][:, 0:w]),
                    reads=[f"psTb{pq2}"], writes=[f"PTs{pq2}"])
                for kt in kts:
                    half = (kt % 2) * 512
                    for j in range(4):
                        P.add("tensor", lambda e, qt=qt, kt=kt, j=j, pq2=pq2,
                              half=half, psPV=psPV:
                              e.matmul(
                            psPV[:, 0:128],
                            VSB[:, (kt * 4 + j) * 128:
                                (kt * 4 + j) * 128 + 128],
                            PTs[pq2][:, half + j * 128: half + (j + 1) * 128],
                            start=(kt == 0 and j == 0),
                            stop=(kt == qt // 4 and j == 3)),
                            reads=["VSB", f"PTs{pq2}"], writes=[psPVn])
            P.add("vector", lambda e, lacq=lacq: e.reciprocal(
                lacq[:, 3:4], lacq[:, 0:1]),
                reads=[f"lac{qq}@l"], writes=[f"lac{qq}@linv"])
            P.add("tensor", lambda e, lacq=lacq: e.transpose(
                psAux[0:1, 0:128], lacq[:, 3:4], identF[:]),
                reads=[f"lac{qq}@linv", "identF"], writes=["psAux"])
            P.add("vector", lambda e: e.tensor_copy(lrs[:], psAux[0:1, 0:128]),
                  reads=["psAux"], writes=["lrs"])
            P.add("tensor", lambda e: e.matmul(
                psAux[:, 0:128], ones1[:], lrs[:], start=True, stop=True),
                reads=["ones1", "lrs"], writes=["psAux"])
            P.add("vector", lambda e: e.tensor_copy(lpls[:], psAux[:, 0:128]),
                  reads=["psAux"], writes=["lpls"])
            og = (qt // 4) % 2
            P.add("vector", lambda e, qt=qt, og=og, psPV=psPV: e.tensor_mul(
                oT4[og][:, (qt % 4) * 128: (qt % 4) * 128 + 128],
                psPV[:, 0:128], lpls[:]),
                reads=[psPVn, "lpls"], writes=[f"oT4_{og}@{qt % 4}"])
            if qt % 4 == 3:
                P.dma("sync", oscr[h, :, (qt - 3) * 128:(qt + 1) * 128],
                      oT4[og][:],
                      reads=[f"oT4_{og}"], writes=["oscr"])

    oTk = [sb(f"oTk{i}", [128, 4 * 512], BF16) for i in range(2)]
    for t5 in range(16):
        t0 = t5 * 512
        tq5 = t5 % 2
        for kc in range(4):
            P.dma("sync", oTk[tq5][:, kc * 512:(kc + 1) * 512],
                  oscr[kc, :, t0:t0 + 512], writes=[f"oTk{tq5}"])
        for do in range(8):
            pp = psS if do % 2 == 0 else psQ
            ppn = "psS" if do % 2 == 0 else "psQ"
            for kc in range(4):
                P.add("tensor", lambda e, do=do, kc=kc, t0=t0, pp=pp, tq5=tq5:
                      e.matmul(pp[:],
                               wos[:, do * 512 + kc * 128:
                                   do * 512 + (kc + 1) * 128],
                               oTk[tq5][:, kc * 512:(kc + 1) * 512],
                               start=(kc == 0), stop=(kc == 3)),
                      reads=["wos", f"oTk{tq5}"], writes=[ppn])
            P.add("vector", lambda e, pp=pp, ppn=ppn: e.tensor_copy(
                oev[:], pp[:]), reads=[ppn], writes=["oev"])
            P.dma("sync", part_ap(do, t0),
                  oev[:], reads=["oev"], writes=["ap_o"])


def stage_attn_inputs(inputs, cid):
    b, p = cid >> 1, cid & 1
    hs = slice(p * 4, (p + 1) * 4)
    wqt = np.stack([tile_lhsT(np.asarray(inputs["L1_attn_wq"])[h]).reshape(128, 1024)
                    for h in range(p * 4, p * 4 + 4)])
    wkt = np.stack([tile_lhsT(np.asarray(inputs["L1_attn_wk"])[h]).reshape(128, 1024)
                    for h in range(p * 4, p * 4 + 4)])
    wvt = np.stack([tile_lhsT(np.asarray(inputs["L1_attn_wv"])[h]).reshape(128, 1024)
                    for h in range(p * 4, p * 4 + 4)])
    wot = tile_lhsT(np.asarray(inputs["L1_attn_wo"])[p * 512:(p + 1) * 512])
    rad = np.arange(T)[None, :] * (10000.0 ** (-np.arange(32) / 32.0))[:, None]
    mask = np.zeros((128, 4, 512), np.float32)
    for v in range(4):
        i = np.arange(128)[:, None]
        j = np.arange(512)[None, :]
        mask[:, v, :] = np.where(v * 128 + i >= j, 0.0, -1e30)
    return {
        "xna": None,
        "wq_t": wqt, "wk_t": wkt, "wv_t": wvt, "wo_t": wot,
        "cos_t": np.cos(rad).astype(np.float32),
        "sin_t": np.sin(rad).astype(np.float32),
        "mask_t": mask.reshape(128, 2048).astype(BF),
        "csc_t": np.full((128, 1), 1.0 / 128.0, np.float32),
    }




def build_moe():
    nc = bass.Bass("TRN2", target_bir_lowering=False, debug=False)
    P = Prog(nc)
    xnm = nc.dram_tensor("xnm", [D, T], BF16, kind="ExternalInput")
    mp_o = nc.dram_tensor("mpart", [2, D, T // 2], BF16, kind="ExternalOutput")
    io = {
        "gw_t": nc.dram_tensor("gw_t", [128, 32], F32, kind="ExternalInput"),
        "sel_t": nc.dram_tensor("sel_t", [4, 256], F32, kind="ExternalInput"),
        "w1_t": nc.dram_tensor("w1_t", [2, 44, 128, 1024], F32, kind="ExternalInput"),
        "w3_t": nc.dram_tensor("w3_t", [2, 44, 128, 1024], F32, kind="ExternalInput"),
        "w2_t": nc.dram_tensor("w2_t", [2, 8, 128, 5632], F32, kind="ExternalInput"),
        "xn_src": lambda dc, t0, w: xnm[dc * 128:(dc + 1) * 128, t0:t0 + w],
        "part_ap": lambda do, t0: mp_o[t0 // 4096, do * 128:(do + 1) * 128,
                                       t0 % 4096: t0 % 4096 + 512],
    }
    emit_moe(P, nc, nc.alloc_sbuf_tensor, nc.alloc_psum_tensor, io)
    P.emit()
    return nc





def build_ffn1():
    nc = bass.Bass("TRN2", target_bir_lowering=False, debug=False)
    P = Prog(nc)
    xnf = nc.dram_tensor("xnf", [D, T], BF16, kind="ExternalInput")
    fp_o = nc.dram_tensor("fpart", [2, D, T // 2], BF16, kind="ExternalOutput")
    io = {
        "wg_t": nc.dram_tensor("wg_t", [12, 128, 1024], I32, kind="ExternalInput"),
        "wu_t": nc.dram_tensor("wu_t", [12, 128, 1024], I32, kind="ExternalInput"),
        "wdf_t": nc.dram_tensor("wd_t", [8, 128, 1536], I32, kind="ExternalInput"),
        "c127_t": nc.dram_tensor("c127_t", [128, 1], F32, kind="ExternalInput"),
        "xn_src": lambda dc, t0, w: xnf[dc * 128:(dc + 1) * 128, t0:t0 + w],
        "part_ap": lambda do, t0: fp_o[t0 // 4096, do * 128:(do + 1) * 128,
                                       t0 % 4096: t0 % 4096 + 512],
    }
    emit_ffn1(P, nc, nc.alloc_sbuf_tensor, nc.alloc_psum_tensor, io)
    P.emit()
    return nc


def emit_ffn1(P, nc, sbp, psp, io):
    ldq = io.get("ldq", "gpsimd")
    sb, ps = sbp, psp
    wg, wu, wdn, c127 = io["wg_t"], io["wu_t"], io["wdf_t"], io["c127_t"]
    xn_src = io["xn_src"]
    part_ap = io["part_ap"]

    wgs = sb("wgs", [128, 12 * 1024], BF16)
    wus = sb("wus", [128, 12 * 1024], BF16)
    wds = sb("wds", [128, 8 * 1536], BF16)
    c127s = sb("c127s", [128, 1], F32)
    xnb = sb("xnb", [128, 8 * 512], BF16)
    sil = sb("sil", [128, 512], F32)
    hstr = sb("hstr", [128, 12 * 512], BF16)
    pevac = sb("pevac", [128, 512], BF16)
    ps_ = [ps(f"ps{i}", [128, 512], F32) for i in range(6)]

    for i in range(12):
        P.dma(ldq, wgs[:, i * 1024:(i + 1) * 1024], wg[i, :, :],
              writes=["wgs"])
        P.dma(ldq, wus[:, i * 1024:(i + 1) * 1024], wu[i, :, :],
              writes=["wus"])
    for i in range(8):
        P.dma(ldq, wds[:, i * 1536:(i + 1) * 1536], wdn[i, :, :],
              writes=["wds"])
    P.dma("sync", c127s[:], c127[:, :], writes=["c127s"])

    for t5 in range(16):
        t0 = t5 * 512
        for dc in range(8):
            P.dma("sync", xnb[:, dc * 512:(dc + 1) * 512],
                  xn_src(dc, t0, 512), writes=["xnb"])
        for ht in range(12):
            pg, pu = ps_[ht % 2], ps_[2 + ht % 2]
            for dc in range(8):
                P.add("tensor", lambda e, ht=ht, dc=dc, pg=pg: e.matmul(
                    pg[:], wgs[:, ht * 1024 + dc * 128:
                               ht * 1024 + (dc + 1) * 128],
                    xnb[:, dc * 512:(dc + 1) * 512],
                    start=(dc == 0), stop=(dc == 7)),
                    reads=["wgs", "xnb"], writes=[f"ps{ht % 2}"])
                P.add("tensor", lambda e, ht=ht, dc=dc, pu=pu: e.matmul(
                    pu[:], wus[:, ht * 1024 + dc * 128:
                               ht * 1024 + (dc + 1) * 128],
                    xnb[:, dc * 512:(dc + 1) * 512],
                    start=(dc == 0), stop=(dc == 7)),
                    reads=["wus", "xnb"], writes=[f"ps{2 + ht % 2}"])
            P.add("scalar", lambda e, ht=ht, pg=pg: e.activation(
                sil[:], pg[:], AF.Silu, scale=c127s[:, 0:1]),
                reads=[f"ps{ht % 2}", "c127s"], writes=["sil"])
            P.add("vector", lambda e, ht=ht, pu=pu: e.scalar_tensor_tensor(
                hstr[:, ht * 512:(ht + 1) * 512], pu[:], c127s[:, 0:1],
                sil[:], ALU.mult, ALU.mult),
                reads=[f"ps{2 + ht % 2}", "c127s", "sil"], writes=["hstr"])
        for do in range(8):
            pp = ps_[4 + do % 2]
            for hc in range(12):
                P.add("tensor", lambda e, do=do, hc=hc, pp=pp: e.matmul(
                    pp[:], wds[:, do * 1536 + hc * 128:
                               do * 1536 + (hc + 1) * 128],
                    hstr[:, hc * 512:(hc + 1) * 512],
                    start=(hc == 0), stop=(hc == 11)),
                    reads=["wds", "hstr"], writes=[f"ps{4 + do % 2}"])
            P.add("vector", lambda e, pp=pp: e.tensor_scalar(
                pevac[:], pp[:], c127s[:, 0:1], None, ALU.mult),
                reads=[f"ps{4 + do % 2}", "c127s"], writes=["pevac"])
            P.dma("sync", part_ap(do, t0),
                  pevac[:], reads=["pevac"], writes=["fp_o"])


def stage_ffn1_inputs(inputs, cid):
    b, p = cid >> 1, cid & 1
    own = slice(p * 1536, (p + 1) * 1536)
    wgT = np.ascontiguousarray(np.asarray(inputs["L1_ffn_w_gate"]).T[:, own])
    wuT = np.ascontiguousarray(np.asarray(inputs["L1_ffn_w_up"]).T[:, own])
    wdT = np.ascontiguousarray(np.asarray(inputs["L1_ffn_w_down"]).T[own, :])
    return {
        "wg_t": tile_lhsT(wgT), "wu_t": tile_lhsT(wuT),
        "wd_t": tile_lhsT(wdT),
        "c127_t": np.full((128, 1), 1.0 / 127.0, np.float32),
    }


def stage_moe_inputs(inputs, cid):
    b, p = cid >> 1, cid & 1
    gw4 = np.asarray(inputs["L0_ffn_gate_w"])[0:4]
    gwt = np.ascontiguousarray(
        gw4.T.reshape(8, 128, 4).transpose(1, 0, 2).reshape(128, 32))
    sel = np.zeros((4, 256), np.float32)
    sel[p * 2, 0:128] = 1.0
    sel[p * 2 + 1, 128:256] = 1.0
    w1 = np.asarray(inputs["L0_ffn_w1"])
    w3 = np.asarray(inputs["L0_ffn_w3"])
    w2 = np.asarray(inputs["L0_ffn_w2"])
    w1t = np.stack([tile_lhsT(w1[p * 2 + i]) for i in range(2)])
    w3t = np.stack([tile_lhsT(w3[p * 2 + i]) for i in range(2)])
    w2t = np.stack([tile_lhsT(w2[p * 2 + i]) for i in range(2)])
    return {"gw_t": gwt, "sel_t": sel, "w1_t": w1t, "w3_t": w3t, "w2_t": w2t}


def tile_lhsT(w, blk_out=128):
    Din, Dout = w.shape
    dcn, ocn = Din // 128, Dout // blk_out
    return np.ascontiguousarray(
        w.reshape(dcn, 128, ocn, blk_out).transpose(2, 1, 0, 3)
        .reshape(ocn, 128, dcn * blk_out))


def vec_t(v):
    return np.ascontiguousarray(v.reshape(-1, 128).T)


def stage_s1_inputs(inputs, cid):
    b, p = cid >> 1, cid & 1
    x = inputs["L0_attn_x"][b]
    W_in = np.asarray(inputs["L0_attn_W_in"])
    cols = np.r_[p * 1024:(p + 1) * 1024, 2048 + p * 1024: 2048 + (p + 1) * 1024]
    w_half = W_in[:, cols]
    cwo = np.asarray(inputs["L0_attn_conv_w"])[p * 1024:(p + 1) * 1024, 0, :]
    return {
        "x": np.ascontiguousarray(x),
        "w_in_t": tile_lhsT(w_half),
        "lg_t": vec_t(np.asarray(inputs["L0_attn_ln_g"])),
        "lb_t": vec_t(np.asarray(inputs["L0_attn_ln_b"])),
        "cw_t": np.ascontiguousarray(
            cwo.reshape(8, 128, 4).transpose(1, 0, 2).reshape(128, 32)),
        "cb_t": vec_t(np.asarray(inputs["L0_attn_conv_b"])[p * 1024:(p + 1) * 1024]),
    }


PAIRS = [[0, 1], [2, 3], [4, 5], [6, 7]]
_FUSED = True


def build_fused():
    from contextlib import ExitStack
    nc = bass.Bass("TRN2", target_bir_lowering=False, debug=False)
    P = Prog(nc)
    dt = nc.dram_tensor
    TO = T // 2

    ext = {
        "x": dt("x", [T, D], F32, kind="ExternalInput"),
        "rp0": dt("rp0", [TO, D], F32, kind="ExternalInput"),
        "w_in_t": dt("w_in_t", [16, 128, 1024], F32, kind="ExternalInput"),
        "lg_t": dt("lg_t", [128, 8], F32, kind="ExternalInput"),
        "lb_t": dt("lb_t", [128, 8], F32, kind="ExternalInput"),
        "cw_t": dt("cw_t", [128, 32], F32, kind="ExternalInput"),
        "cb_t": dt("cb_t", [128, 8], F32, kind="ExternalInput"),
        "wd_t": dt("wd_t", [8, 128, 2048], F32, kind="ExternalInput"),
        "bd_t": dt("bd_t", [128, 8], F32, kind="ExternalInput"),
        "wb_t": dt("wb_t", [128, 256], F32, kind="ExternalInput"),
        "wc_t": dt("wc_t", [128, 256], F32, kind="ExternalInput"),
        "la_t": dt("la_t", [128, 128], F32, kind="ExternalInput"),
        "dp_t": dt("dp_t", [128, 8], F32, kind="ExternalInput"),
        "wo_t": dt("wo_t", [8, 128, 1024], F32, kind="ExternalInput"),
        "gw_t": dt("gw_t", [128, 32], F32, kind="ExternalInput"),
        "sel_t": dt("sel_t", [4, 256], F32, kind="ExternalInput"),
        "w1_t": dt("w1_t", [2, 44, 128, 1024], F32, kind="ExternalInput"),
        "w3_t": dt("w3_t", [2, 44, 128, 1024], F32, kind="ExternalInput"),
        "w2_t": dt("w2_t", [2, 8, 128, 5632], F32, kind="ExternalInput"),
        "wq_t": dt("wq_t", [4, 128, 1024], F32, kind="ExternalInput"),
        "wk_t": dt("wk_t", [4, 128, 1024], F32, kind="ExternalInput"),
        "wv_t": dt("wv_t", [4, 128, 1024], F32, kind="ExternalInput"),
        "woA_t": dt("woA_t", [8, 128, 512], F32, kind="ExternalInput"),
        "cos_t": dt("cos_t", [32, T], F32, kind="ExternalInput"),
        "sin_t": dt("sin_t", [32, T], F32, kind="ExternalInput"),
        "mask_t": dt("mask_t", [128, 4 * 512], BF16, kind="ExternalInput"),
        "csc_t": dt("csc_t", [128, 1], F32, kind="ExternalInput"),
        "wg_t": dt("wg_t", [12, 128, 1024], I32, kind="ExternalInput"),
        "wu_t": dt("wu_t", [12, 128, 1024], I32, kind="ExternalInput"),
        "wdf_t": dt("wdf_t", [8, 128, 1536], I32, kind="ExternalInput"),
        "c127_t": dt("c127_t", [128, 1], F32, kind="ExternalInput"),
    }
    xb_own = dt("xb_own", [DI // 2, T], BF16, kind="Internal")
    sz_own = dt("sz_own", [DI // 2, T], BF16, kind="Internal")
    xn0T = dt("xn0T", [D, T], BF16, kind="Internal")
    ag_xb = dt("ag_xb", [DI, T], BF16, kind="Internal")
    rs_in = dt("rs_in", [2 * D, TO], BF16, kind="Internal")
    rs_out = dt("rs_out", [D, TO], BF16, kind="Internal")
    ag_xn_in = dt("ag_xn_in", [D, TO], BF16, kind="Internal")
    ag_xn_out = dt("ag_xn_out", [2 * D, TO], BF16, kind="Internal")
    r1 = dt("r1", [TO, D], F32, kind="Internal")
    r2 = dt("r2", [TO, D], F32, kind="Internal")
    r3 = dt("r3", [TO, D], F32, kind="Internal")
    bscr = dt("bscr", [2, 16, 512], BF16, kind="Internal")
    oscr = dt("oscr", [4, 128, T], BF16, kind="Internal")
    rn_o = dt("rn", [TO, D], F32, kind="ExternalOutput")

    def part_ap(do, t0):
        return rs_in[(t0 // 4096) * D + do * 128:
                     (t0 // 4096) * D + (do + 1) * 128,
                     t0 % 4096: t0 % 4096 + 512]

    def xn_src(dc, t0, w):
        r0 = (t0 // 4096) * D + dc * 128
        c0 = t0 % 4096
        return ag_xn_out[r0:r0 + 128, c0:c0 + w]

    def xn_ap(dc, t0):
        return ag_xn_in[dc * 128:(dc + 1) * 128, t0:t0 + 128]

    stage_n = [0]

    def stage(emit_fn, io, name=None):
        pre = f"s{stage_n[0]}_"
        stage_n[0] += 1
        es = ExitStack()
        sbp = lambda n, sh, dty: es.enter_context(
            nc.sbuf_tensor(pre + n, sh, dty))
        psp = lambda n, sh, dty: es.enter_context(
            nc.psum_tensor(pre + n, sh, dty))
        with nc.named_scope(name or pre):
            emit_fn(P, nc, sbp, psp, io)
        es.close()
        P.barrier()

    def cc(kind, op, ins, outs):
        P.add("gpsimd", lambda e: e.collective_compute(
            kind, op, PAIRS, ins, outs), kind="cc")
        P.barrier()

    twin = {
        "wd_t": dt("wd_b", [8, 128, 2048], BF16, kind="Internal"),
        "wo_t": dt("wo_b", [8, 128, 1024], BF16, kind="Internal"),
        "wb_t": dt("wb_b", [128, 256], BF16, kind="Internal"),
        "wc_t": dt("wc_b", [128, 256], BF16, kind="Internal"),
        "w1_t": dt("w1_b", [2, 44, 128, 1024], BF16, kind="Internal"),
        "w3_t": dt("w3_b", [2, 44, 128, 1024], BF16, kind="Internal"),
        "w2_t": dt("w2_b", [2, 8, 128, 5632], BF16, kind="Internal"),
        "wq_t": dt("wq_b", [4, 128, 1024], BF16, kind="Internal"),
        "wk_t": dt("wk_b", [4, 128, 1024], BF16, kind="Internal"),
        "wv_t": dt("wv_b", [4, 128, 1024], BF16, kind="Internal"),
        "woA_t": dt("woA_b", [8, 128, 512], BF16, kind="Internal"),
        "wg_t": dt("wg_b", [12, 128, 1024], BF16, kind="Internal"),
        "wu_t": dt("wu_b", [12, 128, 1024], BF16, kind="Internal"),
        "wdf_t": dt("wdf_b", [8, 128, 1536], BF16, kind="Internal"),
    }

    stage(emit_s1, {"x": ext["x"], "w_in_t": ext["w_in_t"],
                    "lg_t": ext["lg_t"], "lb_t": ext["lb_t"],
                    "cw_t": ext["cw_t"], "cb_t": ext["cb_t"],
                    "xb": xb_own, "sz": sz_own, "xn0T": xn0T}, "S1")
    def conv(dst, srct, slices):
        for sl in slices:
            P.dma("gpsimd", dst[sl], srct[sl], writes=["twin"])
    conv(twin["wd_t"], ext["wd_t"], [(i,) for i in range(8)])
    conv(twin["wo_t"], ext["wo_t"], [(i,) for i in range(8)])
    conv(twin["wb_t"], ext["wb_t"], [slice(None)])
    conv(twin["wc_t"], ext["wc_t"], [slice(None)])
    for le in range(2):
        for a in range(4):
            conv(twin["w1_t"], ext["w1_t"], [(le, slice(a * 11, a * 11 + 11))])
            conv(twin["w3_t"], ext["w3_t"], [(le, slice(a * 11, a * 11 + 11))])
        for do_ in range(8):
            conv(twin["w2_t"], ext["w2_t"], [(le, do_)])
    for k in ("wq_t", "wk_t", "wv_t", "woA_t", "wg_t", "wu_t", "wdf_t"):
        conv(twin[k], ext[k], [(i,) for i in range(ext[k].shape[0])])
    cc("AllGather", ALU.bypass, [xb_own[:, :]], [ag_xb[:, :]])
    stage(emit_scan, {"xbF": ag_xb, "xbO": xb_own, "szO": sz_own,
                      "xn0T": xn0T, "wd_t": twin["wd_t"], "bd_t": ext["bd_t"],
                      "wb_t": twin["wb_t"], "wc_t": twin["wc_t"],
                      "la_t": ext["la_t"], "dp_t": ext["dp_t"],
                      "wo_t": twin["wo_t"], "bscr": bscr,
                      "part_ap": part_ap, "ldq": "sync"}, "SCAN")
    cc("ReduceScatter", ALU.add, [rs_in[:, :]], [rs_out[:, :]])
    stage(lambda P_, nc_, sbp, psp, io: emit_combine(
        P_, nc_, sbp, psp, io, True, True),
        {"psum": rs_out, "rp": ext["rp0"], "rn": r1, "xn_ap": xn_ap}, "C1")
    cc("AllGather", ALU.bypass, [ag_xn_in[:, :]], [ag_xn_out[:, :]])
    stage(emit_moe, {"gw_t": ext["gw_t"], "sel_t": ext["sel_t"],
                     "w1_t": twin["w1_t"], "w3_t": twin["w3_t"],
                     "w2_t": twin["w2_t"], "xn_src": xn_src,
                     "part_ap": part_ap, "ldq": "sync"}, "MOE")
    cc("ReduceScatter", ALU.add, [rs_in[:, :]], [rs_out[:, :]])
    stage(lambda P_, nc_, sbp, psp, io: emit_combine(
        P_, nc_, sbp, psp, io, True, True),
        {"psum": rs_out, "rp": r1, "rn": r2, "xn_ap": xn_ap}, "C2")
    cc("AllGather", ALU.bypass, [ag_xn_in[:, :]], [ag_xn_out[:, :]])
    stage(emit_attn, {"wq_t": twin["wq_t"], "wk_t": twin["wk_t"],
                      "wv_t": twin["wv_t"], "wo_t": twin["woA_t"],
                      "cos_t": ext["cos_t"], "sin_t": ext["sin_t"],
                      "mask_t": ext["mask_t"], "csc_t": ext["csc_t"],
                      "xn_src": xn_src, "part_ap": part_ap,
                      "oscr": oscr, "ldq": "sync"}, "ATTN")
    cc("ReduceScatter", ALU.add, [rs_in[:, :]], [rs_out[:, :]])
    stage(lambda P_, nc_, sbp, psp, io: emit_combine(
        P_, nc_, sbp, psp, io, True, True),
        {"psum": rs_out, "rp": r2, "rn": r3, "xn_ap": xn_ap}, "C3")
    cc("AllGather", ALU.bypass, [ag_xn_in[:, :]], [ag_xn_out[:, :]])
    stage(emit_ffn1, {"wg_t": twin["wg_t"], "wu_t": twin["wu_t"],
                      "wdf_t": twin["wdf_t"], "c127_t": ext["c127_t"],
                      "xn_src": xn_src, "part_ap": part_ap,
                      "ldq": "sync"}, "FFN")
    cc("ReduceScatter", ALU.add, [rs_in[:, :]], [rs_out[:, :]])
    stage(lambda P_, nc_, sbp, psp, io: emit_combine(
        P_, nc_, sbp, psp, io, False, True),
        {"psum": rs_out, "rp": r3, "rn": rn_o}, "C4")
    P.emit()
    return nc


def stage_fused_inputs(inputs, cid):
    b, p = cid >> 1, cid & 1
    own = slice(p * 4096, (p + 1) * 4096)
    m = {}
    m.update(stage_s1_inputs(inputs, cid))
    m.update(stage_scan_inputs(inputs, cid, None, None, None, None))
    for k in ("xbF", "xbO", "sz", "xn0"):
        m.pop(k, None)
    m.update(stage_moe_inputs(inputs, cid))
    att = stage_attn_inputs(inputs, cid)
    att.pop("xna", None)
    att["woA_t"] = att.pop("wo_t")
    m.update(att)
    ffn = stage_ffn1_inputs(inputs, cid)
    ffn["wdf_t"] = ffn.pop("wd_t")
    m.update(ffn)
    m["rp0"] = np.ascontiguousarray(np.asarray(inputs["L0_attn_x"])[b, own])
    return m


def kernel(**inputs):
    cores = list(range(8))
    if _FUSED:
        nc = build_fused()
        in_maps = [stage_fused_inputs(inputs, c) for c in cores]
        res = run_bass_kernel_spmd(nc, in_maps, core_ids=cores)
        out = np.zeros((4, 8192, 1024), np.float32)
        for c in cores:
            b, p = c >> 1, c & 1
            out[b, p * 4096:(p + 1) * 4096] = res.results[c]["rn"]
        return out
    return _kernel_unfused(inputs, cores)


def _kernel_unfused(inputs, cores):
    nc = build_s1()
    res = run_bass_kernel_spmd(nc, [stage_s1_inputs(inputs, c) for c in cores],
                               core_ids=cores)
    l1 = res.results

    nc = build_scan()
    in_maps = []
    for c in cores:
        ev, od = (c >> 1) * 2, (c >> 1) * 2 + 1
        xbf = np.ascontiguousarray(np.concatenate(
            [l1[ev]["xb"], l1[od]["xb"]], axis=0))
        in_maps.append(stage_scan_inputs(inputs, c, xbf, l1[c]["xb"],
                                         l1[c]["sz"], l1[c]["xn0"]))
    res = run_bass_kernel_spmd(nc, in_maps, core_ids=cores)
    l2 = res.results

    def combine(nc_c, parts, rps, want_xn):
        in_maps = []
        for c in cores:
            b, p = c >> 1, c & 1
            m = {"pa": np.ascontiguousarray(parts[2 * b][p]),
                 "pb": np.ascontiguousarray(parts[2 * b + 1][p]),
                 "rp": rps[c]}
            in_maps.append(m)
        return run_bass_kernel_spmd(nc_c, in_maps, core_ids=cores).results

    nc_cx = build_combine(True, False)
    nc_cl = build_combine(False, False)

    xs = np.asarray(inputs["L0_attn_x"])
    rp0 = [np.ascontiguousarray(xs[c >> 1, (c & 1) * 4096:(c & 1) * 4096 + 4096])
           for c in cores]
    l3 = combine(nc_cx, [l2[c]["part"] for c in cores], rp0, True)

    nc = build_moe()
    in_maps = []
    for c in cores:
        b = c >> 1
        m = stage_moe_inputs(inputs, c)
        m["xnm"] = np.ascontiguousarray(np.concatenate(
            [l3[2 * b]["xnx"], l3[2 * b + 1]["xnx"]], axis=1))
        in_maps.append(m)
    res = run_bass_kernel_spmd(nc, in_maps, core_ids=cores)
    l4 = res.results

    l5 = combine(nc_cx, [l4[c]["mpart"] for c in cores],
                 [l3[c]["rn"] for c in cores], True)

    nc = build_attn()
    in_maps = []
    for c in cores:
        b = c >> 1
        m = stage_attn_inputs(inputs, c)
        m["xna"] = np.ascontiguousarray(np.concatenate(
            [l5[2 * b]["xnx"], l5[2 * b + 1]["xnx"]], axis=1))
        in_maps.append(m)
    res = run_bass_kernel_spmd(nc, in_maps, core_ids=cores)
    l6 = res.results

    l7 = combine(nc_cx, [l6[c]["apart"] for c in cores],
                 [l5[c]["rn"] for c in cores], True)

    nc = build_ffn1()
    in_maps = []
    for c in cores:
        b = c >> 1
        m = stage_ffn1_inputs(inputs, c)
        m["xnf"] = np.ascontiguousarray(np.concatenate(
            [l7[2 * b]["xnx"], l7[2 * b + 1]["xnx"]], axis=1))
        in_maps.append(m)
    res = run_bass_kernel_spmd(nc, in_maps, core_ids=cores)
    l8 = res.results

    l9 = combine(nc_cl, [l8[c]["fpart"] for c in cores],
                 [l7[c]["rn"] for c in cores], False)

    out = np.zeros((4, 8192, 1024), np.float32)
    for c in cores:
        b, p = c >> 1, c & 1
        out[b, p * 4096:(p + 1) * 4096] = l9[c]["rn"]
    return out
